# version 21
# baseline (speedup 1.0000x reference)
import contextlib
import numpy as np
import ml_dtypes
import concourse.bass as bass
import concourse.mybir as mybir
from concourse.bass_utils import run_bass_kernel_spmd

F32 = mybir.dt.float32
BF16 = mybir.dt.bfloat16
FP8 = mybir.dt.float8e4
AF = mybir.ActivationFunctionType
OP = mybir.AluOpType
AX = mybir.AxisListType.X

D = 1024
H = 16
DH = 64
DF = 4096
C = 128
NCK = 32
LN_EPS = 1e-5
M32 = float(np.float32(12582912.0))
M52 = float(np.float32(6755399441055744.0))
INV_D = float(np.float32(1.0 / D))
INV_DH = float(np.float32(1.0 / DH))
JNEU = 3
STOP_AFTER = None
DEBUG_FFN = False

AR_LO = 0
AR_CONST = 5260
AR_HI = 11404
AR_END = 31200


class Sem:
    def __init__(self, h):
        self.h = h
        self.n = 0

    def inc(self, instr, k=1):
        instr.then_inc(self.h, k)
        self.n += k
        return instr


class KB:

    def __init__(self, NT):
        self.NT = NT
        self.NW = NT * C
        self.NCH = NT - 2
        self.NMW = self.NW // 256
        self.nc = bass.Bass("TRN2", target_bir_lowering=False)
        self._dirty = {"v": set(), "s": set()}
        self._sdpend = {}
        self._dma_sems = set()
        self._lastw = {}

    def _rng(self, a):
        t = getattr(a, "tensor", None)
        if t is None or not hasattr(t, "name"):
            return None
        try:
            ap = a.ap
            off = a.offset
            span = 0
            for step, cnt in ap:
                span += abs(int(step)) * (int(cnt) - 1)
            esz = 2 if a.dtype in (BF16, mybir.dt.float16) else (
                1 if a.dtype == FP8 else 4)
            return (t.name, int(off) * esz, (int(off) + span + 1) * esz)
        except Exception:
            return (t.name, 0, 1 << 40)

    def _nm(self, aps):
        out = set()
        for a in aps:
            r = self._rng(a)
            if r is not None:
                out.add(r[0])
        return out

    def _pre(self, eng, key, aps):
        dirty = self._dirty[key]
        for a in aps:
            r = self._rng(a)
            if r is None:
                continue
            for (nm, lo, hi) in dirty:
                if nm == r[0] and r[1] < hi and lo < r[2]:
                    eng.drain()
                    dirty.clear()
                    return

    def vtt(self, out, a, bb, op):
        self._pre(self.nc.vector, "v", [out, a, bb])
        r = self.nc.vector.tensor_tensor(out, a, bb, op)
        self._dirty["v"].add(self._rng(out) or ("?",0,0))
        return r

    def vts(self, out, a, s1, s2, op0, op1=None):
        ext = [x for x in (s1, s2) if x is not None and not isinstance(x, float)]
        self._pre(self.nc.vector, "v", [out, a] + ext)
        if op1 is None:
            r = self.nc.vector.tensor_scalar(out, a, s1, None, op0)
        else:
            r = self.nc.vector.tensor_scalar(out, a, s1, s2, op0, op1)
        self._dirty["v"].add(self._rng(out) or ("?",0,0))
        return r

    def vstt(self, out, in0, scalar, in1, op0, op1):
        ext = [scalar] if not isinstance(scalar, float) else []
        self._pre(self.nc.vector, "v", [out, in0, in1] + ext)
        r = self.nc.vector.scalar_tensor_tensor(out, in0, scalar, in1, op0, op1)
        self._dirty["v"].add(self._rng(out) or ("?",0,0))
        return r

    def vred(self, out, in_, op, absval=False):
        self._pre(self.nc.vector, "v", [out, in_])
        kw = dict(axis=AX, op=op)
        if absval:
            kw["apply_absolute_value"] = True
        r = self.nc.vector.tensor_reduce(out, in_, **kw)
        self._dirty["v"].add(self._rng(out) or ("?",0,0))
        return r

    def vcopy(self, out, in_):
        self._pre(self.nc.vector, "v", [out, in_])
        r = self.nc.vector.tensor_copy(out, in_)
        self._dirty["v"].add(self._rng(out) or ("?",0,0))
        return r

    def vrecip(self, out, in_):
        self._pre(self.nc.vector, "v", [out, in_])
        r = self.nc.vector.reciprocal(out, in_)
        self._dirty["v"].add(self._rng(out) or ("?",0,0))
        return r

    def vmemset(self, ap, val):
        self._pre(self.nc.vector, "v", [ap])
        r = self.nc.vector.memset(ap, val)
        self._dirty["v"].add(self._rng(ap) or ("?",0,0))
        return r

    def sact(self, out, in_, func, bias=0.0, scale=1.0, accum=None):
        aps = [out, in_] + [x for x in (bias, scale, accum)
                            if x is not None and not isinstance(x, float)]
        self._pre(self.nc.scalar, "s", aps)
        kw = dict(bias=bias, scale=scale)
        if accum is not None:
            kw["accum_out"] = accum
        r = self.nc.scalar.activation(out, in_, func, **kw)
        self._dirty["s"].add(self._rng(out) or ("?",0,0))
        if accum is not None:
            self._dirty["s"].add(self._rng(accum) or ("?",0,0))
        return r

    def scopy(self, out, in_):
        self._pre(self.nc.scalar, "s", [out, in_])
        r = self.nc.scalar.copy(out, in_)
        self._dirty["s"].add(self._rng(out) or ("?",0,0))
        return r

    def mm(self, out, lhsT, rhs, start=True, stop=True):
        return self.nc.tensor.matmul(out, lhsT, rhs, start=start, stop=stop)

    def tp(self, out, in_, ident):
        return self.nc.tensor.transpose(out, in_, ident)

    def w(self, engs, *sems):
        m = dict(v=self.nc.vector, s=self.nc.scalar, t=self.nc.tensor,
                 y=self.nc.sync, g=self.nc.gpsimd)
        for e in engs:
            for sm in sems:
                key = (e, id(sm))
                if sm.n > 0 and self._lastw.get(key, -1) < sm.n:
                    m[e].wait_ge(sm.h, sm.n)
                    self._lastw[key] = sm.n
                    if id(sm) in self._dma_sems and e != "y":
                        ps = self._psem(e)
                        if ps is not None:
                            self._sdpend[(e, id(sm))] = ps.n

    def wv(self, e, sm, val):
        if val <= 0:
            return
        key = (e, id(sm))
        if self._lastw.get(key, -1) < val:
            m = dict(v=self.nc.vector, s=self.nc.scalar, t=self.nc.tensor,
                     y=self.nc.sync, g=self.nc.gpsimd)
            m[e].wait_ge(sm.h, val)
            self._lastw[key] = val
            if id(sm) in self._dma_sems and e != "y":
                ps = self._psem(e)
                if ps is not None:
                    self._sdpend[(e, id(sm))] = ps.n

    def _psem(self, e):
        return {"v": getattr(self, "sv", None), "s": getattr(self, "ss", None),
                "t": getattr(self, "st", None),
                "g": getattr(self, "sc", None)}.get(e)

    def barrier(self):
        nc = self.nc
        for e in "vstg":
            for sm in (self.sd, self.sdw, self.sdx, self.sdn, self.sdu,
                       self.sv, self.ss, self.st, self.sg16):
                if sm.n > 0:
                    dict(v=nc.vector, s=nc.scalar, t=nc.tensor,
                         g=nc.gpsimd)[e].wait_ge(sm.h, sm.n)
        self.sv.inc(nc.vector.nop())
        self.ss.inc(nc.scalar.nop())
        self.st.inc(nc.tensor.nop())
        self.sb.inc(nc.gpsimd.nop())
        for sm in (self.sv, self.ss, self.st, self.sb):
            nc.sync.wait_ge(sm.h, sm.n)
        self._sdpend.clear()

    def dma(self, out, in_, sem=None, weak=False):
        sem = sem or self.sd
        eng = dict(v=self.nc.vector, s=self.nc.scalar, t=self.nc.tensor,
                   g=self.nc.gpsimd)
        for (e, smid), mark in list(self._sdpend.items()):
            if smid != id(sem):
                continue
            ps = self._psem(e)
            if ps.n == mark:
                ps.inc(eng[e].nop())
            tgt = mark + 1 if weak else ps.n
            key = ("y", id(ps))
            if self._lastw.get(key, -1) < tgt:
                self.nc.sync.wait_ge(ps.h, tgt)
                self._lastw[key] = tgt
            del self._sdpend[(e, smid)]
        r = self.nc.sync.dma_start(out, in_)
        sem.inc(r, 16)
        return r

    def ar_lo(self, n, ty=F32):
        o = self._lo
        self._lo += n
        assert self._lo <= AR_CONST, f"arena-lo overflow {self._lo}"
        v = self.AR[:, o:o + n]
        return v.bitcast(ty) if ty is not F32 else v

    def ar_hi(self, n, ty=F32):
        o = self._hi
        self._hi += n
        assert self._hi <= AR_END, f"arena-hi overflow {self._hi}"
        v = self.AR[:, o:o + n]
        return v.bitcast(ty) if ty is not F32 else v

    def ar_reset(self):
        self._lo = AR_LO
        self._hi = AR_HI

    def build(self):
        nc = self.nc
        NT, NW, NCH = self.NT, self.NW, self.NCH
        di = lambda n, s, ty: nc.dram_tensor(n, s, ty, kind="ExternalInput").ap()
        dxx = lambda n, s, ty: nc.dram_tensor(n, s, ty).ap()
        I = {}
        for nm, sh, ty in [
                ("x0", [NW, D], F32), ("WTH", [D, D], F32),
                ("W1THP", [128, DF], F32), ("W2THP", [DF // 8, D], F32),
                ("CWc", [128, 3 * NCK], F32), ("CBc", [128, NCK], F32),
                ("KNW", [1, D], F32), ("ONW", [1, D], F32), ("ONB", [1, D], F32),
                ("ETA", [16, 1], F32), ("LML", [16, 1], F32),
                ("UP01", [128, 128], F32), ("IDENT", [128, 128], F32),
                ("IDENTB", [128, 128], BF16), ("HIND", [16, D], F32),
                ("EMASKL", [128, 1], F32), ("EMASKR", [128, 1], F32),
                ("PSEL", [128, 4], F32), ("ONESR", [1, 128], F32),
                ("EXPG", [128, 128], F32), ("EXPC1", [128, 16], F32),
                ("EXPC2", [128, 16], F32)]:
            I[nm] = di(nm, sh, ty)
        self.I = I
        self.XOUT = nc.dram_tensor("XOUT", [NCH * C, D], F32,
                                   kind="ExternalOutput").ap()
        self.W1Q = dxx("W1Q", [D, DF], FP8)
        self.W2Q = dxx("W2Q", [DF, D], FP8)
        self.XQSP = dxx("XQSP", [D, NW], BF16)
        self.N2SP = dxx("N2SP", [DF, NW], BF16)
        self.USP = dxx("USP", [NCH * C, H * C], F32)
        self.WSP = dxx("WSP", [NCH * DH, H * C], F32)
        self.MCSP = dxx("MCSP", [NCH * DH, H * 2 * DH], F32)
        self.AGI = dxx("AGI", [64, H * DH], F32)
        self.W1QP = dxx("W1QP", [128, DF], FP8)
        self.W2QP = dxx("W2QP", [DF // 8, D], FP8)
        self.PSCI = dxx("PSCI", [1, 2], F32)
        self.PSCO = dxx("PSCO", [8, 2], F32)
        self.AGO = dxx("AGO", [256, H * DH], F32)
        self.HROWS = dxx("HROWS", [2, D], F32)
        if DEBUG_FFN:
            self.DBGY = dxx("DBGY", [DF, self.NW], F32)
            self.DBGS = dxx("DBGS", [DF, self.NW], F32)
            self.DBGQ = dxx("DBGQ", [128, self.NW], F32)
        self.HCOLS = dxx("HCOLS", [64, 32], F32)

        with contextlib.ExitStack() as ctx:
            en = ctx.enter_context
            SS = lambda n, s, ty: en(nc.sbuf_tensor(n, s, ty))
            self.sd = Sem(en(nc.semaphore("sd")))
            self.sdw = Sem(en(nc.semaphore("sdw")))
            self.sdx = Sem(en(nc.semaphore("sdx")))
            self.sdn = Sem(en(nc.semaphore("sdn")))
            self.sdu = Sem(en(nc.semaphore("sdu")))
            self.sv = Sem(en(nc.semaphore("sv")))
            self.ss = Sem(en(nc.semaphore("ss")))
            self.st = Sem(en(nc.semaphore("st")))
            self.sg16 = Sem(en(nc.semaphore("sg16")))
            self.sc = Sem(en(nc.semaphore("scol")))
            self.sb = Sem(en(nc.semaphore("sbar")))
            self._dma_sems = {id(s) for s in
                              (self.sd, self.sdw, self.sdx, self.sdn,
                               self.sdu, self.sg16)}

            self.X = SS("X", [128, NT * D], F32)
            self.slot = list(range(NT))
            self.AR = SS("ARENA", [128, AR_END], F32)
            self.IDENT = SS("sID", [128, 128], F32)
            self.IDB = SS("sIDB", [128, 128], BF16)
            self.UP = SS("sUP", [128, 128], F32)
            self.CW = SS("sCW", [128, 3 * NCK], F32)
            self.CB = SS("sCB", [128, NCK], F32)
            self.SC = SS("SCc", [128, 16], F32)
            self.TB = SS("TB", [128, 96], F32)
            self.LR = SS("LRt", [128, 48], F32)
            self.BR = SS("BRc", [16, 8], F32)
            self.ST = SS("STATS", [128, 8 * NT], F32)
            self.GB = SS("GBc", [128, NT * H], F32)
            self.ND = SS("NDc", [128, NT * H], F32)
            self.EML = SS("sEML", [128, 1], F32)
            self.EMR = SS("sEMR", [128, 1], F32)
            self.PSEL = SS("sPSEL", [128, 4], F32)
            self.ONER = SS("sONER", [1, 128], F32)
            self.G2C = SS("sG2C", [128, 32], F32)
            self.psum = [en(nc.psum_tensor(f"pb{i}", [128, 512], F32))
                         for i in range(8)]
            block = en(nc.Block())

            @block.sync
            def _(sync):
                self.program()
        return nc

    def program(self):
        self.ar_reset()
        self.KNWB = self.AR[:, AR_CONST + 0 * D:AR_CONST + 1 * D]
        self.ONWB = self.AR[:, AR_CONST + 1 * D:AR_CONST + 2 * D]
        self.ONBB = self.AR[:, AR_CONST + 2 * D:AR_CONST + 3 * D]
        self.NEGBB = self.AR[:, AR_CONST + 3 * D:AR_CONST + 4 * D]
        self.NEGIBB = self.AR[:, AR_CONST + 4 * D:AR_CONST + 5 * D]
        self.IDLC = self.AR[:, AR_CONST + 5 * D:AR_CONST + 5 * D + 1024]

        self.p0_consts()
        self.p1_wquant_dist()
        self.barrier()
        self.ffn()
        self.barrier()
        if STOP_AFTER != "ffn1":
            self.attn_a()
            self.barrier()
            if STOP_AFTER == "attn_a":
                pass
            else:
                self.attn_b()
            self.barrier()
            if STOP_AFTER not in ("attn", "attn_a"):
                self.ffn()
                self.barrier()
        NT = self.NT
        for t in range(1, NT - 1):
            self.dma(self.XOUT.rearrange("(t p) f -> t p f", p=128)[t - 1],
                     self.X[:, t * D:(t + 1) * D])
        self.nc.sync.wait_ge(self.sd.h, self.sd.n)

    def p0_consts(self):
        I, P = self.I, self.psum
        sd, sv, ss, st = self.sd, self.sv, self.ss, self.st
        self.ar_reset()
        ROW = self.ar_lo(512)
        HB = self.ar_lo(1024)
        STG = self.ar_lo(1024)
        self.dma(self.IDENT[:, :], I["IDENT"])
        self.dma(self.IDB[:, :], I["IDENTB"])
        self.dma(self.UP[:, :], I["UP01"])
        self.dma(self.CW[:, :], I["CWc"])
        self.dma(self.CB[:, :], I["CBc"])
        self.dma(self.EML[:, :], I["EMASKL"])
        self.dma(self.EMR[:, :], I["EMASKR"])
        self.dma(self.PSEL[:, :], I["PSEL"])
        self.dma(self.BR[0:16, 0:1], I["LML"])
        self.dma(self.BR[0:16, 1:2], I["ETA"])
        self.dma(self.ONER[0:1, :], I["ONESR"])
        self.dma(HB[0:16, :], I["HIND"])
        self.dma(self.X[:, 0:self.NT * D].rearrange("p (t f) -> p t f", f=D),
                 I["x0"].rearrange("(t p) f -> p t f", p=128))
        self.dma(self.ST[:, 0:16], I["EXPC1"])
        self.dma(self.ST[:, 16:32], I["EXPC2"])

        self.w("v", sd)
        self.vmemset(self.SC[:, 0:1], 1.0)
        self.vts(self.SC[:, 1:2], self.SC[:, 0:1],
                 float(np.float32(2.0 ** -30)), 1.0, OP.add, OP.subtract)
        self.vts(self.SC[:, 1:2], self.SC[:, 1:2],
                 float(np.float32(2.0 ** 30)), None, OP.mult)
        self.vts(self.SC[:, 2:3], self.SC[:, 1:2],
                 float(np.float32(M52 - M32)), M32, OP.mult, OP.add)
        self.vts(self.SC[:, 8:9], self.SC[:, 1:2], -1.0, 1.0, OP.mult, OP.add)
        self.vts(self.SC[:, 9:10], self.SC[:, 1:2],
                 float(np.float32(2.0 ** -29)), None, OP.mult)
        self.vtt(self.SC[:, 3:4], self.SC[:, 8:9], self.SC[:, 9:10], OP.add)
        sv.inc(self.vts(self.SC[:, 3:4], self.SC[:, 3:4],
                        float(np.float32(2.0 ** -23)), None, OP.mult))
        self.w("s", sd, sv)
        self.sact(self.BR[0:16, 0:1], self.BR[0:16, 0:1], AF.Sigmoid)
        ss.inc(self.sact(self.BR[0:16, 1:2], self.BR[0:16, 1:2], AF.Tanh))
        self.w("v", ss)
        self.vts(self.BR[0:16, 7:8], self.BR[0:16, 0:1], -1.0, 1.0,
                 OP.mult, OP.add)
        self.vtt(self.BR[0:16, 2:3], self.BR[0:16, 1:2], self.BR[0:16, 7:8],
                 OP.mult)
        self.vts(self.BR[0:16, 2:3], self.BR[0:16, 2:3],
                 float(np.float32(-1.0 / DH)), None, OP.mult)
        self.vts(self.BR[0:16, 3:4], self.BR[0:16, 2:3], -1.0, None, OP.mult)
        self.vrecip(self.BR[0:16, 4:5], self.BR[0:16, 2:3])
        sv.inc(self.vts(self.BR[0:16, 4:5], self.BR[0:16, 4:5], -1.0, None,
                        OP.mult))
        self.w("s", sv)
        ss.inc(self.sact(self.BR[0:16, 5:6], self.BR[0:16, 0:1], AF.Ln))
        self.w("s", ss)
        ss.inc(self.sact(self.BR[0:16, 6:7], self.BR[0:16, 5:6], AF.Exp,
                         scale=float(C)))
        self.w("y", ss)
        for j, col in enumerate([0, 2, 3, 4, 5, 6]):
            self.dma(ROW[0:1, 16 * j:16 * j + 16], self.BR[0:16, col:col + 1])
        self.w("t", sd)
        st.inc(self.mm(P[7][:, 0:96], self.ROWONES(), ROW[0:1, 0:96]))
        self.w("v", st)
        sv.inc(self.vcopy(self.TB[:, :], P[7][:, 0:96]))
        self.w("v", sv)
        self.vtt(self.LR[:, 32:48], self.ST[:, 0:16], self.TB[:, 64:80], OP.mult)
        sv.inc(self.vtt(self.LR[:, 16:32], self.ST[:, 16:32], self.TB[:, 64:80],
                        OP.mult))
        self.w("s", sv)
        self.sact(self.LR[:, 32:48], self.LR[:, 32:48], AF.Exp)
        ss.inc(self.sact(self.LR[:, 16:32], self.LR[:, 16:32], AF.Exp))
        self.w("v", ss)
        sv.inc(self.vtt(self.LR[:, 0:16], self.LR[:, 32:48], self.TB[:, 32:48],
                        OP.mult))
        self.w("t", sd, ss, sv)
        st.inc(self.mm(P[6][0:1, 0:512], self.BR[0:16, 3:4], HB[0:16, 0:512]))
        st.inc(self.mm(P[5][0:1, 0:512], self.BR[0:16, 3:4], HB[0:16, 512:1024]))
        self.w("v", st)
        self.vcopy(STG[0:1, 0:512], P[6][0:1, 0:512])
        sv.inc(self.vcopy(STG[0:1, 512:1024], P[5][0:1, 0:512]))
        self.w("t", sv)

        def bcast(dst, row):
            st.inc(self.mm(P[0][:, 0:512], self.ROWONES(), row[0:1, 0:512]))
            st.inc(self.mm(P[1][:, 0:512], self.ROWONES(), row[0:1, 512:1024]))
            self.w("v", st)
            self.vcopy(dst[:, 0:512], P[0][:, 0:512])
            sv.inc(self.vcopy(dst[:, 512:1024], P[1][:, 0:512]))
            self.w("t", sv)

        bcast(self.NEGBB, STG)
        st.inc(self.mm(P[6][0:1, 0:512], self.BR[0:16, 4:5], HB[0:16, 0:512]))
        st.inc(self.mm(P[5][0:1, 0:512], self.BR[0:16, 4:5], HB[0:16, 512:1024]))
        self.w("v", st)
        self.vcopy(STG[0:1, 0:512], P[6][0:1, 0:512])
        sv.inc(self.vcopy(STG[0:1, 512:1024], P[5][0:1, 0:512]))
        self.w("t", sv)
        bcast(self.NEGIBB, STG)
        for name, dst in [("KNW", self.KNWB), ("ONW", self.ONWB),
                          ("ONB", self.ONBB)]:
            self.dma(STG[0:1, 0:1024], I[name])
            self.w("t", sd)
            bcast(dst, STG)
        self.w("v", self.ss)
        for h in range(H):
            sv.inc(self.vts(self.IDLC[0:64, h * 64:(h + 1) * 64],
                            self.IDENT[0:64, 0:64],
                            self.TB[0:64, 80 + h:81 + h], None, OP.mult))
        self.barrier()

    def ROWONES(self):
        return self.ONER[0:1, :]

    def p1_wquant_dist(self):
        sd, sv, ss, st = self.sd, self.sv, self.ss, self.st
        P = self.psum
        nc = self.nc
        self.ar_reset()
        W1P = self.ar_hi(4096)
        W2P = self.ar_hi(4096)
        WQ81 = self.ar_hi(1024, FP8)
        WQ82 = self.ar_hi(1024, FP8)
        ACC2 = self.AR[:, 8:10]
        TMP = self.AR[0:8, 10:14]
        RR = self.AR[0:1, 14:18]
        G8 = self.AR[0:8, 18:20]
        nelem = float(D * DF)
        self.dma(W1P, self.I["W1THP"], sem=self.sdn)
        self.dma(W2P.rearrange("p (c f) -> p c f", f=D),
                 self.I["W2THP"].rearrange("(c p) f -> p c f", p=128),
                 sem=self.sdn)
        self.w("v", self.sdn)
        self.vred(ACC2[:, 0:1], W1P, OP.add, absval=True)
        sv.inc(self.vred(ACC2[:, 1:2], W2P, OP.add, absval=True))
        self.w("t", sv)
        st.inc(self.mm(P[7][0:1, 0:2], self.SC[:, 0:1], ACC2))
        self.w("v", st)
        sv.inc(self.vcopy(TMP[0:1, 0:2], P[7][0:1, 0:2]))
        self.w("y", sv)
        self.dma(self.PSCI, TMP[0:1, 0:2])
        self.w("g", self.sd)
        self.sc.inc(nc.gpsimd.collective_compute(
            "AllGather", OP.bypass,
            replica_groups=[[0, 1, 2, 3, 4, 5, 6, 7]],
            ins=[self.PSCI.opt()], outs=[self.PSCO.opt()]))
        self.w("y", self.sc)
        self.dma(G8[0:8, 0:2], self.PSCO)
        self.w("t", self.sd)
        st.inc(self.mm(P[7][0:1, 2:4], self.SC[0:8, 0:1], G8[0:8, 0:2]))
        self.w("v", st)
        self.vcopy(TMP[0:1, 0:2], P[7][0:1, 2:4])
        self.vts(TMP[0:1, 0:2], TMP[0:1, 0:2],
                 float(np.float32(1.0 / nelem)), 1e-5, OP.mult, OP.max)
        sv.inc(self.vrecip(TMP[0:1, 2:4], TMP[0:1, 0:2]))
        self.w("v", self.sv)
        self.vcopy(RR[0:1, 0:1], TMP[0:1, 0:1])
        self.vcopy(RR[0:1, 1:2], TMP[0:1, 2:3])
        self.vcopy(RR[0:1, 2:3], TMP[0:1, 1:2])
        sv.inc(self.vcopy(RR[0:1, 3:4], TMP[0:1, 3:4]))
        self.w("t", sv)
        st.inc(self.mm(P[7][:, 4:8], self.ROWONES(), RR[0:1, 0:4]))
        self.w("s", st)
        ss.inc(self.scopy(self.SC[:, 4:8], P[7][:, 4:8]))
        for W, WQ8, col in ((W1P, WQ81, 5), (W2P, WQ82, 7)):
            self.w("s", self.ss)
            self.wv("s", sv, sv.n)
            ss.inc(self.sact(W, W, AF.Identity,
                             scale=self.SC[:, col:col + 1],
                             bias=self.SC[:, 2:3]))
            self.wv("v", ss, ss.n)
            self.vts(W, W, self.SC[:, 2:3], 1.0, OP.subtract, OP.min)
            sv.inc(self.vts(WQ8, W, -1.0, None, OP.max))
            self.wv("y", sv, sv.n)
        self.dma(self.W1QP, WQ81)
        self.dma(self.W2QP.rearrange("(c p) f -> p c f", p=128),
                 WQ82.rearrange("p (c f) -> p c f", f=D))
        self.w("g", self.sd)
        self.sc.inc(nc.gpsimd.collective_compute(
            "AllGather", OP.bypass,
            replica_groups=[[0, 1, 2, 3, 4, 5, 6, 7]],
            ins=[self.W1QP.opt()], outs=[self.W1Q.opt()]))
        self.sc.inc(nc.gpsimd.collective_compute(
            "AllGather", OP.bypass,
            replica_groups=[[0, 1, 2, 3, 4, 5, 6, 7]],
            ins=[self.W2QP.opt()], outs=[self.W2Q.opt()]))

    def p1_wquant(self, WTH, WQ, nchunks, width, sccol):
        sd, sv, ss, st = self.sd, self.sv, self.ss, self.st
        P = self.psum
        self.ar_reset()
        WS = [self.ar_hi(width), self.ar_hi(width)]
        WQ8 = [self.ar_hi(width // 4, FP8), self.ar_hi(width // 4, FP8)]
        nelem = nchunks * 128 * width
        ACC = self.AR[:, 8:9]
        TMP = self.AR[:, 9:10]
        RCP = self.AR[:, 10:11]
        lsem = [self.sdn, self.sdu]
        lmark = [0, 0]

        def load(c):
            self.dma(WS[c % 2], WTH.rearrange("(c p) f -> c p f", p=128)[c],
                     sem=lsem[c % 2], weak=True)
            lmark[c % 2] = lsem[c % 2].n

        load(0)
        load(1)
        for c in range(nchunks):
            self.wv("v", lsem[c % 2], lmark[c % 2])
            if c == 0:
                sv.inc(self.vred(ACC, WS[0], OP.add, absval=True))
            else:
                self.vred(TMP, WS[c % 2], OP.add, absval=True)
                sv.inc(self.vtt(ACC, ACC, TMP, OP.add))
            if c + 2 < nchunks:
                self.wv("y", sv, sv.n)
                load(c + 2)
        self.w("t", sv)
        st.inc(self.mm(P[7][0:1, 0:1], ACC, self.SC[:, 0:1]))
        self.w("v", st)
        self.vcopy(TMP[0:1, 0:1], P[7][0:1, 0:1])
        self.vts(TMP[0:1, 0:1], TMP[0:1, 0:1],
                 float(np.float32(1.0 / nelem)), 1e-5, OP.mult, OP.max)
        sv.inc(self.vrecip(RCP[0:1, 0:1], TMP[0:1, 0:1]))
        RR = self.AR[0:1, 0:4]
        self.w("y", sv)
        self.dma(RR[0:1, 0:1], TMP[0:1, 0:1])
        self.dma(RR[0:1, 1:2], RCP[0:1, 0:1])
        self.w("t", sd)
        st.inc(self.mm(P[7][:, 0:2], self.ROWONES(), RR[0:1, 0:2]))
        self.w("s", st)
        ss.inc(self.scopy(self.SC[:, sccol:sccol + 2], P[7][:, 0:2]))
        qmark = [0, 0]
        vmark = [0, 0]
        ssem = [self.sdx, self.sdw]
        smark = [0, 0]
        load(0)
        load(1)
        lmark2 = [lmark[0], lmark[1]]
        for c in range(nchunks):
            b = c % 2
            self.wv("s", lsem[b], lmark2[b])
            ss.inc(self.sact(WS[b], WS[b], AF.Identity,
                             scale=self.SC[:, sccol + 1:sccol + 2],
                             bias=self.SC[:, 2:3]))
            qmark[b] = ss.n
            self.wv("v", ss, qmark[b])
            self.wv("v", ssem[b], smark[b])
            self.vts(WS[b], WS[b], self.SC[:, 2:3], 1.0, OP.subtract, OP.min)
            sv.inc(self.vts(WQ8[b], WS[b], -1.0, None, OP.max))
            vmark[b] = sv.n
            self.wv("y", sv, vmark[b])
            if c + 2 < nchunks:
                load(c + 2)
                lmark2[b] = lmark[b]
            self.dma(WQ.rearrange("(c p) f -> c p f", p=128)[c], WQ8[b],
                     sem=ssem[b], weak=True)
            smark[b] = ssem[b].n
        self.w("v", self.sd)

    def ffn(self):
        NT, NW, NMW = self.NT, self.NW, self.NMW
        sd, sv, ss, st = self.sd, self.sv, self.ss, self.st
        P = self.psum
        X, slot, ST = self.X, self.slot, self.ST
        self.ar_reset()
        XN = self.ar_lo(1024)
        XQB = self.ar_lo(1024, BF16)
        XQE = self.ar_lo(512, BF16)
        CVAB = self.ar_lo(516)
        CVa = CVAB[:, 0:258]
        CVb = CVAB[:, 258:516]
        ABa = self.ar_lo(256)
        ABb = self.ar_lo(256)
        AMXa = self.ar_lo(256)
        AMXb = self.ar_lo(256)
        S1B = self.ar_lo(258)
        S2B = self.ar_lo(256)
        N2P = [self.ar_lo(256, BF16), self.ar_lo(256, BF16)]
        CVP = CVAB[:, 0:512]
        ROWS = XN[:, 0:256]
        S1RT = self.ar_lo(128)
        W1R = self.ar_hi(8192, FP8)
        XQW = self.ar_hi(1040, BF16)
        YW = self.ar_hi(NCK * 258)
        S1ROW = self.ar_hi(NW)

        for t in range(NT):
            xt = X[:, slot[t] * D:(slot[t] + 1) * D]
            self.vred(ST[:, 0 * NT + t:0 * NT + t + 1], xt, OP.add)
            self.vred(ST[:, 4 * NT + t:4 * NT + t + 1], xt, OP.max)
            sv.inc(self.vred(ST[:, 5 * NT + t:5 * NT + t + 1], xt, OP.min))
            ss.inc(self.sact(XN, xt, AF.Square,
                             accum=ST[:, 1 * NT + t:1 * NT + t + 1]))
        self.w("v", ss, sv)
        mu = ST[:, 2 * NT:3 * NT]
        var = ST[:, 6 * NT:7 * NT]
        rstd = ST[:, 7 * NT:8 * NT]
        s1 = ST[:, 3 * NT:4 * NT]
        qs = ST[:, 4 * NT:5 * NT]
        qb = ST[:, 5 * NT:6 * NT]
        self.vts(mu, ST[:, 0:NT], INV_D, None, OP.mult)
        self.vts(var, ST[:, NT:2 * NT], INV_D, None, OP.mult)
        self.vtt(rstd, mu, mu, OP.mult)
        self.vtt(var, var, rstd, OP.subtract)
        self.vts(var, var, LN_EPS, None, OP.add)
        sv.inc(self.vrecip(var, var))
        self.w("s", sv)
        ss.inc(self.sact(rstd, var, AF.Sqrt))
        self.w("v", ss)
        self.vtt(s1, ST[:, 4 * NT:5 * NT], mu, OP.subtract)
        self.vtt(var, mu, ST[:, 5 * NT:6 * NT], OP.subtract)
        self.vtt(s1, s1, var, OP.max)
        self.vtt(s1, s1, rstd, OP.mult)
        self.vts(s1, s1, 1e-5, None, OP.max)
        g1 = ST[:, 0 * NT:1 * NT]
        self.vstt(g1, s1, float(np.float32(1.0 / 127.0)),
                  self.SC[:, 4:5].broadcast_to([128, NT]), OP.mult, OP.mult)
        self.vrecip(s1, s1)
        self.vts(s1, s1, 127.0, None, OP.mult)
        self.vtt(qs, rstd, s1, OP.mult)
        self.vtt(qb, mu, qs, OP.mult)
        sv.inc(self.vts(qb, qb, -1.0, None, OP.mult))
        self.w("t", sv)
        st.inc(self.tp(P[7][0:NT, 0:128], g1, self.IDENT[:, :]))
        self.w("s", st)
        ss.inc(self.scopy(S1RT[0:NT, :], P[7][0:NT, 0:128]))
        p7s_mark = ss.n
        self.w("y", ss)
        self.dma(S1ROW[0:1, :].rearrange("a (t c) -> a t c", c=128),
                 S1RT[0:NT, :])
        vQ_mark = [0] * NT
        vE_mark = [0] * NT
        yD_mark = [0] * NT
        tT_mark = [0] * NT
        for t in range(NT):
            xt = X[:, slot[t] * D:(slot[t] + 1) * D]
            half = t % 2
            xq = XQB.bitcast(F32)[:, half * 512:half * 512 + 512].bitcast(BF16)
            xe = XQE
            bank = P[6] if half == 0 else P[7]
            if t >= 1:
                self.wv("s", sv, vQ_mark[t - 1])
            ss.inc(self.sact(XN, xt, AF.Identity, bias=qb[:, t:t + 1],
                             scale=qs[:, t:t + 1]))
            self.wv("v", ss, ss.n)
            if t >= 2:
                self.wv("v", st, tT_mark[t - 2])
            sv.inc(self.vts(xq, XN, self.SC[:, 2:3], self.SC[:, 2:3],
                            OP.add, OP.subtract))
            vQ_mark[t] = sv.n
            self.wv("t", sv, vQ_mark[t])
            if t == 1:
                self.wv("t", ss, p7s_mark)
            if t >= 2:
                self.wv("t", sv, vE_mark[t - 2])
            for k in range(8):
                r = self.tp(bank[:, k * 64:k * 64 + 64].bitcast(BF16),
                            xq[:, k * 128:(k + 1) * 128], self.IDB[:, :])
            st.inc(r)
            tT_mark[t] = st.n
            self.wv("v", st, tT_mark[t])
            if t >= 1:
                self.wv("v", self.sdx, yD_mark[t - 1])
            sv.inc(self.vcopy(xe, bank[:, 0:512].bitcast(BF16)))
            vE_mark[t] = sv.n
            self.wv("y", sv, vE_mark[t])
            self.dma(self.XQSP.rearrange("(k p) w -> p k w", p=128)
                     [:, :, t * C:(t + 1) * C],
                     xe.rearrange("p (k c) -> p k c", k=8), sem=self.sdx,
                     weak=True)
            yD_mark[t] = self.sdx.n
        self.w("y", self.st, self.sdx, self.sc)
        self.dma(W1R.rearrange("p (k f) -> p k f", k=8),
                 self.W1Q.rearrange("(k p) f -> p k f", p=128), sem=self.sdw)
        w1r_mark = self.sdw.n
        bank_mark = [0, 0, 0, 0]
        p7_mark = vE_mark[NT - 1 if (NT - 1) % 2 == 1 else NT - 2]
        p6_mark = 0
        m1_last = 0
        n2dma_mark = [0, 0]
        xqw_mark = 0
        def xqw_load(j):
            lo = max(0, 256 * j - 1)
            hi = min(NW, 256 * j + 257)
            n = hi - lo
            off = 1 if j == 0 else 0
            self.wv("y", st, m1_last)
            for k in range(8):
                self.dma(XQW.rearrange("p (k c) -> p k c", c=260)
                         [:, k:k + 1, off:off + n],
                         self.XQSP.rearrange("(k p) w -> p k w", p=128)
                         [:, k:k + 1, lo:hi], sem=self.sdx, weak=True)
            return self.sdx.n

        xqw_mark = xqw_load(0)
        for j in range(NMW):
            lo = max(0, 256 * j - 1)
            hi = min(NW, 256 * j + 257)
            n = hi - lo
            off = 1 if j == 0 else 0
            self.wv("t", sd, sd.n)
            self.wv("t", sv, p7_mark)
            st.inc(self.mm(P[7][:, 0:n], self.ROWONES(), S1ROW[0:1, lo:hi]))
            self.wv("v", st, st.n)
            sv.inc(self.vcopy(S1B[:, off:off + n], P[7][:, 0:n]))
            p7_mark = sv.n
            if j == 0:
                sv.inc(self.vmemset(S1B[:, 0:1], 0.0))
                sv.inc(self.vmemset(
                    YW.rearrange("p (c w) -> p c w", w=258)[:, :, 0:1], 0.0))
            if j == NMW - 1:
                sv.inc(self.vmemset(S1B[:, 257:258], 0.0))
                sv.inc(self.vmemset(
                    YW.rearrange("p (c w) -> p c w", w=258)[:, :, 257:258], 0.0))
            s1b_mark = sv.n

            evac_mark = [0] * NCK
            silu_mark = [0] * NCK

            def maxacc(c):
                AB = ABa if c % 2 == 0 else ABb
                self.wv("v", ss, silu_mark[c])
                if c == 0:
                    sv.inc(self.vcopy(AMXa, AB))
                else:
                    srcb = AMXa if c % 2 == 1 else AMXb
                    dstb = AMXb if c % 2 == 1 else AMXa
                    sv.inc(self.vtt(dstb, srcb, AB, OP.max))

            self.wv("t", self.sdx, xqw_mark)
            self.wv("t", self.sdw, w1r_mark)
            for c in range(NCK):
                bank = P[c % 4]
                self.wv("t", sv, bank_mark[c % 4])
                for k in range(8):
                    r = self.mm(bank[:, 0:n],
                                W1R.rearrange("p (k f) -> p k f", k=8)
                                [:, k, c * 128:(c + 1) * 128],
                                XQW.rearrange("p (k c) -> p k c", c=260)
                                [:, k, off:off + n],
                                start=(k == 0), stop=(k == 7))
                st.inc(r)
                mmk = st.n
                CV = CVa if c % 2 == 0 else CVb
                yb = YW[:, c * 258:(c + 1) * 258]
                self.wv("v", st, mmk)
                sv.inc(self.vtt(YW[:, c * 258 + off:c * 258 + off + n],
                                bank[:, 0:n], S1B[:, off:off + n], OP.mult))
                bank_mark[c % 4] = sv.n
                self.wv("s", sv, sv.n)
                ss.inc(self.sact(CV[:, 0:256], yb[:, 0:256], AF.Identity,
                                 scale=self.CW[:, 3 * c:3 * c + 1],
                                 bias=self.CB[:, c:c + 1]))
                if c >= 1:
                    maxacc(c - 1)
                self.wv("v", ss, ss.n)
                self.vstt(CV[:, 0:256], yb[:, 1:257],
                          self.CW[:, 3 * c + 1:3 * c + 2], CV[:, 0:256],
                          OP.mult, OP.add)
                sv.inc(self.vstt(CV[:, 0:256], yb[:, 2:258],
                                 self.CW[:, 3 * c + 2:3 * c + 3], CV[:, 0:256],
                                 OP.mult, OP.add))
                evac_mark[c] = sv.n
                AB = ABa if c % 2 == 0 else ABb
                self.wv("s", sv, evac_mark[c])
                ss.inc(self.sact(yb[:, 1:257], CV[:, 0:256], AF.Silu))
                ss.inc(self.sact(AB, yb[:, 1:257], AF.Abs))
                silu_mark[c] = ss.n
                if c == NCK - 1:
                    m1_last = mmk
                if DEBUG_FFN and False:
                    self.w("y", self.ss)
                    self.dma(self.DBGS.rearrange("(c p) w -> c p w", p=128)
                             [c][:, 256 * j:256 * j + 256], yb[:, 1:257])
                    self.w("v", self.sd)
                    self.w("s", self.sd)
            maxacc(NCK - 1)
            AMX = AMXa if (NCK - 1) % 2 == 0 else AMXb
            self.wv("t", sv, sv.n)
            self.tp(P[6][:, 0:128], AMX[:, 0:128], self.IDENT[:, :])
            st.inc(self.tp(P[6][:, 128:256], AMX[:, 128:256], self.IDENT[:, :]))
            self.wv("v", st, st.n)
            am = ST[:, 6 * NT:6 * NT + 2]
            s2c = ST[:, 6 * NT + 2:6 * NT + 4]
            g2c = self.G2C[:, 2 * j:2 * j + 2]
            self.vred(am[:, 0:1], P[6][:, 0:128], OP.max)
            sv.inc(self.vred(am[:, 1:2], P[6][:, 128:256], OP.max))
            p6_mark = sv.n
            self.vts(am, am, 1e-5, None, OP.max)
            self.vrecip(s2c[:, 0:1], am[:, 0:1])
            self.vrecip(s2c[:, 1:2], am[:, 1:2])
            self.vts(s2c, s2c, 127.0, None, OP.mult)
            sv.inc(self.vstt(g2c, am, float(np.float32(0.5 / 127.0)),
                             self.SC[:, 6:7].broadcast_to([128, 2]),
                             OP.mult, OP.mult))
            self.wv("t", sv, sv.n)
            st.inc(self.tp(P[7][0:2, 0:128], s2c, self.IDENT[:, :]))
            self.wv("s", st, st.n)
            ss.inc(self.scopy(S1RT[64:66, :], P[7][0:2, 0:128]))
            self.wv("y", ss, ss.n)
            self.dma(ROWS[0:1, 0:256].rearrange("a (t c) -> a t c", c=128),
                     S1RT[64:66, :])
            self.wv("t", sd, sd.n)
            st.inc(self.mm(P[7][:, 0:256], self.ROWONES(), ROWS[0:1, 0:256]))
            self.wv("v", st, st.n)
            sv.inc(self.vcopy(S2B, P[7][:, 0:256]))
            p7_mark = sv.n
            next_xqw = xqw_load(j + 1) if j + 1 < NMW else xqw_mark
            self.wv("v", ss, silu_mark[NCK - 1])
            for c in range(0, NCK, 2):
                pi = (c // 2) % 2
                N2 = N2P[pi]
                n2sem = self.sdn if pi == 0 else self.sdu
                ybp = YW.rearrange("p (c w) -> p c w", w=258)[:, c:c + 2, 1:257]
                self.wv("v", n2sem, n2dma_mark[pi])
                self.vtt(CVP.rearrange("p (c w) -> p c w", w=256),
                         ybp, S2B.unsqueeze(1).broadcast_to([128, 2, 256]),
                         OP.mult)
                sv.inc(self.vts(N2, CVP, self.SC[:, 2:3],
                                self.SC[:, 2:3], OP.add, OP.subtract))
                self.wv("y", sv, sv.n)
                self.dma(self.N2SP.rearrange("(c p) w -> p c w", p=128)
                         [:, c:c + 2, 256 * j:256 * j + 256],
                         N2.rearrange("p (c w) -> p c w", w=256),
                         sem=n2sem, weak=True)
                n2dma_mark[pi] = n2sem.n
            xqw_mark = next_xqw
        self.w("y", self.st, self.sv, self.sdn, self.sdu, self.sc)
        self.dma(W1R.rearrange("p (c f) -> p c f", c=NCK),
                 self.W2Q.rearrange("(c p) f -> p c f", p=128), sem=self.sdw)
        w2r_mark = self.sdw.n
        ring_mark = [0] * 8
        res_mark = 0
        n2ring = XQW
        self.wv("t", self.sdw, w2r_mark)
        for j in range(NMW):
            for c in range(NCK):
                tile = n2ring[:, (c % 8) * 260:(c % 8) * 260 + 256]
                rsem = (self.sdx, self.sdn, self.sdu, self.sdw)[c % 4]
                self.wv("y", st, ring_mark[c % 8])
                self.dma(tile, self.N2SP.rearrange("(c p) w -> c p w", p=128)
                         [c][:, 256 * j:256 * j + 256], sem=rsem,
                         weak=True)
                self.wv("t", rsem, rsem.n)
                if c == 0:
                    self.wv("t", sv, res_mark)
                for tt in range(2):
                    for fh in range(2):
                        r = self.mm(P[2 + 2 * tt + fh][:, 0:512],
                                    tile[:, tt * 128:(tt + 1) * 128],
                                    W1R[:, c * 1024 + fh * 512:
                                        c * 1024 + fh * 512 + 512],
                                    start=(c == 0), stop=(c == NCK - 1))
                st.inc(r)
                ring_mark[c % 8] = st.n
            self.wv("v", st, st.n)
            for tt in range(2):
                t_idx = 2 * j + tt
                xo = X[:, t_idx * D:(t_idx + 1) * D]
                g2one = self.G2C[:, 2 * j + tt:2 * j + tt + 1]
                self.vstt(P[6][:, 0:512], P[2 + 2 * tt][:, 0:512],
                          g2one, xo[:, 0:512], OP.mult, OP.add)
                self.vstt(P[7][:, 0:512], P[3 + 2 * tt][:, 0:512],
                          g2one, xo[:, 512:1024], OP.mult, OP.add)
                self.vcopy(xo[:, 0:512], P[6][:, 0:512])
                sv.inc(self.vcopy(xo[:, 512:1024], P[7][:, 0:512]))
            res_mark = sv.n

    def attn_a(self):
        NT = self.NT
        JN = JNEU
        sd, sv, ss, st = self.sd, self.sv, self.ss, self.st
        P = self.psum
        X, slot, ST = self.X, self.slot, self.ST
        self.ar_reset()
        XN = self.ar_lo(1024)
        XHI = self.ar_lo(512, BF16)
        XLO = self.ar_lo(512, BF16)
        XATH = XHI
        XATL = XLO
        KF = self.ar_lo(16)
        KBAR = [[self.ar_lo(64), self.ar_lo(64)],
                [self.ar_lo(64), self.ar_lo(64)]]
        RS = [[self.ar_lo(128), self.ar_lo(128)],
              [self.ar_lo(128), self.ar_lo(128)]]
        X1S = [self.ar_lo(128), self.ar_lo(128)]
        BUPS = [[self.ar_lo(128), self.ar_lo(128)],
                [self.ar_lo(128), self.ar_lo(128)]]
        X1H = self.ar_lo(128)
        SQ = self.ar_lo(1024)
        VBTb = [self.ar_hi(1024), self.ar_hi(1024)]
        WHI = self.ar_hi(4096, BF16)
        WLO = self.ar_hi(4096, BF16)
        KTb = [self.ar_hi(1024), self.ar_hi(1024)]
        KTTb = [self.ar_hi(1024), self.ar_hi(1024)]
        X2S = self.ar_hi(1024)
        WNT = self.ar_hi(1024)
        MSKA = self.ar_hi(2048)
        MCB = self.ar_hi(1024)
        self.dma(XN[:, 0:128], self.I["EXPG"])
        self.w("v", sd)
        for h in range(H):
            sv.inc(self.vts(SQ[:, 0:128], XN[:, 0:128],
                            self.TB[:, 64 + h:65 + h], None, OP.mult))
            self.w("s", self.sv)
            ss.inc(self.sact(SQ[:, 0:128], SQ[:, 0:128], AF.Exp))
            self.w("v", ss)
            self.vtt(MSKA[:, h * 128:(h + 1) * 128], SQ[:, 0:128], self.UP[:, :],
                     OP.mult)
            sv.inc(self.vts(MSKA[:, h * 128:(h + 1) * 128],
                            MSKA[:, h * 128:(h + 1) * 128],
                            self.TB[:, 16 + h:17 + h], None, OP.mult))
        for k in range(8):
            self.dma(XN, self.I["WTH"].rearrange("(k p) f -> k p f", p=128)[k])
            self.w("v", sd, self.ss)
            whik = WHI[:, k * D:(k + 1) * D]
            wlok = WLO[:, k * D:(k + 1) * D]
            self.vcopy(whik, XN)
            self.vtt(SQ, XN, whik, OP.subtract)
            sv.inc(self.vcopy(wlok, SQ))
            self.w("y", sv)
        self.w("s", sv)
        self.w("t", sv)
        for t in range(NT):
            xt = X[:, slot[t] * D:(slot[t] + 1) * D]
            sv.inc(self.vred(ST[:, t:t + 1], xt, OP.add))
            ss.inc(self.sact(XN, xt, AF.Square, accum=ST[:, NT + t:NT + t + 1]))
        self.w("v", ss, sv)
        lmu = ST[:, 2 * NT:3 * NT]
        lrstd = ST[:, 3 * NT:4 * NT]
        lbias = ST[:, 4 * NT:5 * NT]
        lvar = ST[:, 5 * NT:6 * NT]
        self.vts(lmu, ST[:, 0:NT], INV_D, None, OP.mult)
        self.vts(lvar, ST[:, NT:2 * NT], INV_D, None, OP.mult)
        self.vtt(lrstd, lmu, lmu, OP.mult)
        self.vtt(lvar, lvar, lrstd, OP.subtract)
        self.vts(lvar, lvar, LN_EPS, None, OP.add)
        sv.inc(self.vrecip(lvar, lvar))
        self.w("s", sv)
        ss.inc(self.sact(lrstd, lvar, AF.Sqrt))
        self.w("v", ss)
        sv.inc(self.vstt(lbias, lmu, -1.0, lrstd, OP.mult, OP.mult))
        ln_mark = sv.n
        pm = [dict() for _ in range(NT)]
        t1m = [0, 0]; t3m = [0, 0]; t5m = [0, 0]; t7m = [0, 0]
        t9m = [0, 0]; v2m = [0, 0]; v4m = [0, 0]; v6m = [0, 0]
        v8m = [0, 0]; v10m = [0, 0]
        xsem = [self.sdn, self.sdu, self.sdx, self.sdw]
        xspm = [0, 0, 0, 0]
        rsm = [0, 0]
        s10m = [0, 0]
        self._sp_mark = 0

        def prelude_slices(t):
            d = t % 2
            KT, KTT, VBT = KTb[d], KTTb[d], VBTb[d]
            xt = X[:, slot[t] * D:(slot[t] + 1) * D]
            m = pm[t]
            pv = pm[t - 1] if t > 0 else {}

            def s1():
                self.wv("s", sv, ln_mark)
                if t > 0:
                    self.wv("s", sv, pv["kt"])
                ss.inc(self.sact(XN, xt, AF.Identity,
                                 bias=lbias[:, t:t + 1],
                                 scale=lrstd[:, t:t + 1]))
                if t > 0:
                    self.wv("s", st, pv["xtp"])
                    self.wv("s", st, pv["st1"])
                ss.inc(self.scopy(XHI, XN))
                m["hi"] = ss.n
                self.wv("v", ss, m["hi"])
                if t > 0:
                    self.wv("v", st, pv["xtp"])
                sv.inc(self.vtt(SQ, XN, XHI, OP.subtract))
                m["splitv"] = sv.n
                self.wv("s", sv, m["splitv"])
                ss.inc(self.scopy(XLO, SQ))
                m["split"] = ss.n

            def s2():
                self.wv("t", ss, m["split"])
                if t > 0:
                    self.wv("t", ss, pv["ktt"])
                for k in range(8):
                    self.tp(P[6][:, k * 64:k * 64 + 64].bitcast(BF16),
                            XHI[:, k * 128:(k + 1) * 128], self.IDB[:, :])
                for k in range(8):
                    r = self.tp(P[7][:, k * 64:k * 64 + 64].bitcast(BF16),
                                XLO[:, k * 128:(k + 1) * 128], self.IDB[:, :])
                st.inc(r)
                m["xtp"] = st.n
                self.wv("s", st, m["xtp"])
                if t > 0:
                    self.wv("s", st, pv["st1"])
                self.scopy(XATH, P[6][:, 0:512].bitcast(BF16))
                ss.inc(self.scopy(XATL, P[7][:, 0:512].bitcast(BF16)))
                m["xat"] = ss.n

            def static_q(fh, half):
                bank = P[fh]
                self.wv("t", ss, m["xat"])
                if t > 0:
                    self.wv("t", sv, pv["vbt"])
                for k in range(4 * half, 4 * half + 4):
                    wh = WHI[:, k * D + fh * 512:k * D + fh * 512 + 512]
                    wl = WLO[:, k * D + fh * 512:k * D + fh * 512 + 512]
                    xh = XATH[:, k * 128:(k + 1) * 128]
                    xl = XATL[:, k * 128:(k + 1) * 128]
                    self.mm(bank[:, 0:512], xh, wh,
                            start=(k == 0), stop=False)
                    self.mm(bank[:, 0:512], xl, wh, start=False, stop=False)
                    r = self.mm(bank[:, 0:512], xh, wl, start=False,
                                stop=(k == 7))
                st.inc(r)
                m["st%d" % fh] = st.n

            def s5():
                self.wv("v", st, m["st1"])
                self.vtt(VBT[:, 0:512], P[0][:, 0:512], self.NEGBB[:, 0:512],
                         OP.mult)
                sv.inc(self.vtt(VBT[:, 512:1024], P[1][:, 0:512],
                                self.NEGBB[:, 512:1024], OP.mult))
                m["vbt"] = sv.n

            def s56():
                s5()
                s6()

            def s78():
                s7()
                s8()

            def s6():
                if t > 0:
                    self.wv("s", sv, pv["gb"])
                ss.inc(self.sact(SQ, XN, AF.Square))
                m["ksq"] = ss.n
                self.wv("v", ss, m["ksq"])
                self.vred(KF.rearrange("p (h o) -> p h o", o=1),
                          SQ.rearrange("p (h d) -> p h d", d=DH), OP.add)
                self.vts(KF, KF, INV_DH, None, OP.mult)
                self.vtt(KF, KF, self.SC[:, 3:4].broadcast_to([128, 16]),
                         OP.add)
                sv.inc(self.vrecip(KF, KF))
                m["kfa"] = sv.n
                self.wv("s", sv, m["kfa"])
                ss.inc(self.sact(KF, KF, AF.Sqrt))
                m["kfs"] = ss.n

            def s7():
                self.wv("v", ss, m["kfs"])
                self.vtt(KT.rearrange("p (h d) -> p h d", d=DH),
                         XN.rearrange("p (h d) -> p h d", d=DH),
                         KF.rearrange("p (h o) -> p h o", o=1)
                         .broadcast_to([128, 16, DH]), OP.mult)
                sv.inc(self.vtt(KT, KT, self.KNWB, OP.mult))
                m["kt"] = sv.n

            def s8():
                self.wv("s", sv, m["kt"])
                ss.inc(self.sact(SQ, KT, AF.Square))
                m["ksq2"] = ss.n
                self.wv("v", ss, m["ksq2"])
                sv.inc(self.vred(self.GB[:, (t * H) + 0:(t * H) + H]
                                 .rearrange("p (h o) -> p h o", o=1),
                                 SQ.rearrange("p (h d) -> p h d", d=DH),
                                 OP.add))
                m["gb"] = sv.n
                sv.inc(self.vtt(self.ND[:, t * H:t * H + H],
                                self.GB[:, t * H:t * H + H],
                                self.TB[:, 48:64], OP.add))
                self.wv("t", sv, m["kt"])
                self.wv("t", ss, m["xat"])
                for k in range(4):
                    self.tp(P[6][:, k * 128:(k + 1) * 128],
                            KT[:, k * 128:(k + 1) * 128], self.IDENT[:, :])
                st.inc(self.tp(P[7][:, 0:128], KT[:, 512:640],
                               self.IDENT[:, :]))
                for k in range(5, 8):
                    r = self.tp(P[7][:, (k - 4) * 128:(k - 3) * 128],
                                KT[:, k * 128:(k + 1) * 128],
                                self.IDENT[:, :])
                st.inc(r)
                m["ktp"] = st.n
                self.wv("s", st, m["ktp"])
                if t > 1:
                    if "g_last" in pm[t - 2]:
                        self.wv("s", st, pm[t - 2]["g_last"])
                    if "halo" in pm[t - 2]:
                        self.wv("s", sv, pm[t - 2]["halo"])
                self.scopy(KTT[:, 0:512], P[6][:, 0:512])
                ss.inc(self.scopy(KTT[:, 512:1024], P[7][:, 0:512]))
                m["ktt"] = ss.n

            import os as _os9
            if "wholestatic" in _os9.environ.get("ATTNA_MODE", ""):
                return [s1, s2,
                        lambda: (static_q(0, 0), static_q(0, 1)),
                        lambda: (static_q(1, 0), static_q(1, 1)),
                        s5, s6, s7, s8]
            return [s1, lambda: (s2(), s6()), s7, s8,
                    lambda: static_q(0, 0), lambda: static_q(0, 1),
                    lambda: static_q(1, 0),
                    lambda: (static_q(1, 1), s5())]

        def halo(t):
            d = t % 2
            KTT, VBT = KTTb[d], VBTb[d]
            col = 127 if t == 0 else 0
            base = 0 if t == 0 else 16
            kv = KTT.rearrange("p (k c) -> p k c", c=128)
            self.wv("v", ss, pm[t]["ktt"])
            self.vcopy(X1H[0:64, 0:16].rearrange("p (k o) -> p k o", o=2)
                       [:, :, 0:1], kv[0:64, :, col:col + 1])
            sv.inc(self.vcopy(X1H[0:64, 0:16]
                              .rearrange("p (k o) -> p k o", o=2)
                              [:, :, 1:2], kv[64:128, :, col:col + 1]))
            pm[t]["halo"] = sv.n
            self.w("y", sv)
            self.dma(self.HCOLS[:, base:base + 16], X1H[0:64, 0:16])
            self.dma(self.HROWS[t // (NT - 1):t // (NT - 1) + 1, :],
                     VBT[col:col + 1, :])
            self.w("v", self.sd)

        GB_ = [P[2], P[4]]
        S0 = [P[3][:, 0:128], P[5][:, 256:384]]
        S1 = [P[3][:, 128:256], P[5][:, 384:512]]

        def stage0(t, p):
            d = t % 2
            KT, KTT, VBT = KTb[d], KTTb[d], VBTb[d]
            s = p % 2
            for q, h in ((0, 2 * p), (1, 2 * p + 1)):
                kth = KT[:, h * DH:(h + 1) * DH]
                self.wv("v", st, t9m[q])
                self.wv("s", st, t9m[q])
                self.wv("s", sv, pm[t]["vbt"])
                self.vts(RS[q][s][:, DH:128], kth, self.LR[:, h:h + 1],
                         None, OP.mult)
                ss.inc(self.scopy(RS[q][s][:, 0:DH],
                                  VBT[:, h * DH:(h + 1) * DH]))
                rsm[q] = ss.n
                sv.inc(self.vts(KBAR[q][s], kth,
                                self.LR[:, 16 + h:17 + h], None, OP.mult))
            for q, h in ((0, 2 * p), (1, 2 * p + 1)):
                kfs = KTT[(h % 2) * 64:(h % 2) * 64 + 64,
                          (h // 2) * 128:(h // 2) * 128 + 128]
                self.wv("t", ss, pm[t]["ktt"])
                self.wv("t", sv, v2m[q])
                st.inc(self.mm(GB_[q][:, s * 128:s * 128 + 128], kfs, kfs))
                t1m[q] = st.n
            pm[t]["g_last"] = st.n
            for q, h in ((0, 2 * p), (1, 2 * p + 1)):
                self.wv("v", st, t1m[q])
                sv.inc(self.vtt(BUPS[q][s], GB_[q][:, s * 128:s * 128 + 128],
                                MSKA[:, h * 128:(h + 1) * 128], OP.mult))
                v2m[q] = sv.n

        def emit_pair(t, p):
            d = t % 2
            KT, KTT, VBT = KTb[d], KTTb[d], VBTb[d]
            hA, hB = 2 * p, 2 * p + 1
            sl = (p % 4) * 256
            s = p % 2

            def hS(h):
                return (KT[:, h * DH:(h + 1) * DH],
                        KTT[(h % 2) * 64:(h % 2) * 64 + 64,
                            (h // 2) * 128:(h // 2) * 128 + 128],
                        X2S[:, sl + (h % 2) * 128:sl + (h % 2) * 128 + 128])

            for q in (0, 1):
                self.wv("t", sv, v2m[q])
                self.wv("t", ss, rsm[q])
                if q == 1:
                    self.wv("t", sv, v10m[1])
                    self.wv("t", ss, s10m[1])
                st.inc(self.mm(S0[q], BUPS[q][s], RS[q][s]))
                t3m[q] = st.n
            for q in (0, 1):
                self.wv("v", st, t3m[q])
                sv.inc(self.vtt(X1S[q], RS[q][s], S0[q], OP.subtract))
                v4m[q] = sv.n
            for q in (0, 1):
                self.wv("t", sv, v4m[q])
                st.inc(self.mm(S1[q], BUPS[q][s], X1S[q]))
                t5m[q] = st.n
            for q in (0, 1):
                self.wv("v", st, t5m[q])
                sv.inc(self.vtt(X1S[q], RS[q][s], S1[q], OP.subtract))
                v6m[q] = sv.n
            for q in (0, 1):
                self.wv("t", sv, v6m[q])
                st.inc(self.mm(S0[q], BUPS[q][s], X1S[q]))
                t7m[q] = st.n
            for q, h in ((0, hA), (1, hB)):
                kth, kfs, x2h = hS(h)
                self.wv("v", st, t7m[q])
                self.wv("v", xsem[p % 4], xspm[p % 4])
                sv.inc(self.vtt(x2h, RS[q][s], S0[q], OP.subtract))
                v8m[q] = sv.n
            self.wv("y", sv, v8m[1])
            self.dma(self.USP.rearrange("(c p) w -> c p w", p=128)[t - 1]
                     [:, p * 256:(p + 1) * 256], X2S[:, sl:sl + 256],
                     sem=xsem[p % 4], weak=True)
            for q, h in ((0, hA), (1, hB)):
                kth, kfs, x2h = hS(h)
                mb = sl + q * 128
                self.wv("t", sv, v8m[1])
                self.wv("t", sv, v10m[1])
                self.wv("t", ss, s10m[1])
                st.inc(self.mm(P[5][0:DH, q * 256:q * 256 + DH],
                               x2h[:, DH:128], KBAR[q][s]))
                st.inc(self.mm(P[5][0:DH, q * 256 + DH:q * 256 + 128],
                               KBAR[q][s], x2h[:, 0:DH]))
                st.inc(self.tp(P[5][0:DH, q * 256 + 128:q * 256 + 256],
                               x2h[:, DH:128], self.IDENT[:, :]))
                t9m[q] = st.n
            for q, h in ((0, hA), (1, hB)):
                mb = sl + q * 128
                self.wv("v", st, t9m[1])
                self.wv("v", xsem[p % 4], xspm[p % 4])
                self.wv("s", st, t9m[1])
                self.wv("s", xsem[p % 4], xspm[p % 4])
                self.vtt(MCB[0:DH, mb:mb + DH],
                         P[5][0:DH, q * 256:q * 256 + DH],
                         self.IDLC[0:DH, h * 64:(h + 1) * 64], OP.add)
                self.vcopy(MCB[0:DH, mb + DH:mb + 128],
                           P[5][0:DH, q * 256 + DH:q * 256 + 128])
                sv.inc(self.vcopy(WNT[0:DH, mb:mb + 128],
                                  P[5][0:DH, q * 256 + 128:q * 256 + 256]))
                v10m[q] = sv.n
                s10m[q] = ss.n
            self.wv("y", sv, v10m[1])
            self.wv("y", ss, s10m[1])
            self.dma(self.WSP.rearrange("(c p) w -> c p w", p=DH)[t - 1]
                     [:, p * 256:(p + 1) * 256], WNT[0:DH, sl:sl + 256],
                     sem=xsem[p % 4], weak=True)
            self.dma(self.MCSP.rearrange("(c p) w -> c p w", p=DH)[t - 1]
                     [:, p * 256:(p + 1) * 256], MCB[0:DH, sl:sl + 256],
                     sem=xsem[p % 4], weak=True)
            xspm[p % 4] = xsem[p % 4].n

        for s in prelude_slices(0):
            if s is not None:
                s()
        halo(0)
        for s in prelude_slices(1):
            if s is not None:
                s()
        stage0(1, 0)
        for t in range(1, NT - 1):
            slices = prelude_slices(t + 1) if t + 1 < NT else [None] * 8
            for p in range(H // 2):
                emit_pair(t, p)
                if slices[p] is not None:
                    slices[p]()
                if p + 1 < H // 2:
                    stage0(t, p + 1)
                elif t + 1 <= NT - 2:
                    stage0(t + 1, 0)

        halo(NT - 1)
        self.w("v", self.sd)

    def attn_b(self):
        NT, NCH = self.NT, self.NCH
        sd, sv, ss, st = self.sd, self.sv, self.ss, self.st
        P = self.psum
        X, slot, ST = self.X, self.slot, self.ST
        self.ar_reset()
        YTb = [self.ar_lo(1024), self.ar_lo(1024)]
        ZTb = [self.ar_lo(1024), self.ar_lo(1024)]
        SQ2 = self.ar_lo(1024)
        HR = self.ar_lo(32)
        MCB = self.ar_hi(2048)
        MCB2 = self.ar_hi(2048)
        UBb = [self.ar_hi(2048), self.ar_hi(2048)]
        WTBb = [self.ar_hi(2048), self.ar_hi(2048)]
        VHL = self.ar_hi(1024)
        VHR = self.ar_hi(1024)
        YROW = self.ar_hi(1024)
        UROW = self.ar_hi(1024)
        SIN = self.ar_hi(1024)
        SSTA = self.ar_hi(1024)
        AGB = self.ar_hi(1024)

        def s_step(MC, bank_lo, bank_hi, state, wt=None):
            if wt:
                wt()
            for h in range(H):
                bank = bank_lo if h < 8 else bank_hi
                cb = (h % 8) * 64
                r = self.mm(bank[0:64, cb:cb + 64],
                            MC[0:DH, h * 128:h * 128 + DH],
                            state[0:64, h * 64:(h + 1) * 64])
            st.inc(r)
            smm = st.n
            self.wv("v", st, smm)
            MCv = MC[0:64, 0:2048].rearrange("p (h x) -> p h x", x=128)
            self.vtt(state[0:64, 0:512].rearrange("p (h x) -> p h x", x=64),
                     bank_lo[0:64, 0:512].rearrange("p (h x) -> p h x", x=64),
                     MCv[:, 0:8, DH:128], OP.add)
            sv.inc(self.vtt(state[0:64, 512:1024]
                            .rearrange("p (h x) -> p h x", x=64),
                            bank_hi[0:64, 0:512]
                            .rearrange("p (h x) -> p h x", x=64),
                            MCv[:, 8:16, DH:128], OP.add))
            return smm, sv.n

        lsem = [self.sdn, self.sdu]
        mcm = [0, 0]
        sv.inc(self.vmemset(SSTA[0:64, :], 0.0))
        self.dma(MCB[0:DH, :],
                 self.MCSP.rearrange("(c p) w -> c p w", p=DH)[0],
                 sem=lsem[0], weak=True)
        mcm[0] = lsem[0].n
        sadd = sv.n
        smm = 0
        for ch in range(NCH):
            MC = MCB if ch % 2 == 0 else MCB2
            if ch + 1 < NCH:
                MC2 = MCB2 if ch % 2 == 0 else MCB
                self.wv("y", st, smm)
                self.dma(MC2[0:DH, :],
                         self.MCSP.rearrange("(c p) w -> c p w",
                                             p=DH)[ch + 1],
                         sem=lsem[(ch + 1) % 2], weak=True)
                mcm[(ch + 1) % 2] = lsem[(ch + 1) % 2].n

            def wt(c=ch):
                self.wv("t", lsem[c % 2], mcm[c % 2])
                self.wv("t", sv, sadd)
            smm, sadd = s_step(MC, P[(ch % 2) * 2], P[(ch % 2) * 2 + 1],
                               SSTA, wt)
        self.w("y", sv)
        self.dma(self.AGI, SSTA[0:64, :])
        self.w("g", self.sd)
        self.sc.inc(self.nc.gpsimd.collective_compute(
            "AllGather", OP.bypass,
            replica_groups=[[0, 1, 2, 3], [4, 5, 6, 7]],
            ins=[self.AGI.opt()], outs=[self.AGO.opt()]))
        self.w("y", self.sc)
        for q in range(4):
            self.dma(AGB[0:64, :],
                     self.AGO.rearrange("(q p) w -> q p w", p=64)[q])
            self.w("v", self.sd)
            if q == 0:
                sv.inc(self.vts(SIN[0:64, :], AGB[0:64, :],
                                self.PSEL[0:64, 0:1], None, OP.mult))
            else:
                sv.inc(self.vstt(SIN[0:64, :], AGB[0:64, :],
                                 self.PSEL[0:64, q:q + 1], SIN[0:64, :],
                                 OP.mult, OP.add))
            self.w("y", sv)
        sv.inc(self.vcopy(SSTA[0:64, :], SIN[0:64, :]))
        sadd = sv.n
        smm = 0
        self.dma(HR[0:64, :], self.HCOLS)
        self.dma(VHL[0:1, :], self.HROWS[0:1, :])
        self.dma(VHR[0:1, :], self.HROWS[1:2, :])
        self.w("v", self.sd)
        ubm = [0, 0]
        onact = 0
        onred = 0
        ymk = [0, 0]
        ufx = [0, 0]

        def load_tile(tt):
            ch = tt - 1
            dd = tt % 2
            self.dma(UBb[dd], self.USP.rearrange("(c p) w -> c p w",
                                                 p=128)[ch],
                     sem=lsem[dd], weak=True)
            self.dma(WTBb[dd][0:DH, :],
                     self.WSP.rearrange("(c p) w -> c p w", p=DH)[ch],
                     sem=lsem[dd], weak=True)
            MC = MCB if ch % 2 == 0 else MCB2
            self.dma(MC[0:DH, :],
                     self.MCSP.rearrange("(c p) w -> c p w", p=DH)[ch],
                     sem=lsem[dd], weak=True)
            ubm[dd] = lsem[dd].n

        if NT > 2:
            load_tile(1)
        for t in range(NT):
            d = t % 2
            YT, ZT = YTb[d], ZTb[d]
            if 1 <= t <= NT - 2:
                ch = t - 1
                if t + 1 <= NT - 2:
                    self.wv("y", st, ufx[d])
                    self.wv("y", sv, onred)
                    load_tile(t + 1)
                self.wv("t", lsem[d], ubm[d])
                self.wv("t", sv, sadd)
                self.wv("t", sv, max(ymk))
                for h in range(H):
                    bank = P[4] if h < 8 else P[5]
                    cb = (h % 8) * 64
                    r = self.mm(bank[:, cb:cb + 64],
                                WTBb[d][0:DH, h * 128:(h + 1) * 128],
                                SSTA[0:64, h * 64:(h + 1) * 64])
                st.inc(r)
                ufx[d] = st.n
                self.wv("v", st, ufx[d])
                self.wv("v", ss, onact)
                UBv = UBb[d].rearrange("p (h x) -> p h x", x=128)
                NDv = self.ND[:, t * H:(t + 1) * H]
                for g in range(2):
                    Yv = YT[:, g * 512:(g + 1) * 512].rearrange(
                        "p (h x) -> p h x", x=64)
                    bank = (P[4], P[5])[g]
                    self.vtt(Yv, bank[:, 0:512].rearrange(
                        "p (h x) -> p h x", x=64),
                        UBv[:, g * 8:(g + 1) * 8, 0:DH], OP.add)
                    sv.inc(self.vtt(Yv, Yv,
                                    NDv[:, g * 8:(g + 1) * 8].unsqueeze(2)
                                    .broadcast_to([128, 8, DH]), OP.mult))
                ymk[d] = sv.n
                MC = MCB if ch % 2 == 0 else MCB2

                def wt2(c=ch, dd=d):
                    self.wv("t", lsem[dd], ubm[dd])
                    self.wv("t", sv, sadd)
                smm, sadd = s_step(MC, P[(ch % 2) * 2], P[(ch % 2) * 2 + 1],
                                   SSTA, wt2)
            elif t == 0:
                self.wv("v", ss, onact)
                sv.inc(self.vmemset(YT, 0.0))
                self.w("t", sv, self.sd)
                for h in range(H):
                    bank = P[4] if h < 8 else P[5]
                    r = self.mm(bank[0:1, (h % 8) * DH:(h % 8) * DH + DH],
                                HR[0:64, h:h + 1],
                                SIN[0:64, h * 64:(h + 1) * 64])
                st.inc(r)
                self.w("v", st)
                self.vtt(YROW[0:1, :], VHL[0:1, :], self.NEGIBB[0:1, :],
                         OP.mult)
                self.vtt(YROW[0:1, 0:512], YROW[0:1, 0:512], P[4][0:1, 0:512],
                         OP.add)
                sv.inc(self.vtt(YROW[0:1, 512:1024], YROW[0:1, 512:1024],
                                P[5][0:1, 0:512], OP.add))
                self.w("y", self.sv)
                self.dma(YT[127:128, :], YROW[0:1, :])
                self.w("v", self.sd)
                ymk[d] = sv.n
            else:
                self.wv("v", ss, onact)
                self.vmemset(YT, 0.0)
                sv.inc(self.vtt(SQ2[:, 0:16], self.TB[:, 16:32],
                                self.TB[:, 0:16], OP.mult))
                self.w("t", sv, self.sd)
                for h in range(H):
                    bank = P[4] if h < 8 else P[5]
                    r = self.mm(bank[0:1, (h % 8) * DH:(h % 8) * DH + DH],
                                HR[0:64, 16 + h:17 + h],
                                SSTA[0:64, h * 64:(h + 1) * 64])
                st.inc(r)
                self.w("v", st)
                self.vcopy(UROW[0:1, 0:512], P[4][0:1, 0:512])
                sv.inc(self.vcopy(UROW[0:1, 512:1024], P[5][0:1, 0:512]))
                self.nc.vector.drain()
                self._dirty["v"].clear()
                for h in range(H):
                    self.nc.vector.tensor_scalar(
                        UROW[0:1, h * DH:(h + 1) * DH],
                        UROW[0:1, h * DH:(h + 1) * DH],
                        SQ2[0:1, h:h + 1], None, OP.mult)
                self.nc.vector.drain()
                self.vtt(UROW[0:1, :], VHR[0:1, :], UROW[0:1, :], OP.subtract)
                self.nc.vector.drain()
                for h in range(H):
                    self.nc.vector.tensor_scalar(
                        YROW[0:1, h * DH:(h + 1) * DH],
                        UROW[0:1, h * DH:(h + 1) * DH],
                        self.ND[0:1, t * H + h:t * H + h + 1], None, OP.mult)
                self._dirty["v"].add(("ARENA", 0, 1 << 40))
                sv.inc(self.nc.vector.tensor_scalar(
                    SQ2[:, 0:1], SQ2[:, 0:1], 1.0, None, OP.mult))
                self.w("y", self.sv)
                self.dma(YT[0:1, :], YROW[0:1, :])
                self.w("v", self.sd)
                ymk[d] = sv.n
            self.wv("v", sv, ymk[d])
            sv.inc(self.vred(ST[:, 0:1], YT, OP.add))
            vr = sv.n
            self.wv("s", sv, vr)
            ss.inc(self.sact(ZT, YT, AF.Square, accum=ST[:, 1:2]))
            sqm = ss.n
            self.wv("v", ss, sqm)
            self.vts(ST[:, 2:3], ST[:, 0:1], INV_D, None, OP.mult)
            self.vts(ST[:, 3:4], ST[:, 1:2], INV_D, None, OP.mult)
            self.vtt(ST[:, 4:5], ST[:, 2:3], ST[:, 2:3], OP.mult)
            self.vtt(ST[:, 3:4], ST[:, 3:4], ST[:, 4:5], OP.subtract)
            self.vts(ST[:, 3:4], ST[:, 3:4], LN_EPS, None, OP.add)
            sv.inc(self.vrecip(ST[:, 3:4], ST[:, 3:4]))
            self.wv("s", sv, sv.n)
            ss.inc(self.sact(ST[:, 3:4], ST[:, 3:4], AF.Sqrt))
            self.wv("v", ss, ss.n)
            sv.inc(self.vstt(ST[:, 4:5], ST[:, 2:3], -1.0, ST[:, 3:4],
                             OP.mult, OP.mult))
            self.wv("s", sv, sv.n)
            ss.inc(self.sact(ZT, YT, AF.Identity, bias=ST[:, 4:5],
                             scale=ST[:, 3:4]))
            onact = ss.n
            self.wv("v", ss, onact)
            xo = X[:, t * D:(t + 1) * D]
            self.vtt(SQ2, ZT, self.ONWB, OP.mult)
            self.vtt(SQ2, SQ2, self.ONBB, OP.add)
            if t == 0 or t == NT - 1:
                msk = self.EML if t == 0 else self.EMR
                self.vtt(P[6][:, 0:512], SQ2[:, 0:512], xo[:, 0:512], OP.add)
                self.vtt(P[7][:, 0:512], SQ2[:, 512:1024], xo[:, 512:1024],
                         OP.add)
                self.vts(P[6][:, 0:512], P[6][:, 0:512], msk[:, 0:1],
                         None, OP.mult)
                self.vts(P[7][:, 0:512], P[7][:, 0:512], msk[:, 0:1],
                         None, OP.mult)
                self.vcopy(xo[:, 0:512], P[6][:, 0:512])
                sv.inc(self.vcopy(xo[:, 512:1024], P[7][:, 0:512]))
            else:
                sv.inc(self.vtt(xo, SQ2, xo, OP.add))
            onred = sv.n


def _host_inputs(inputs, NT):
    x = np.asarray(inputs["L0_attn_x"], dtype=np.float32)
    B, L, _ = x.shape
    NCH = NT - 2
    LC = NCH * C
    W = np.ascontiguousarray(np.asarray(inputs["L0_attn_W"], np.float32).T)
    W1T = np.ascontiguousarray(np.asarray(inputs["L0_ffn_w1"], np.float32).T)
    W2T = np.ascontiguousarray(np.asarray(inputs["L0_ffn_w2"], np.float32).T)
    cw = np.asarray(inputs["L0_ffn_conv_w"], np.float32).reshape(DF, 3)
    CWc = np.ascontiguousarray(cw.reshape(NCK, 128, 3).transpose(1, 0, 2)
                               .reshape(128, 3 * NCK))
    CBc = np.ascontiguousarray(np.asarray(inputs["L0_ffn_conv_b"], np.float32)
                               .reshape(NCK, 128).T)
    knw = np.tile(np.asarray(inputs["L0_attn_k_norm_w"], np.float32), H)[None]
    onw = np.asarray(inputs["L0_attn_o_norm_w"], np.float32)[None]
    onb = np.asarray(inputs["L0_attn_o_norm_b"], np.float32)[None]
    eta = np.asarray(inputs["L0_attn_eta_raw"], np.float32)[:, None]
    lml = np.asarray(inputs["L0_attn_lam_logit"], np.float32)[:, None]
    up01 = np.triu(np.ones((128, 128), np.float32), 1)
    ident = np.eye(128, dtype=np.float32)
    identb = ident.astype(ml_dtypes.bfloat16)
    hind = np.zeros((16, D), np.float32)
    for h in range(H):
        hind[h, h * DH:(h + 1) * DH] = 1.0
    onesr = np.ones((1, 128), np.float32)
    ii = np.arange(128, dtype=np.float32)
    expg = ii[None, :] - ii[:, None]
    expc1 = np.tile((ii + 1.0)[:, None], (1, 16)).astype(np.float32)
    expc2 = np.tile((127.0 - ii)[:, None], (1, 16)).astype(np.float32)

    common = dict(WTH=W, CWc=CWc, CBc=CBc, KNW=knw,
                  ONW=onw, ONB=onb, ETA=eta, LML=lml, UP01=up01, IDENT=ident,
                  IDENTB=identb, HIND=hind, ONESR=onesr, EXPG=expg,
                  EXPC1=expc1, EXPC2=expc2)
    maps = []
    for core in range(8):
        b = core // 4
        q = core % 4
        t0 = q * LC
        lo, hi = t0 - C, t0 + LC + C
        xs = np.zeros((NT * C, D), np.float32)
        slo, shi = max(0, lo), min(L, hi)
        xs[slo - lo:shi - lo] = x[b, slo:shi]
        eml = np.ones((128, 1), np.float32)
        emr = np.ones((128, 1), np.float32)
        if q == 0:
            eml[127] = 0.0
        if q == 3:
            emr[0] = 0.0
        psel = np.zeros((128, 4), np.float32)
        if q > 0:
            psel[:, q - 1] = 1.0
        m = dict(common)
        m.update(x0=xs, EMASKL=eml, EMASKR=emr, PSEL=psel,
                 W1THP=W1T[core * 128:(core + 1) * 128],
                 W2THP=W2T[core * 512:(core + 1) * 512])
        maps.append(m)
    return maps


_NC_CACHE = {}


def kernel(**inputs):
    x = np.asarray(inputs["L0_attn_x"])
    B, L, _ = x.shape
    assert B == 2
    NCH = L // (4 * C)
    NT = NCH + 2
    if NT not in _NC_CACHE:
        _NC_CACHE[NT] = KB(NT).build()
    nc = _NC_CACHE[NT]
    maps = _host_inputs(inputs, NT)
    res = run_bass_kernel_spmd(nc, maps, core_ids=list(range(8)))
    out = np.zeros((B, L, D), np.float32)
    LC = NCH * C
    for core in range(8):
        b = core // 4
        q = core % 4
        out[b, q * LC:(q + 1) * LC] = np.asarray(res.results[core]["XOUT"])
    return out


# version 22
# speedup vs baseline: 1.0353x; 1.0353x over previous
import contextlib
import numpy as np
import ml_dtypes
import concourse.bass as bass
import concourse.mybir as mybir
from concourse.bass_utils import run_bass_kernel_spmd

F32 = mybir.dt.float32
BF16 = mybir.dt.bfloat16
FP8 = mybir.dt.float8e4
AF = mybir.ActivationFunctionType
OP = mybir.AluOpType
AX = mybir.AxisListType.X

D = 1024
H = 16
DH = 64
DF = 4096
C = 128
NCK = 32
LN_EPS = 1e-5
M32 = float(np.float32(12582912.0))
M52 = float(np.float32(6755399441055744.0))
INV_D = float(np.float32(1.0 / D))
INV_DH = float(np.float32(1.0 / DH))
JNEU = 3
STOP_AFTER = None
DEBUG_FFN = False

AR_LO = 0
AR_CONST = 5260
AR_HI = 11404
AR_END = 31200


class Sem:
    def __init__(self, h):
        self.h = h
        self.n = 0

    def inc(self, instr, k=1):
        instr.then_inc(self.h, k)
        self.n += k
        return instr


class KB:

    def __init__(self, NT):
        self.NT = NT
        self.NW = NT * C
        self.NCH = NT - 2
        self.NMW = self.NW // 256
        self.nc = bass.Bass("TRN2", target_bir_lowering=False)
        self._dirty = {"v": set(), "s": set()}
        self._sdpend = {}
        self._dma_sems = set()
        self._lastw = {}

    def _rng(self, a):
        t = getattr(a, "tensor", None)
        if t is None or not hasattr(t, "name"):
            return None
        try:
            ap = a.ap
            off = a.offset
            span = 0
            for step, cnt in ap:
                span += abs(int(step)) * (int(cnt) - 1)
            esz = 2 if a.dtype in (BF16, mybir.dt.float16) else (
                1 if a.dtype == FP8 else 4)
            return (t.name, int(off) * esz, (int(off) + span + 1) * esz)
        except Exception:
            return (t.name, 0, 1 << 40)

    def _nm(self, aps):
        out = set()
        for a in aps:
            r = self._rng(a)
            if r is not None:
                out.add(r[0])
        return out

    def _pre(self, eng, key, aps):
        dirty = self._dirty[key]
        for a in aps:
            r = self._rng(a)
            if r is None:
                continue
            for (nm, lo, hi) in dirty:
                if nm == r[0] and r[1] < hi and lo < r[2]:
                    eng.drain()
                    dirty.clear()
                    return

    def vtt(self, out, a, bb, op):
        self._pre(self.nc.vector, "v", [out, a, bb])
        r = self.nc.vector.tensor_tensor(out, a, bb, op)
        self._dirty["v"].add(self._rng(out) or ("?",0,0))
        return r

    def vts(self, out, a, s1, s2, op0, op1=None):
        ext = [x for x in (s1, s2) if x is not None and not isinstance(x, float)]
        self._pre(self.nc.vector, "v", [out, a] + ext)
        if op1 is None:
            r = self.nc.vector.tensor_scalar(out, a, s1, None, op0)
        else:
            r = self.nc.vector.tensor_scalar(out, a, s1, s2, op0, op1)
        self._dirty["v"].add(self._rng(out) or ("?",0,0))
        return r

    def vstt(self, out, in0, scalar, in1, op0, op1):
        ext = [scalar] if not isinstance(scalar, float) else []
        self._pre(self.nc.vector, "v", [out, in0, in1] + ext)
        r = self.nc.vector.scalar_tensor_tensor(out, in0, scalar, in1, op0, op1)
        self._dirty["v"].add(self._rng(out) or ("?",0,0))
        return r

    def vred(self, out, in_, op, absval=False):
        self._pre(self.nc.vector, "v", [out, in_])
        kw = dict(axis=AX, op=op)
        if absval:
            kw["apply_absolute_value"] = True
        r = self.nc.vector.tensor_reduce(out, in_, **kw)
        self._dirty["v"].add(self._rng(out) or ("?",0,0))
        return r

    def vcopy(self, out, in_):
        self._pre(self.nc.vector, "v", [out, in_])
        r = self.nc.vector.tensor_copy(out, in_)
        self._dirty["v"].add(self._rng(out) or ("?",0,0))
        return r

    def vrecip(self, out, in_):
        self._pre(self.nc.vector, "v", [out, in_])
        r = self.nc.vector.reciprocal(out, in_)
        self._dirty["v"].add(self._rng(out) or ("?",0,0))
        return r

    def vmemset(self, ap, val):
        self._pre(self.nc.vector, "v", [ap])
        r = self.nc.vector.memset(ap, val)
        self._dirty["v"].add(self._rng(ap) or ("?",0,0))
        return r

    def sact(self, out, in_, func, bias=0.0, scale=1.0, accum=None):
        aps = [out, in_] + [x for x in (bias, scale, accum)
                            if x is not None and not isinstance(x, float)]
        self._pre(self.nc.scalar, "s", aps)
        kw = dict(bias=bias, scale=scale)
        if accum is not None:
            kw["accum_out"] = accum
        r = self.nc.scalar.activation(out, in_, func, **kw)
        self._dirty["s"].add(self._rng(out) or ("?",0,0))
        if accum is not None:
            self._dirty["s"].add(self._rng(accum) or ("?",0,0))
        return r

    def scopy(self, out, in_):
        self._pre(self.nc.scalar, "s", [out, in_])
        r = self.nc.scalar.copy(out, in_)
        self._dirty["s"].add(self._rng(out) or ("?",0,0))
        return r

    def mm(self, out, lhsT, rhs, start=True, stop=True):
        return self.nc.tensor.matmul(out, lhsT, rhs, start=start, stop=stop)

    def tp(self, out, in_, ident):
        return self.nc.tensor.transpose(out, in_, ident)

    def w(self, engs, *sems):
        m = dict(v=self.nc.vector, s=self.nc.scalar, t=self.nc.tensor,
                 y=self.nc.sync, g=self.nc.gpsimd)
        for e in engs:
            for sm in sems:
                key = (e, id(sm))
                if sm.n > 0 and self._lastw.get(key, -1) < sm.n:
                    m[e].wait_ge(sm.h, sm.n)
                    self._lastw[key] = sm.n
                    if id(sm) in self._dma_sems and e != "y":
                        ps = self._psem(e)
                        if ps is not None:
                            self._sdpend[(e, id(sm))] = ps.n

    def wv(self, e, sm, val):
        if val <= 0:
            return
        key = (e, id(sm))
        if self._lastw.get(key, -1) < val:
            m = dict(v=self.nc.vector, s=self.nc.scalar, t=self.nc.tensor,
                     y=self.nc.sync, g=self.nc.gpsimd)
            m[e].wait_ge(sm.h, val)
            self._lastw[key] = val
            if id(sm) in self._dma_sems and e != "y":
                ps = self._psem(e)
                if ps is not None:
                    self._sdpend[(e, id(sm))] = ps.n

    def _psem(self, e):
        return {"v": getattr(self, "sv", None), "s": getattr(self, "ss", None),
                "t": getattr(self, "st", None),
                "g": getattr(self, "sc", None)}.get(e)

    def barrier(self):
        nc = self.nc
        for e in "vstg":
            for sm in (self.sd, self.sdw, self.sdx, self.sdn, self.sdu,
                       self.sv, self.ss, self.st, self.sg16):
                if sm.n > 0:
                    dict(v=nc.vector, s=nc.scalar, t=nc.tensor,
                         g=nc.gpsimd)[e].wait_ge(sm.h, sm.n)
        self.sv.inc(nc.vector.nop())
        self.ss.inc(nc.scalar.nop())
        self.st.inc(nc.tensor.nop())
        self.sb.inc(nc.gpsimd.nop())
        for sm in (self.sv, self.ss, self.st, self.sb):
            nc.sync.wait_ge(sm.h, sm.n)
        self._sdpend.clear()

    def dma(self, out, in_, sem=None, weak=False):
        sem = sem or self.sd
        eng = dict(v=self.nc.vector, s=self.nc.scalar, t=self.nc.tensor,
                   g=self.nc.gpsimd)
        for (e, smid), mark in list(self._sdpend.items()):
            if smid != id(sem):
                continue
            ps = self._psem(e)
            if ps.n == mark:
                ps.inc(eng[e].nop())
            tgt = mark + 1 if weak else ps.n
            key = ("y", id(ps))
            if self._lastw.get(key, -1) < tgt:
                self.nc.sync.wait_ge(ps.h, tgt)
                self._lastw[key] = tgt
            del self._sdpend[(e, smid)]
        r = self.nc.sync.dma_start(out, in_)
        sem.inc(r, 16)
        return r

    def ar_lo(self, n, ty=F32):
        o = self._lo
        self._lo += n
        assert self._lo <= AR_CONST, f"arena-lo overflow {self._lo}"
        v = self.AR[:, o:o + n]
        return v.bitcast(ty) if ty is not F32 else v

    def ar_hi(self, n, ty=F32):
        o = self._hi
        self._hi += n
        assert self._hi <= AR_END, f"arena-hi overflow {self._hi}"
        v = self.AR[:, o:o + n]
        return v.bitcast(ty) if ty is not F32 else v

    def ar_reset(self):
        self._lo = AR_LO
        self._hi = AR_HI

    def build(self):
        nc = self.nc
        NT, NW, NCH = self.NT, self.NW, self.NCH
        di = lambda n, s, ty: nc.dram_tensor(n, s, ty, kind="ExternalInput").ap()
        dxx = lambda n, s, ty: nc.dram_tensor(n, s, ty).ap()
        I = {}
        for nm, sh, ty in [
                ("x0", [NW, D], F32), ("WTH", [D, D], F32),
                ("W1THP", [128, DF], F32), ("W2THP", [DF // 8, D], F32),
                ("CWc", [128, 3 * NCK], F32), ("CBc", [128, NCK], F32),
                ("KNW", [1, D], F32), ("ONW", [1, D], F32), ("ONB", [1, D], F32),
                ("ETA", [16, 1], F32), ("LML", [16, 1], F32),
                ("UP01", [128, 128], F32), ("IDENT", [128, 128], F32),
                ("IDENTB", [128, 128], BF16), ("HIND", [16, D], F32),
                ("EMASKL", [128, 1], F32), ("EMASKR", [128, 1], F32),
                ("PSEL", [128, 4], F32), ("ONESR", [1, 128], F32),
                ("EXPG", [128, 128], F32), ("EXPC1", [128, 16], F32),
                ("EXPC2", [128, 16], F32)]:
            I[nm] = di(nm, sh, ty)
        self.I = I
        self.XOUT = nc.dram_tensor("XOUT", [NCH * C, D], F32,
                                   kind="ExternalOutput").ap()
        self.W1Q = dxx("W1Q", [D, DF], FP8)
        self.W2Q = dxx("W2Q", [DF, D], FP8)
        self.XQSP = dxx("XQSP", [D, NW], BF16)
        self.N2SP = dxx("N2SP", [DF, NW], BF16)
        self.USP = dxx("USP", [NCH * C, H * C], F32)
        self.WSP = dxx("WSP", [NCH * DH, H * C], F32)
        self.MCSP = dxx("MCSP", [NCH * DH, H * 2 * DH], F32)
        self.AGI = dxx("AGI", [64, H * DH], F32)
        self.W1QP = dxx("W1QP", [128, DF], FP8)
        self.W2QP = dxx("W2QP", [DF // 8, D], FP8)
        self.PSCI = dxx("PSCI", [1, 2], F32)
        self.PSCO = dxx("PSCO", [8, 2], F32)
        self.AGO = dxx("AGO", [256, H * DH], F32)
        self.HROWS = dxx("HROWS", [2, D], F32)
        if DEBUG_FFN:
            self.DBGY = dxx("DBGY", [DF, self.NW], F32)
            self.DBGS = dxx("DBGS", [DF, self.NW], F32)
            self.DBGQ = dxx("DBGQ", [128, self.NW], F32)
        self.HCOLS = dxx("HCOLS", [64, 32], F32)

        with contextlib.ExitStack() as ctx:
            en = ctx.enter_context
            SS = lambda n, s, ty: en(nc.sbuf_tensor(n, s, ty))
            self.sd = Sem(en(nc.semaphore("sd")))
            self.sdw = Sem(en(nc.semaphore("sdw")))
            self.sdx = Sem(en(nc.semaphore("sdx")))
            self.sdn = Sem(en(nc.semaphore("sdn")))
            self.sdu = Sem(en(nc.semaphore("sdu")))
            self.sv = Sem(en(nc.semaphore("sv")))
            self.ss = Sem(en(nc.semaphore("ss")))
            self.st = Sem(en(nc.semaphore("st")))
            self.sg16 = Sem(en(nc.semaphore("sg16")))
            self.sc = Sem(en(nc.semaphore("scol")))
            self.sb = Sem(en(nc.semaphore("sbar")))
            self._dma_sems = {id(s) for s in
                              (self.sd, self.sdw, self.sdx, self.sdn,
                               self.sdu, self.sg16)}

            self.X = SS("X", [128, NT * D], F32)
            self.slot = list(range(NT))
            self.AR = SS("ARENA", [128, AR_END], F32)
            self.IDENT = SS("sID", [128, 128], F32)
            self.IDB = SS("sIDB", [128, 128], BF16)
            self.UP = SS("sUP", [128, 128], F32)
            self.CW = SS("sCW", [128, 3 * NCK], F32)
            self.CB = SS("sCB", [128, NCK], F32)
            self.SC = SS("SCc", [128, 16], F32)
            self.TB = SS("TB", [128, 96], F32)
            self.LR = SS("LRt", [128, 48], F32)
            self.BR = SS("BRc", [16, 8], F32)
            self.ST = SS("STATS", [128, 8 * NT], F32)
            self.GB = SS("GBc", [128, NT * H], F32)
            self.ND = SS("NDc", [128, NT * H], F32)
            self.EML = SS("sEML", [128, 1], F32)
            self.EMR = SS("sEMR", [128, 1], F32)
            self.PSEL = SS("sPSEL", [128, 4], F32)
            self.ONER = SS("sONER", [1, 128], F32)
            self.G2C = SS("sG2C", [128, 32], F32)
            self.psum = [en(nc.psum_tensor(f"pb{i}", [128, 512], F32))
                         for i in range(8)]
            block = en(nc.Block())

            @block.sync
            def _(sync):
                self.program()
        return nc

    def program(self):
        self.ar_reset()
        self.KNWB = self.AR[:, AR_CONST + 0 * D:AR_CONST + 1 * D]
        self.ONWB = self.AR[:, AR_CONST + 1 * D:AR_CONST + 2 * D]
        self.ONBB = self.AR[:, AR_CONST + 2 * D:AR_CONST + 3 * D]
        self.NEGBB = self.AR[:, AR_CONST + 3 * D:AR_CONST + 4 * D]
        self.NEGIBB = self.AR[:, AR_CONST + 4 * D:AR_CONST + 5 * D]
        self.IDLC = self.AR[:, AR_CONST + 5 * D:AR_CONST + 5 * D + 1024]

        self.p0_consts()
        self.p1_wquant_dist()
        self.barrier()
        self.ffn()
        self.barrier()
        if STOP_AFTER != "ffn1":
            self.attn_a()
            self.barrier()
            if STOP_AFTER == "attn_a":
                pass
            else:
                self.attn_b()
            self.barrier()
            if STOP_AFTER not in ("attn", "attn_a"):
                self.ffn(reduced=True)
                self.barrier()
        NT = self.NT
        for t in range(1, NT - 1):
            self.dma(self.XOUT.rearrange("(t p) f -> t p f", p=128)[t - 1],
                     self.X[:, t * D:(t + 1) * D])
        self.nc.sync.wait_ge(self.sd.h, self.sd.n)

    def p0_consts(self):
        I, P = self.I, self.psum
        sd, sv, ss, st = self.sd, self.sv, self.ss, self.st
        self.ar_reset()
        ROW = self.ar_lo(512)
        HB = self.ar_lo(1024)
        STG = self.ar_lo(1024)
        self.dma(self.IDENT[:, :], I["IDENT"])
        self.dma(self.IDB[:, :], I["IDENTB"])
        self.dma(self.UP[:, :], I["UP01"])
        self.dma(self.CW[:, :], I["CWc"])
        self.dma(self.CB[:, :], I["CBc"])
        self.dma(self.EML[:, :], I["EMASKL"])
        self.dma(self.EMR[:, :], I["EMASKR"])
        self.dma(self.PSEL[:, :], I["PSEL"])
        self.dma(self.BR[0:16, 0:1], I["LML"])
        self.dma(self.BR[0:16, 1:2], I["ETA"])
        self.dma(self.ONER[0:1, :], I["ONESR"])
        self.dma(HB[0:16, :], I["HIND"])
        self.dma(self.X[:, 0:self.NT * D].rearrange("p (t f) -> p t f", f=D),
                 I["x0"].rearrange("(t p) f -> p t f", p=128))
        self.dma(self.ST[:, 0:16], I["EXPC1"])
        self.dma(self.ST[:, 16:32], I["EXPC2"])

        self.w("v", sd)
        self.vmemset(self.SC[:, 0:1], 1.0)
        self.vts(self.SC[:, 1:2], self.SC[:, 0:1],
                 float(np.float32(2.0 ** -30)), 1.0, OP.add, OP.subtract)
        self.vts(self.SC[:, 1:2], self.SC[:, 1:2],
                 float(np.float32(2.0 ** 30)), None, OP.mult)
        self.vts(self.SC[:, 2:3], self.SC[:, 1:2],
                 float(np.float32(M52 - M32)), M32, OP.mult, OP.add)
        self.vts(self.SC[:, 8:9], self.SC[:, 1:2], -1.0, 1.0, OP.mult, OP.add)
        self.vts(self.SC[:, 9:10], self.SC[:, 1:2],
                 float(np.float32(2.0 ** -29)), None, OP.mult)
        self.vtt(self.SC[:, 3:4], self.SC[:, 8:9], self.SC[:, 9:10], OP.add)
        sv.inc(self.vts(self.SC[:, 3:4], self.SC[:, 3:4],
                        float(np.float32(2.0 ** -23)), None, OP.mult))
        self.w("s", sd, sv)
        self.sact(self.BR[0:16, 0:1], self.BR[0:16, 0:1], AF.Sigmoid)
        ss.inc(self.sact(self.BR[0:16, 1:2], self.BR[0:16, 1:2], AF.Tanh))
        self.w("v", ss)
        self.vts(self.BR[0:16, 7:8], self.BR[0:16, 0:1], -1.0, 1.0,
                 OP.mult, OP.add)
        self.vtt(self.BR[0:16, 2:3], self.BR[0:16, 1:2], self.BR[0:16, 7:8],
                 OP.mult)
        self.vts(self.BR[0:16, 2:3], self.BR[0:16, 2:3],
                 float(np.float32(-1.0 / DH)), None, OP.mult)
        self.vts(self.BR[0:16, 3:4], self.BR[0:16, 2:3], -1.0, None, OP.mult)
        self.vrecip(self.BR[0:16, 4:5], self.BR[0:16, 2:3])
        sv.inc(self.vts(self.BR[0:16, 4:5], self.BR[0:16, 4:5], -1.0, None,
                        OP.mult))
        self.w("s", sv)
        ss.inc(self.sact(self.BR[0:16, 5:6], self.BR[0:16, 0:1], AF.Ln))
        self.w("s", ss)
        ss.inc(self.sact(self.BR[0:16, 6:7], self.BR[0:16, 5:6], AF.Exp,
                         scale=float(C)))
        self.w("y", ss)
        for j, col in enumerate([0, 2, 3, 4, 5, 6]):
            self.dma(ROW[0:1, 16 * j:16 * j + 16], self.BR[0:16, col:col + 1])
        self.w("t", sd)
        st.inc(self.mm(P[7][:, 0:96], self.ROWONES(), ROW[0:1, 0:96]))
        self.w("v", st)
        sv.inc(self.vcopy(self.TB[:, :], P[7][:, 0:96]))
        self.w("v", sv)
        self.vtt(self.LR[:, 32:48], self.ST[:, 0:16], self.TB[:, 64:80], OP.mult)
        sv.inc(self.vtt(self.LR[:, 16:32], self.ST[:, 16:32], self.TB[:, 64:80],
                        OP.mult))
        self.w("s", sv)
        self.sact(self.LR[:, 32:48], self.LR[:, 32:48], AF.Exp)
        ss.inc(self.sact(self.LR[:, 16:32], self.LR[:, 16:32], AF.Exp))
        self.w("v", ss)
        sv.inc(self.vtt(self.LR[:, 0:16], self.LR[:, 32:48], self.TB[:, 32:48],
                        OP.mult))
        self.w("t", sd, ss, sv)
        st.inc(self.mm(P[6][0:1, 0:512], self.BR[0:16, 3:4], HB[0:16, 0:512]))
        st.inc(self.mm(P[5][0:1, 0:512], self.BR[0:16, 3:4], HB[0:16, 512:1024]))
        self.w("v", st)
        self.vcopy(STG[0:1, 0:512], P[6][0:1, 0:512])
        sv.inc(self.vcopy(STG[0:1, 512:1024], P[5][0:1, 0:512]))
        self.w("t", sv)

        def bcast(dst, row):
            st.inc(self.mm(P[0][:, 0:512], self.ROWONES(), row[0:1, 0:512]))
            st.inc(self.mm(P[1][:, 0:512], self.ROWONES(), row[0:1, 512:1024]))
            self.w("v", st)
            self.vcopy(dst[:, 0:512], P[0][:, 0:512])
            sv.inc(self.vcopy(dst[:, 512:1024], P[1][:, 0:512]))
            self.w("t", sv)

        bcast(self.NEGBB, STG)
        st.inc(self.mm(P[6][0:1, 0:512], self.BR[0:16, 4:5], HB[0:16, 0:512]))
        st.inc(self.mm(P[5][0:1, 0:512], self.BR[0:16, 4:5], HB[0:16, 512:1024]))
        self.w("v", st)
        self.vcopy(STG[0:1, 0:512], P[6][0:1, 0:512])
        sv.inc(self.vcopy(STG[0:1, 512:1024], P[5][0:1, 0:512]))
        self.w("t", sv)
        bcast(self.NEGIBB, STG)
        for name, dst in [("KNW", self.KNWB), ("ONW", self.ONWB),
                          ("ONB", self.ONBB)]:
            self.dma(STG[0:1, 0:1024], I[name])
            self.w("t", sd)
            bcast(dst, STG)
        self.w("v", self.ss)
        for h in range(H):
            sv.inc(self.vts(self.IDLC[0:64, h * 64:(h + 1) * 64],
                            self.IDENT[0:64, 0:64],
                            self.TB[0:64, 80 + h:81 + h], None, OP.mult))
        self.barrier()

    def ROWONES(self):
        return self.ONER[0:1, :]

    def p1_wquant_dist(self):
        sd, sv, ss, st = self.sd, self.sv, self.ss, self.st
        P = self.psum
        nc = self.nc
        self.ar_reset()
        W1P = self.ar_hi(4096)
        W2P = self.ar_hi(4096)
        WQ81 = self.ar_hi(1024, FP8)
        WQ82 = self.ar_hi(1024, FP8)
        ACC2 = self.AR[:, 8:10]
        TMP = self.AR[0:8, 10:14]
        RR = self.AR[0:1, 14:18]
        G8 = self.AR[0:8, 18:20]
        nelem = float(D * DF)
        self.dma(W1P, self.I["W1THP"], sem=self.sdn)
        self.dma(W2P.rearrange("p (c f) -> p c f", f=D),
                 self.I["W2THP"].rearrange("(c p) f -> p c f", p=128),
                 sem=self.sdn)
        self.w("v", self.sdn)
        self.vred(ACC2[:, 0:1], W1P, OP.add, absval=True)
        sv.inc(self.vred(ACC2[:, 1:2], W2P, OP.add, absval=True))
        self.w("t", sv)
        st.inc(self.mm(P[7][0:1, 0:2], self.SC[:, 0:1], ACC2))
        self.w("v", st)
        sv.inc(self.vcopy(TMP[0:1, 0:2], P[7][0:1, 0:2]))
        self.w("y", sv)
        self.dma(self.PSCI, TMP[0:1, 0:2])
        self.w("g", self.sd)
        self.sc.inc(nc.gpsimd.collective_compute(
            "AllGather", OP.bypass,
            replica_groups=[[0, 1, 2, 3, 4, 5, 6, 7]],
            ins=[self.PSCI.opt()], outs=[self.PSCO.opt()]))
        self.w("y", self.sc)
        self.dma(G8[0:8, 0:2], self.PSCO)
        self.w("t", self.sd)
        st.inc(self.mm(P[7][0:1, 2:4], self.SC[0:8, 0:1], G8[0:8, 0:2]))
        self.w("v", st)
        self.vcopy(TMP[0:1, 0:2], P[7][0:1, 2:4])
        self.vts(TMP[0:1, 0:2], TMP[0:1, 0:2],
                 float(np.float32(1.0 / nelem)), 1e-5, OP.mult, OP.max)
        sv.inc(self.vrecip(TMP[0:1, 2:4], TMP[0:1, 0:2]))
        self.w("v", self.sv)
        self.vcopy(RR[0:1, 0:1], TMP[0:1, 0:1])
        self.vcopy(RR[0:1, 1:2], TMP[0:1, 2:3])
        self.vcopy(RR[0:1, 2:3], TMP[0:1, 1:2])
        sv.inc(self.vcopy(RR[0:1, 3:4], TMP[0:1, 3:4]))
        self.w("t", sv)
        st.inc(self.mm(P[7][:, 4:8], self.ROWONES(), RR[0:1, 0:4]))
        self.w("s", st)
        ss.inc(self.scopy(self.SC[:, 4:8], P[7][:, 4:8]))
        for W, WQ8, col in ((W1P, WQ81, 5), (W2P, WQ82, 7)):
            self.w("s", self.ss)
            self.wv("s", sv, sv.n)
            ss.inc(self.sact(W, W, AF.Identity,
                             scale=self.SC[:, col:col + 1],
                             bias=self.SC[:, 2:3]))
            self.wv("v", ss, ss.n)
            self.vts(W, W, self.SC[:, 2:3], 1.0, OP.subtract, OP.min)
            sv.inc(self.vts(WQ8, W, -1.0, None, OP.max))
            self.wv("y", sv, sv.n)
        self.dma(self.W1QP, WQ81)
        self.dma(self.W2QP.rearrange("(c p) f -> p c f", p=128),
                 WQ82.rearrange("p (c f) -> p c f", f=D))
        self.w("g", self.sd)
        self.sc.inc(nc.gpsimd.collective_compute(
            "AllGather", OP.bypass,
            replica_groups=[[0, 1, 2, 3, 4, 5, 6, 7]],
            ins=[self.W1QP.opt()], outs=[self.W1Q.opt()]))
        self.sc.inc(nc.gpsimd.collective_compute(
            "AllGather", OP.bypass,
            replica_groups=[[0, 1, 2, 3, 4, 5, 6, 7]],
            ins=[self.W2QP.opt()], outs=[self.W2Q.opt()]))

    def p1_wquant(self, WTH, WQ, nchunks, width, sccol):
        sd, sv, ss, st = self.sd, self.sv, self.ss, self.st
        P = self.psum
        self.ar_reset()
        WS = [self.ar_hi(width), self.ar_hi(width)]
        WQ8 = [self.ar_hi(width // 4, FP8), self.ar_hi(width // 4, FP8)]
        nelem = nchunks * 128 * width
        ACC = self.AR[:, 8:9]
        TMP = self.AR[:, 9:10]
        RCP = self.AR[:, 10:11]
        lsem = [self.sdn, self.sdu]
        lmark = [0, 0]

        def load(c):
            self.dma(WS[c % 2], WTH.rearrange("(c p) f -> c p f", p=128)[c],
                     sem=lsem[c % 2], weak=True)
            lmark[c % 2] = lsem[c % 2].n

        load(0)
        load(1)
        for c in range(nchunks):
            self.wv("v", lsem[c % 2], lmark[c % 2])
            if c == 0:
                sv.inc(self.vred(ACC, WS[0], OP.add, absval=True))
            else:
                self.vred(TMP, WS[c % 2], OP.add, absval=True)
                sv.inc(self.vtt(ACC, ACC, TMP, OP.add))
            if c + 2 < nchunks:
                self.wv("y", sv, sv.n)
                load(c + 2)
        self.w("t", sv)
        st.inc(self.mm(P[7][0:1, 0:1], ACC, self.SC[:, 0:1]))
        self.w("v", st)
        self.vcopy(TMP[0:1, 0:1], P[7][0:1, 0:1])
        self.vts(TMP[0:1, 0:1], TMP[0:1, 0:1],
                 float(np.float32(1.0 / nelem)), 1e-5, OP.mult, OP.max)
        sv.inc(self.vrecip(RCP[0:1, 0:1], TMP[0:1, 0:1]))
        RR = self.AR[0:1, 0:4]
        self.w("y", sv)
        self.dma(RR[0:1, 0:1], TMP[0:1, 0:1])
        self.dma(RR[0:1, 1:2], RCP[0:1, 0:1])
        self.w("t", sd)
        st.inc(self.mm(P[7][:, 0:2], self.ROWONES(), RR[0:1, 0:2]))
        self.w("s", st)
        ss.inc(self.scopy(self.SC[:, sccol:sccol + 2], P[7][:, 0:2]))
        qmark = [0, 0]
        vmark = [0, 0]
        ssem = [self.sdx, self.sdw]
        smark = [0, 0]
        load(0)
        load(1)
        lmark2 = [lmark[0], lmark[1]]
        for c in range(nchunks):
            b = c % 2
            self.wv("s", lsem[b], lmark2[b])
            ss.inc(self.sact(WS[b], WS[b], AF.Identity,
                             scale=self.SC[:, sccol + 1:sccol + 2],
                             bias=self.SC[:, 2:3]))
            qmark[b] = ss.n
            self.wv("v", ss, qmark[b])
            self.wv("v", ssem[b], smark[b])
            self.vts(WS[b], WS[b], self.SC[:, 2:3], 1.0, OP.subtract, OP.min)
            sv.inc(self.vts(WQ8[b], WS[b], -1.0, None, OP.max))
            vmark[b] = sv.n
            self.wv("y", sv, vmark[b])
            if c + 2 < nchunks:
                load(c + 2)
                lmark2[b] = lmark[b]
            self.dma(WQ.rearrange("(c p) f -> c p f", p=128)[c], WQ8[b],
                     sem=ssem[b], weak=True)
            smark[b] = ssem[b].n
        self.w("v", self.sd)

    def ffn(self, reduced=False):
        NT, NW, NMW = self.NT, self.NW, self.NMW
        if reduced:
            NMW = (NW - 2 * C) // 256
        wlo = (C - 1) if reduced else None
        sd, sv, ss, st = self.sd, self.sv, self.ss, self.st
        P = self.psum
        X, slot, ST = self.X, self.slot, self.ST
        self.ar_reset()
        XN = self.ar_lo(1024)
        XQB = self.ar_lo(1024, BF16)
        XQE = self.ar_lo(512, BF16)
        CVAB = self.ar_lo(516)
        CVa = CVAB[:, 0:258]
        CVb = CVAB[:, 258:516]
        ABa = self.ar_lo(256)
        ABb = self.ar_lo(256)
        AMXa = self.ar_lo(256)
        AMXb = self.ar_lo(256)
        S1B = self.ar_lo(258)
        S2B = self.ar_lo(256)
        N2P = [self.ar_lo(256, BF16), self.ar_lo(256, BF16)]
        CVP = CVAB[:, 0:512]
        ROWS = XN[:, 0:256]
        S1RT = self.ar_lo(128)
        W1R = self.ar_hi(8192, FP8)
        XQW = self.ar_hi(1040, BF16)
        YW = self.ar_hi(NCK * 258)
        S1ROW = self.ar_hi(NW)

        for t in range(NT):
            xt = X[:, slot[t] * D:(slot[t] + 1) * D]
            self.vred(ST[:, 0 * NT + t:0 * NT + t + 1], xt, OP.add)
            self.vred(ST[:, 4 * NT + t:4 * NT + t + 1], xt, OP.max)
            sv.inc(self.vred(ST[:, 5 * NT + t:5 * NT + t + 1], xt, OP.min))
            ss.inc(self.sact(XN, xt, AF.Square,
                             accum=ST[:, 1 * NT + t:1 * NT + t + 1]))
        self.w("v", ss, sv)
        mu = ST[:, 2 * NT:3 * NT]
        var = ST[:, 6 * NT:7 * NT]
        rstd = ST[:, 7 * NT:8 * NT]
        s1 = ST[:, 3 * NT:4 * NT]
        qs = ST[:, 4 * NT:5 * NT]
        qb = ST[:, 5 * NT:6 * NT]
        self.vts(mu, ST[:, 0:NT], INV_D, None, OP.mult)
        self.vts(var, ST[:, NT:2 * NT], INV_D, None, OP.mult)
        self.vtt(rstd, mu, mu, OP.mult)
        self.vtt(var, var, rstd, OP.subtract)
        self.vts(var, var, LN_EPS, None, OP.add)
        sv.inc(self.vrecip(var, var))
        self.w("s", sv)
        ss.inc(self.sact(rstd, var, AF.Sqrt))
        self.w("v", ss)
        self.vtt(s1, ST[:, 4 * NT:5 * NT], mu, OP.subtract)
        self.vtt(var, mu, ST[:, 5 * NT:6 * NT], OP.subtract)
        self.vtt(s1, s1, var, OP.max)
        self.vtt(s1, s1, rstd, OP.mult)
        self.vts(s1, s1, 1e-5, None, OP.max)
        g1 = ST[:, 0 * NT:1 * NT]
        self.vstt(g1, s1, float(np.float32(1.0 / 127.0)),
                  self.SC[:, 4:5].broadcast_to([128, NT]), OP.mult, OP.mult)
        self.vrecip(s1, s1)
        self.vts(s1, s1, 127.0, None, OP.mult)
        self.vtt(qs, rstd, s1, OP.mult)
        self.vtt(qb, mu, qs, OP.mult)
        sv.inc(self.vts(qb, qb, -1.0, None, OP.mult))
        self.w("t", sv)
        st.inc(self.tp(P[7][0:NT, 0:128], g1, self.IDENT[:, :]))
        self.w("s", st)
        ss.inc(self.scopy(S1RT[0:NT, :], P[7][0:NT, 0:128]))
        p7s_mark = ss.n
        self.w("y", ss)
        self.dma(S1ROW[0:1, :].rearrange("a (t c) -> a t c", c=128),
                 S1RT[0:NT, :])
        vQ_mark = [0] * NT
        vE_mark = [0] * NT
        yD_mark = [0] * NT
        tT_mark = [0] * NT
        for t in range(NT):
            xt = X[:, slot[t] * D:(slot[t] + 1) * D]
            half = t % 2
            xq = XQB.bitcast(F32)[:, half * 512:half * 512 + 512].bitcast(BF16)
            xe = XQE
            bank = P[6] if half == 0 else P[7]
            if t >= 1:
                self.wv("s", sv, vQ_mark[t - 1])
            ss.inc(self.sact(XN, xt, AF.Identity, bias=qb[:, t:t + 1],
                             scale=qs[:, t:t + 1]))
            self.wv("v", ss, ss.n)
            if t >= 2:
                self.wv("v", st, tT_mark[t - 2])
            sv.inc(self.vts(xq, XN, self.SC[:, 2:3], self.SC[:, 2:3],
                            OP.add, OP.subtract))
            vQ_mark[t] = sv.n
            self.wv("t", sv, vQ_mark[t])
            if t == 1:
                self.wv("t", ss, p7s_mark)
            if t >= 2:
                self.wv("t", sv, vE_mark[t - 2])
            for k in range(8):
                r = self.tp(bank[:, k * 64:k * 64 + 64].bitcast(BF16),
                            xq[:, k * 128:(k + 1) * 128], self.IDB[:, :])
            st.inc(r)
            tT_mark[t] = st.n
            self.wv("v", st, tT_mark[t])
            if t >= 1:
                self.wv("v", self.sdx, yD_mark[t - 1])
            sv.inc(self.vcopy(xe, bank[:, 0:512].bitcast(BF16)))
            vE_mark[t] = sv.n
            self.wv("y", sv, vE_mark[t])
            self.dma(self.XQSP.rearrange("(k p) w -> p k w", p=128)
                     [:, :, t * C:(t + 1) * C],
                     xe.rearrange("p (k c) -> p k c", k=8), sem=self.sdx,
                     weak=True)
            yD_mark[t] = self.sdx.n
        self.w("y", self.st, self.sdx, self.sc)
        self.dma(W1R.rearrange("p (k f) -> p k f", k=8),
                 self.W1Q.rearrange("(k p) f -> p k f", p=128), sem=self.sdw)
        w1r_mark = self.sdw.n
        bank_mark = [0, 0, 0, 0]
        p7_mark = vE_mark[NT - 1 if (NT - 1) % 2 == 1 else NT - 2]
        p6_mark = 0
        m1_last = 0
        n2dma_mark = [0, 0]
        xqw_mark = 0
        def xqw_load(j):
            if reduced:
                lo = wlo + 256 * j
                hi = lo + 258
                off = 0
            else:
                lo = max(0, 256 * j - 1)
                hi = min(NW, 256 * j + 257)
                off = 1 if j == 0 else 0
            n = hi - lo
            self.wv("y", st, m1_last)
            for k in range(8):
                self.dma(XQW.rearrange("p (k c) -> p k c", c=260)
                         [:, k:k + 1, off:off + n],
                         self.XQSP.rearrange("(k p) w -> p k w", p=128)
                         [:, k:k + 1, lo:hi], sem=self.sdx, weak=True)
            return self.sdx.n

        xqw_mark = xqw_load(0)
        for j in range(NMW):
            if reduced:
                lo = wlo + 256 * j
                hi = lo + 258
                off = 0
            else:
                lo = max(0, 256 * j - 1)
                hi = min(NW, 256 * j + 257)
                off = 1 if j == 0 else 0
            n = hi - lo
            self.wv("t", sd, sd.n)
            self.wv("t", sv, p7_mark)
            st.inc(self.mm(P[7][:, 0:n], self.ROWONES(), S1ROW[0:1, lo:hi]))
            self.wv("v", st, st.n)
            sv.inc(self.vcopy(S1B[:, off:off + n], P[7][:, 0:n]))
            p7_mark = sv.n
            if j == 0 and not reduced:
                sv.inc(self.vmemset(S1B[:, 0:1], 0.0))
                sv.inc(self.vmemset(
                    YW.rearrange("p (c w) -> p c w", w=258)[:, :, 0:1], 0.0))
            if j == NMW - 1 and not reduced:
                sv.inc(self.vmemset(S1B[:, 257:258], 0.0))
                sv.inc(self.vmemset(
                    YW.rearrange("p (c w) -> p c w", w=258)[:, :, 257:258], 0.0))
            s1b_mark = sv.n

            evac_mark = [0] * NCK
            silu_mark = [0] * NCK

            def maxacc(c):
                AB = ABa if c % 2 == 0 else ABb
                self.wv("v", ss, silu_mark[c])
                if c == 0:
                    sv.inc(self.vcopy(AMXa, AB))
                else:
                    srcb = AMXa if c % 2 == 1 else AMXb
                    dstb = AMXb if c % 2 == 1 else AMXa
                    sv.inc(self.vtt(dstb, srcb, AB, OP.max))

            self.wv("t", self.sdx, xqw_mark)
            self.wv("t", self.sdw, w1r_mark)
            for c in range(NCK):
                bank = P[c % 4]
                self.wv("t", sv, bank_mark[c % 4])
                for k in range(8):
                    r = self.mm(bank[:, 0:n],
                                W1R.rearrange("p (k f) -> p k f", k=8)
                                [:, k, c * 128:(c + 1) * 128],
                                XQW.rearrange("p (k c) -> p k c", c=260)
                                [:, k, off:off + n],
                                start=(k == 0), stop=(k == 7))
                st.inc(r)
                mmk = st.n
                CV = CVa if c % 2 == 0 else CVb
                yb = YW[:, c * 258:(c + 1) * 258]
                self.wv("v", st, mmk)
                sv.inc(self.vtt(YW[:, c * 258 + off:c * 258 + off + n],
                                bank[:, 0:n], S1B[:, off:off + n], OP.mult))
                bank_mark[c % 4] = sv.n
                self.wv("s", sv, sv.n)
                ss.inc(self.sact(CV[:, 0:256], yb[:, 0:256], AF.Identity,
                                 scale=self.CW[:, 3 * c:3 * c + 1],
                                 bias=self.CB[:, c:c + 1]))
                if c >= 1:
                    maxacc(c - 1)
                self.wv("v", ss, ss.n)
                self.vstt(CV[:, 0:256], yb[:, 1:257],
                          self.CW[:, 3 * c + 1:3 * c + 2], CV[:, 0:256],
                          OP.mult, OP.add)
                sv.inc(self.vstt(CV[:, 0:256], yb[:, 2:258],
                                 self.CW[:, 3 * c + 2:3 * c + 3], CV[:, 0:256],
                                 OP.mult, OP.add))
                evac_mark[c] = sv.n
                AB = ABa if c % 2 == 0 else ABb
                self.wv("s", sv, evac_mark[c])
                ss.inc(self.sact(yb[:, 1:257], CV[:, 0:256], AF.Silu))
                ss.inc(self.sact(AB, yb[:, 1:257], AF.Abs))
                silu_mark[c] = ss.n
                if c == NCK - 1:
                    m1_last = mmk
                if DEBUG_FFN and False:
                    self.w("y", self.ss)
                    self.dma(self.DBGS.rearrange("(c p) w -> c p w", p=128)
                             [c][:, 256 * j:256 * j + 256], yb[:, 1:257])
                    self.w("v", self.sd)
                    self.w("s", self.sd)
            maxacc(NCK - 1)
            AMX = AMXa if (NCK - 1) % 2 == 0 else AMXb
            self.wv("t", sv, sv.n)
            self.tp(P[6][:, 0:128], AMX[:, 0:128], self.IDENT[:, :])
            st.inc(self.tp(P[6][:, 128:256], AMX[:, 128:256], self.IDENT[:, :]))
            self.wv("v", st, st.n)
            am = ST[:, 6 * NT:6 * NT + 2]
            s2c = ST[:, 6 * NT + 2:6 * NT + 4]
            g2c = self.G2C[:, 2 * j:2 * j + 2]
            self.vred(am[:, 0:1], P[6][:, 0:128], OP.max)
            sv.inc(self.vred(am[:, 1:2], P[6][:, 128:256], OP.max))
            p6_mark = sv.n
            self.vts(am, am, 1e-5, None, OP.max)
            self.vrecip(s2c[:, 0:1], am[:, 0:1])
            self.vrecip(s2c[:, 1:2], am[:, 1:2])
            self.vts(s2c, s2c, 127.0, None, OP.mult)
            sv.inc(self.vstt(g2c, am, float(np.float32(0.5 / 127.0)),
                             self.SC[:, 6:7].broadcast_to([128, 2]),
                             OP.mult, OP.mult))
            self.wv("t", sv, sv.n)
            st.inc(self.tp(P[7][0:2, 0:128], s2c, self.IDENT[:, :]))
            self.wv("s", st, st.n)
            ss.inc(self.scopy(S1RT[64:66, :], P[7][0:2, 0:128]))
            self.wv("y", ss, ss.n)
            self.dma(ROWS[0:1, 0:256].rearrange("a (t c) -> a t c", c=128),
                     S1RT[64:66, :])
            self.wv("t", sd, sd.n)
            st.inc(self.mm(P[7][:, 0:256], self.ROWONES(), ROWS[0:1, 0:256]))
            self.wv("v", st, st.n)
            sv.inc(self.vcopy(S2B, P[7][:, 0:256]))
            p7_mark = sv.n
            next_xqw = xqw_load(j + 1) if j + 1 < NMW else xqw_mark
            self.wv("v", ss, silu_mark[NCK - 1])
            for c in range(0, NCK, 2):
                pi = (c // 2) % 2
                N2 = N2P[pi]
                n2sem = self.sdn if pi == 0 else self.sdu
                ybp = YW.rearrange("p (c w) -> p c w", w=258)[:, c:c + 2, 1:257]
                self.wv("v", n2sem, n2dma_mark[pi])
                self.vtt(CVP.rearrange("p (c w) -> p c w", w=256),
                         ybp, S2B.unsqueeze(1).broadcast_to([128, 2, 256]),
                         OP.mult)
                sv.inc(self.vts(N2, CVP, self.SC[:, 2:3],
                                self.SC[:, 2:3], OP.add, OP.subtract))
                self.wv("y", sv, sv.n)
                self.dma(self.N2SP.rearrange("(c p) w -> p c w", p=128)
                         [:, c:c + 2, 256 * j:256 * j + 256],
                         N2.rearrange("p (c w) -> p c w", w=256),
                         sem=n2sem, weak=True)
                n2dma_mark[pi] = n2sem.n
            xqw_mark = next_xqw
        self.w("y", self.st, self.sv, self.sdn, self.sdu, self.sc)
        self.dma(W1R.rearrange("p (c f) -> p c f", c=NCK),
                 self.W2Q.rearrange("(c p) f -> p c f", p=128), sem=self.sdw)
        w2r_mark = self.sdw.n
        ring_mark = [0] * 8
        res_mark = 0
        n2ring = XQW
        self.wv("t", self.sdw, w2r_mark)
        for j in range(NMW):
            for c in range(NCK):
                tile = n2ring[:, (c % 8) * 260:(c % 8) * 260 + 256]
                rsem = (self.sdx, self.sdn, self.sdu, self.sdw)[c % 4]
                self.wv("y", st, ring_mark[c % 8])
                self.dma(tile, self.N2SP.rearrange("(c p) w -> c p w", p=128)
                         [c][:, 256 * j:256 * j + 256], sem=rsem,
                         weak=True)
                self.wv("t", rsem, rsem.n)
                if c == 0:
                    self.wv("t", sv, res_mark)
                for tt in range(2):
                    for fh in range(2):
                        r = self.mm(P[2 + 2 * tt + fh][:, 0:512],
                                    tile[:, tt * 128:(tt + 1) * 128],
                                    W1R[:, c * 1024 + fh * 512:
                                        c * 1024 + fh * 512 + 512],
                                    start=(c == 0), stop=(c == NCK - 1))
                st.inc(r)
                ring_mark[c % 8] = st.n
            self.wv("v", st, st.n)
            for tt in range(2):
                t_idx = 2 * j + tt + (1 if reduced else 0)
                xo = X[:, t_idx * D:(t_idx + 1) * D]
                g2one = self.G2C[:, 2 * j + tt:2 * j + tt + 1]
                self.vstt(P[6][:, 0:512], P[2 + 2 * tt][:, 0:512],
                          g2one, xo[:, 0:512], OP.mult, OP.add)
                self.vstt(P[7][:, 0:512], P[3 + 2 * tt][:, 0:512],
                          g2one, xo[:, 512:1024], OP.mult, OP.add)
                self.vcopy(xo[:, 0:512], P[6][:, 0:512])
                sv.inc(self.vcopy(xo[:, 512:1024], P[7][:, 0:512]))
            res_mark = sv.n

    def attn_a(self):
        NT = self.NT
        JN = JNEU
        sd, sv, ss, st = self.sd, self.sv, self.ss, self.st
        P = self.psum
        X, slot, ST = self.X, self.slot, self.ST
        self.ar_reset()
        XN = self.ar_lo(1024)
        XHI = self.ar_lo(512, BF16)
        XLO = self.ar_lo(512, BF16)
        XATH = XHI
        XATL = XLO
        KF = self.ar_lo(16)
        KBAR = [[self.ar_lo(64), self.ar_lo(64)],
                [self.ar_lo(64), self.ar_lo(64)]]
        RS = [[self.ar_lo(128), self.ar_lo(128)],
              [self.ar_lo(128), self.ar_lo(128)]]
        X1S = [self.ar_lo(128), self.ar_lo(128)]
        BUPS = [[self.ar_lo(128), self.ar_lo(128)],
                [self.ar_lo(128), self.ar_lo(128)]]
        X1H = self.ar_lo(128)
        SQ = self.ar_lo(1024)
        VBTb = [self.ar_hi(1024), self.ar_hi(1024)]
        WHI = self.ar_hi(4096, BF16)
        WLO = self.ar_hi(4096, BF16)
        KTb = [self.ar_hi(1024), self.ar_hi(1024)]
        KTTb = [self.ar_hi(1024), self.ar_hi(1024)]
        X2S = self.ar_hi(1024)
        WNT = self.ar_hi(1024)
        MSKA = self.ar_hi(2048)
        MCB = self.ar_hi(1024)
        self.dma(XN[:, 0:128], self.I["EXPG"])
        self.w("v", sd)
        for h in range(H):
            sv.inc(self.vts(SQ[:, 0:128], XN[:, 0:128],
                            self.TB[:, 64 + h:65 + h], None, OP.mult))
            self.w("s", self.sv)
            ss.inc(self.sact(SQ[:, 0:128], SQ[:, 0:128], AF.Exp))
            self.w("v", ss)
            self.vtt(MSKA[:, h * 128:(h + 1) * 128], SQ[:, 0:128], self.UP[:, :],
                     OP.mult)
            sv.inc(self.vts(MSKA[:, h * 128:(h + 1) * 128],
                            MSKA[:, h * 128:(h + 1) * 128],
                            self.TB[:, 16 + h:17 + h], None, OP.mult))
        for k in range(8):
            self.dma(XN, self.I["WTH"].rearrange("(k p) f -> k p f", p=128)[k])
            self.w("v", sd, self.ss)
            whik = WHI[:, k * D:(k + 1) * D]
            wlok = WLO[:, k * D:(k + 1) * D]
            self.vcopy(whik, XN)
            self.vtt(SQ, XN, whik, OP.subtract)
            sv.inc(self.vcopy(wlok, SQ))
            self.w("y", sv)
        self.w("s", sv)
        self.w("t", sv)
        for t in range(NT):
            xt = X[:, slot[t] * D:(slot[t] + 1) * D]
            sv.inc(self.vred(ST[:, t:t + 1], xt, OP.add))
            ss.inc(self.sact(XN, xt, AF.Square, accum=ST[:, NT + t:NT + t + 1]))
        self.w("v", ss, sv)
        lmu = ST[:, 2 * NT:3 * NT]
        lrstd = ST[:, 3 * NT:4 * NT]
        lbias = ST[:, 4 * NT:5 * NT]
        lvar = ST[:, 5 * NT:6 * NT]
        self.vts(lmu, ST[:, 0:NT], INV_D, None, OP.mult)
        self.vts(lvar, ST[:, NT:2 * NT], INV_D, None, OP.mult)
        self.vtt(lrstd, lmu, lmu, OP.mult)
        self.vtt(lvar, lvar, lrstd, OP.subtract)
        self.vts(lvar, lvar, LN_EPS, None, OP.add)
        sv.inc(self.vrecip(lvar, lvar))
        self.w("s", sv)
        ss.inc(self.sact(lrstd, lvar, AF.Sqrt))
        self.w("v", ss)
        sv.inc(self.vstt(lbias, lmu, -1.0, lrstd, OP.mult, OP.mult))
        ln_mark = sv.n
        pm = [dict() for _ in range(NT)]
        t1m = [0, 0]; t3m = [0, 0]; t5m = [0, 0]; t7m = [0, 0]
        t9m = [0, 0]; v2m = [0, 0]; v4m = [0, 0]; v6m = [0, 0]
        v8m = [0, 0]; v10m = [0, 0]
        xsem = [self.sdn, self.sdu, self.sdx, self.sdw]
        xspm = [0, 0, 0, 0]
        rsm = [0, 0]
        s10m = [0, 0]
        self._sp_mark = 0

        def prelude_slices(t):
            d = t % 2
            KT, KTT, VBT = KTb[d], KTTb[d], VBTb[d]
            xt = X[:, slot[t] * D:(slot[t] + 1) * D]
            m = pm[t]
            pv = pm[t - 1] if t > 0 else {}

            def s1():
                self.wv("s", sv, ln_mark)
                if t > 0:
                    self.wv("s", sv, pv["kt"])
                ss.inc(self.sact(XN, xt, AF.Identity,
                                 bias=lbias[:, t:t + 1],
                                 scale=lrstd[:, t:t + 1]))
                if t > 0:
                    self.wv("s", st, pv["xtp"])
                    self.wv("s", st, pv["st1"])
                ss.inc(self.scopy(XHI, XN))
                m["hi"] = ss.n
                self.wv("v", ss, m["hi"])
                if t > 0:
                    self.wv("v", st, pv["xtp"])
                sv.inc(self.vtt(SQ, XN, XHI, OP.subtract))
                m["splitv"] = sv.n
                self.wv("s", sv, m["splitv"])
                ss.inc(self.scopy(XLO, SQ))
                m["split"] = ss.n

            def s2():
                self.wv("t", ss, m["split"])
                if t > 0:
                    self.wv("t", ss, pv["ktt"])
                for k in range(8):
                    self.tp(P[6][:, k * 64:k * 64 + 64].bitcast(BF16),
                            XHI[:, k * 128:(k + 1) * 128], self.IDB[:, :])
                for k in range(8):
                    r = self.tp(P[7][:, k * 64:k * 64 + 64].bitcast(BF16),
                                XLO[:, k * 128:(k + 1) * 128], self.IDB[:, :])
                st.inc(r)
                m["xtp"] = st.n
                self.wv("s", st, m["xtp"])
                if t > 0:
                    self.wv("s", st, pv["st1"])
                self.scopy(XATH, P[6][:, 0:512].bitcast(BF16))
                ss.inc(self.scopy(XATL, P[7][:, 0:512].bitcast(BF16)))
                m["xat"] = ss.n

            def static_q(fh, half):
                bank = P[fh]
                self.wv("t", ss, m["xat"])
                for k in range(4 * half, 4 * half + 4):
                    wh = WHI[:, k * D + fh * 512:k * D + fh * 512 + 512]
                    wl = WLO[:, k * D + fh * 512:k * D + fh * 512 + 512]
                    xh = XATH[:, k * 128:(k + 1) * 128]
                    xl = XATL[:, k * 128:(k + 1) * 128]
                    self.mm(bank[:, 0:512], xh, wh,
                            start=(k == 0), stop=False)
                    self.mm(bank[:, 0:512], xl, wh, start=False, stop=False)
                    r = self.mm(bank[:, 0:512], xh, wl, start=False,
                                stop=(k == 7))
                st.inc(r)
                m["st%d" % fh] = st.n

            def s5():
                self.wv("v", st, m["st1"])
                self.vtt(VBT[:, 0:512], P[0][:, 0:512], self.NEGBB[:, 0:512],
                         OP.mult)
                sv.inc(self.vtt(VBT[:, 512:1024], P[1][:, 0:512],
                                self.NEGBB[:, 512:1024], OP.mult))
                m["vbt"] = sv.n

            def s56():
                s5()
                s6()

            def s78():
                s7()
                s8()

            def s6():
                if t > 0:
                    self.wv("s", sv, pv["gb"])
                ss.inc(self.sact(SQ, XN, AF.Square))
                m["ksq"] = ss.n
                self.wv("v", ss, m["ksq"])
                self.vred(KF.rearrange("p (h o) -> p h o", o=1),
                          SQ.rearrange("p (h d) -> p h d", d=DH), OP.add)
                self.vts(KF, KF, INV_DH, None, OP.mult)
                self.vtt(KF, KF, self.SC[:, 3:4].broadcast_to([128, 16]),
                         OP.add)
                sv.inc(self.vrecip(KF, KF))
                m["kfa"] = sv.n
                self.wv("s", sv, m["kfa"])
                ss.inc(self.sact(KF, KF, AF.Sqrt))
                m["kfs"] = ss.n

            def s7():
                self.wv("v", ss, m["kfs"])
                self.vtt(KT.rearrange("p (h d) -> p h d", d=DH),
                         XN.rearrange("p (h d) -> p h d", d=DH),
                         KF.rearrange("p (h o) -> p h o", o=1)
                         .broadcast_to([128, 16, DH]), OP.mult)
                sv.inc(self.vtt(KT, KT, self.KNWB, OP.mult))
                m["kt"] = sv.n

            def s8():
                self.wv("s", sv, m["kt"])
                ss.inc(self.sact(SQ, KT, AF.Square))
                m["ksq2"] = ss.n
                self.wv("v", ss, m["ksq2"])
                sv.inc(self.vred(self.GB[:, (t * H) + 0:(t * H) + H]
                                 .rearrange("p (h o) -> p h o", o=1),
                                 SQ.rearrange("p (h d) -> p h d", d=DH),
                                 OP.add))
                m["gb"] = sv.n
                sv.inc(self.vtt(self.ND[:, t * H:t * H + H],
                                self.GB[:, t * H:t * H + H],
                                self.TB[:, 48:64], OP.add))
                self.wv("t", sv, m["kt"])
                self.wv("t", ss, m["xat"])
                for k in range(4):
                    self.tp(P[6][:, k * 128:(k + 1) * 128],
                            KT[:, k * 128:(k + 1) * 128], self.IDENT[:, :])
                st.inc(self.tp(P[7][:, 0:128], KT[:, 512:640],
                               self.IDENT[:, :]))
                for k in range(5, 8):
                    r = self.tp(P[7][:, (k - 4) * 128:(k - 3) * 128],
                                KT[:, k * 128:(k + 1) * 128],
                                self.IDENT[:, :])
                st.inc(r)
                m["ktp"] = st.n
                self.wv("s", st, m["ktp"])
                if t > 1:
                    if "g_last" in pm[t - 2]:
                        self.wv("s", st, pm[t - 2]["g_last"])
                    if "halo" in pm[t - 2]:
                        self.wv("s", sv, pm[t - 2]["halo"])
                self.scopy(KTT[:, 0:512], P[6][:, 0:512])
                ss.inc(self.scopy(KTT[:, 512:1024], P[7][:, 0:512]))
                m["ktt"] = ss.n

            import os as _os9
            if "wholestatic" in _os9.environ.get("ATTNA_MODE", ""):
                return [s1, s2,
                        lambda: (static_q(0, 0), static_q(0, 1)),
                        lambda: (static_q(1, 0), static_q(1, 1)),
                        s5, s6, s7, s8]
            return [s1, s2,
                    lambda: static_q(0, 0), lambda: static_q(0, 1),
                    lambda: static_q(1, 0), lambda: static_q(1, 1),
                    s56, s78]

        def halo(t):
            d = t % 2
            KTT, VBT = KTTb[d], VBTb[d]
            col = 127 if t == 0 else 0
            base = 0 if t == 0 else 16
            kv = KTT.rearrange("p (k c) -> p k c", c=128)
            self.wv("v", ss, pm[t]["ktt"])
            self.vcopy(X1H[0:64, 0:16].rearrange("p (k o) -> p k o", o=2)
                       [:, :, 0:1], kv[0:64, :, col:col + 1])
            sv.inc(self.vcopy(X1H[0:64, 0:16]
                              .rearrange("p (k o) -> p k o", o=2)
                              [:, :, 1:2], kv[64:128, :, col:col + 1]))
            pm[t]["halo"] = sv.n
            self.w("y", sv)
            self.dma(self.HCOLS[:, base:base + 16], X1H[0:64, 0:16])
            self.dma(self.HROWS[t // (NT - 1):t // (NT - 1) + 1, :],
                     VBT[col:col + 1, :])
            self.w("v", self.sd)

        GB_ = [P[2], P[4]]
        S0 = [P[3][:, 0:128], P[5][:, 256:384]]
        S1 = [P[3][:, 128:256], P[5][:, 384:512]]

        def stage0(t, p):
            d = t % 2
            KT, KTT, VBT = KTb[d], KTTb[d], VBTb[d]
            s = p % 2
            for q, h in ((0, 2 * p), (1, 2 * p + 1)):
                kth = KT[:, h * DH:(h + 1) * DH]
                self.wv("v", st, t9m[q])
                self.wv("s", st, t9m[q])
                self.wv("s", sv, pm[t]["vbt"])
                self.vts(RS[q][s][:, DH:128], kth, self.LR[:, h:h + 1],
                         None, OP.mult)
                ss.inc(self.scopy(RS[q][s][:, 0:DH],
                                  VBT[:, h * DH:(h + 1) * DH]))
                rsm[q] = ss.n
                sv.inc(self.vts(KBAR[q][s], kth,
                                self.LR[:, 16 + h:17 + h], None, OP.mult))
            for q, h in ((0, 2 * p), (1, 2 * p + 1)):
                kfs = KTT[(h % 2) * 64:(h % 2) * 64 + 64,
                          (h // 2) * 128:(h // 2) * 128 + 128]
                self.wv("t", ss, pm[t]["ktt"])
                self.wv("t", sv, v2m[q])
                st.inc(self.mm(GB_[q][:, s * 128:s * 128 + 128], kfs, kfs))
                t1m[q] = st.n
            pm[t]["g_last"] = st.n
            for q, h in ((0, 2 * p), (1, 2 * p + 1)):
                self.wv("v", st, t1m[q])
                sv.inc(self.vtt(BUPS[q][s], GB_[q][:, s * 128:s * 128 + 128],
                                MSKA[:, h * 128:(h + 1) * 128], OP.mult))
                v2m[q] = sv.n

        def emit_pair(t, p):
            d = t % 2
            KT, KTT, VBT = KTb[d], KTTb[d], VBTb[d]
            hA, hB = 2 * p, 2 * p + 1
            sl = (p % 4) * 256
            s = p % 2

            def hS(h):
                return (KT[:, h * DH:(h + 1) * DH],
                        KTT[(h % 2) * 64:(h % 2) * 64 + 64,
                            (h // 2) * 128:(h // 2) * 128 + 128],
                        X2S[:, sl + (h % 2) * 128:sl + (h % 2) * 128 + 128])

            for q in (0, 1):
                self.wv("t", sv, v2m[q])
                self.wv("t", ss, rsm[q])
                if q == 1:
                    self.wv("t", sv, v10m[1])
                    self.wv("t", ss, s10m[1])
                st.inc(self.mm(S0[q], BUPS[q][s], RS[q][s]))
                t3m[q] = st.n
            for q in (0, 1):
                self.wv("v", st, t3m[q])
                sv.inc(self.vtt(X1S[q], RS[q][s], S0[q], OP.subtract))
                v4m[q] = sv.n
            for q in (0, 1):
                self.wv("t", sv, v4m[q])
                st.inc(self.mm(S1[q], BUPS[q][s], X1S[q]))
                t5m[q] = st.n
            for q in (0, 1):
                self.wv("v", st, t5m[q])
                sv.inc(self.vtt(X1S[q], RS[q][s], S1[q], OP.subtract))
                v6m[q] = sv.n
            for q in (0, 1):
                self.wv("t", sv, v6m[q])
                st.inc(self.mm(S0[q], BUPS[q][s], X1S[q]))
                t7m[q] = st.n
            for q, h in ((0, hA), (1, hB)):
                kth, kfs, x2h = hS(h)
                self.wv("v", st, t7m[q])
                self.wv("v", xsem[p % 4], xspm[p % 4])
                sv.inc(self.vtt(x2h, RS[q][s], S0[q], OP.subtract))
                v8m[q] = sv.n
            self.wv("y", sv, v8m[1])
            self.dma(self.USP.rearrange("(c p) w -> c p w", p=128)[t - 1]
                     [:, p * 256:(p + 1) * 256], X2S[:, sl:sl + 256],
                     sem=xsem[p % 4], weak=True)
            for q, h in ((0, hA), (1, hB)):
                kth, kfs, x2h = hS(h)
                mb = sl + q * 128
                self.wv("t", sv, v8m[1])
                self.wv("t", sv, v10m[1])
                self.wv("t", ss, s10m[1])
                st.inc(self.mm(P[5][0:DH, q * 256:q * 256 + DH],
                               x2h[:, DH:128], KBAR[q][s]))
                st.inc(self.mm(P[5][0:DH, q * 256 + DH:q * 256 + 128],
                               KBAR[q][s], x2h[:, 0:DH]))
                st.inc(self.tp(P[5][0:DH, q * 256 + 128:q * 256 + 256],
                               x2h[:, DH:128], self.IDENT[:, :]))
                t9m[q] = st.n
            for q, h in ((0, hA), (1, hB)):
                mb = sl + q * 128
                self.wv("v", st, t9m[1])
                self.wv("v", xsem[p % 4], xspm[p % 4])
                self.wv("s", st, t9m[1])
                self.wv("s", xsem[p % 4], xspm[p % 4])
                self.vtt(MCB[0:DH, mb:mb + DH],
                         P[5][0:DH, q * 256:q * 256 + DH],
                         self.IDLC[0:DH, h * 64:(h + 1) * 64], OP.add)
                self.vcopy(MCB[0:DH, mb + DH:mb + 128],
                           P[5][0:DH, q * 256 + DH:q * 256 + 128])
                sv.inc(self.vcopy(WNT[0:DH, mb:mb + 128],
                                  P[5][0:DH, q * 256 + 128:q * 256 + 256]))
                v10m[q] = sv.n
                s10m[q] = ss.n
            self.wv("y", sv, v10m[1])
            self.wv("y", ss, s10m[1])
            self.dma(self.WSP.rearrange("(c p) w -> c p w", p=DH)[t - 1]
                     [:, p * 256:(p + 1) * 256], WNT[0:DH, sl:sl + 256],
                     sem=xsem[p % 4], weak=True)
            self.dma(self.MCSP.rearrange("(c p) w -> c p w", p=DH)[t - 1]
                     [:, p * 256:(p + 1) * 256], MCB[0:DH, sl:sl + 256],
                     sem=xsem[p % 4], weak=True)
            xspm[p % 4] = xsem[p % 4].n

        for s in prelude_slices(0):
            if s is not None:
                s()
        halo(0)
        for s in prelude_slices(1):
            if s is not None:
                s()
        stage0(1, 0)
        for t in range(1, NT - 1):
            slices = prelude_slices(t + 1) if t + 1 < NT else [None] * 8
            for p in range(H // 2):
                emit_pair(t, p)
                if slices[p] is not None:
                    slices[p]()
                if p + 1 < H // 2:
                    stage0(t, p + 1)
                elif t + 1 <= NT - 2:
                    stage0(t + 1, 0)

        halo(NT - 1)
        self.w("v", self.sd)

    def attn_b(self):
        NT, NCH = self.NT, self.NCH
        sd, sv, ss, st = self.sd, self.sv, self.ss, self.st
        P = self.psum
        X, slot, ST = self.X, self.slot, self.ST
        self.ar_reset()
        YTb = [self.ar_lo(1024), self.ar_lo(1024)]
        ZTb = [self.ar_lo(1024), self.ar_lo(1024)]
        SQ2 = self.ar_lo(1024)
        HR = self.ar_lo(32)
        MCB = self.ar_hi(2048)
        MCB2 = self.ar_hi(2048)
        UBb = [self.ar_hi(2048), self.ar_hi(2048)]
        WTBb = [self.ar_hi(2048), self.ar_hi(2048)]
        VHL = self.ar_hi(1024)
        VHR = self.ar_hi(1024)
        YROW = self.ar_hi(1024)
        UROW = self.ar_hi(1024)
        SIN = self.ar_hi(1024)
        SSTA = self.ar_hi(1024)
        AGB = self.ar_hi(1024)

        def s_step(MC, bank_lo, bank_hi, state, wt=None):
            if wt:
                wt()
            for h in range(H):
                bank = bank_lo if h < 8 else bank_hi
                cb = (h % 8) * 64
                r = self.mm(bank[0:64, cb:cb + 64],
                            MC[0:DH, h * 128:h * 128 + DH],
                            state[0:64, h * 64:(h + 1) * 64])
            st.inc(r)
            smm = st.n
            self.wv("v", st, smm)
            MCv = MC[0:64, 0:2048].rearrange("p (h x) -> p h x", x=128)
            self.vtt(state[0:64, 0:512].rearrange("p (h x) -> p h x", x=64),
                     bank_lo[0:64, 0:512].rearrange("p (h x) -> p h x", x=64),
                     MCv[:, 0:8, DH:128], OP.add)
            sv.inc(self.vtt(state[0:64, 512:1024]
                            .rearrange("p (h x) -> p h x", x=64),
                            bank_hi[0:64, 0:512]
                            .rearrange("p (h x) -> p h x", x=64),
                            MCv[:, 8:16, DH:128], OP.add))
            return smm, sv.n

        lsem = [self.sdn, self.sdu]
        mcm = [0, 0]
        sv.inc(self.vmemset(SSTA[0:64, :], 0.0))
        self.dma(MCB[0:DH, :],
                 self.MCSP.rearrange("(c p) w -> c p w", p=DH)[0],
                 sem=lsem[0], weak=True)
        mcm[0] = lsem[0].n
        sadd = sv.n
        smm = 0
        for ch in range(NCH):
            MC = MCB if ch % 2 == 0 else MCB2
            if ch + 1 < NCH:
                MC2 = MCB2 if ch % 2 == 0 else MCB
                self.wv("y", st, smm)
                self.dma(MC2[0:DH, :],
                         self.MCSP.rearrange("(c p) w -> c p w",
                                             p=DH)[ch + 1],
                         sem=lsem[(ch + 1) % 2], weak=True)
                mcm[(ch + 1) % 2] = lsem[(ch + 1) % 2].n

            def wt(c=ch):
                self.wv("t", lsem[c % 2], mcm[c % 2])
                self.wv("t", sv, sadd)
            smm, sadd = s_step(MC, P[(ch % 2) * 2], P[(ch % 2) * 2 + 1],
                               SSTA, wt)
        self.w("y", sv)
        self.dma(self.AGI, SSTA[0:64, :])
        self.w("g", self.sd)
        self.sc.inc(self.nc.gpsimd.collective_compute(
            "AllGather", OP.bypass,
            replica_groups=[[0, 1, 2, 3], [4, 5, 6, 7]],
            ins=[self.AGI.opt()], outs=[self.AGO.opt()]))
        self.w("y", self.sc)
        for q in range(4):
            self.dma(AGB[0:64, :],
                     self.AGO.rearrange("(q p) w -> q p w", p=64)[q])
            self.w("v", self.sd)
            if q == 0:
                sv.inc(self.vts(SIN[0:64, :], AGB[0:64, :],
                                self.PSEL[0:64, 0:1], None, OP.mult))
            else:
                sv.inc(self.vstt(SIN[0:64, :], AGB[0:64, :],
                                 self.PSEL[0:64, q:q + 1], SIN[0:64, :],
                                 OP.mult, OP.add))
            self.w("y", sv)
        sv.inc(self.vcopy(SSTA[0:64, :], SIN[0:64, :]))
        sadd = sv.n
        smm = 0
        self.dma(HR[0:64, :], self.HCOLS)
        self.dma(VHL[0:1, :], self.HROWS[0:1, :])
        self.dma(VHR[0:1, :], self.HROWS[1:2, :])
        self.w("v", self.sd)
        ubm = [0, 0]
        onact = 0
        onred = 0
        ymk = [0, 0]
        ufx = [0, 0]

        def load_tile(tt):
            ch = tt - 1
            dd = tt % 2
            self.dma(UBb[dd], self.USP.rearrange("(c p) w -> c p w",
                                                 p=128)[ch],
                     sem=lsem[dd], weak=True)
            self.dma(WTBb[dd][0:DH, :],
                     self.WSP.rearrange("(c p) w -> c p w", p=DH)[ch],
                     sem=lsem[dd], weak=True)
            MC = MCB if ch % 2 == 0 else MCB2
            self.dma(MC[0:DH, :],
                     self.MCSP.rearrange("(c p) w -> c p w", p=DH)[ch],
                     sem=lsem[dd], weak=True)
            ubm[dd] = lsem[dd].n

        if NT > 2:
            load_tile(1)
        for t in range(NT):
            d = t % 2
            YT, ZT = YTb[d], ZTb[d]
            if 1 <= t <= NT - 2:
                ch = t - 1
                if t + 1 <= NT - 2:
                    self.wv("y", st, ufx[d])
                    self.wv("y", sv, onred)
                    load_tile(t + 1)
                self.wv("t", lsem[d], ubm[d])
                self.wv("t", sv, sadd)
                self.wv("t", sv, max(ymk))
                for h in range(H):
                    bank = P[4] if h < 8 else P[5]
                    cb = (h % 8) * 64
                    r = self.mm(bank[:, cb:cb + 64],
                                WTBb[d][0:DH, h * 128:(h + 1) * 128],
                                SSTA[0:64, h * 64:(h + 1) * 64])
                st.inc(r)
                ufx[d] = st.n
                self.wv("v", st, ufx[d])
                self.wv("v", ss, onact)
                UBv = UBb[d].rearrange("p (h x) -> p h x", x=128)
                NDv = self.ND[:, t * H:(t + 1) * H]
                for g in range(2):
                    Yv = YT[:, g * 512:(g + 1) * 512].rearrange(
                        "p (h x) -> p h x", x=64)
                    bank = (P[4], P[5])[g]
                    self.vtt(Yv, bank[:, 0:512].rearrange(
                        "p (h x) -> p h x", x=64),
                        UBv[:, g * 8:(g + 1) * 8, 0:DH], OP.add)
                    sv.inc(self.vtt(Yv, Yv,
                                    NDv[:, g * 8:(g + 1) * 8].unsqueeze(2)
                                    .broadcast_to([128, 8, DH]), OP.mult))
                ymk[d] = sv.n
                MC = MCB if ch % 2 == 0 else MCB2

                def wt2(c=ch, dd=d):
                    self.wv("t", lsem[dd], ubm[dd])
                    self.wv("t", sv, sadd)
                smm, sadd = s_step(MC, P[(ch % 2) * 2], P[(ch % 2) * 2 + 1],
                                   SSTA, wt2)
            elif t == 0:
                self.wv("v", ss, onact)
                sv.inc(self.vmemset(YT, 0.0))
                self.w("t", sv, self.sd)
                for h in range(H):
                    bank = P[4] if h < 8 else P[5]
                    r = self.mm(bank[0:1, (h % 8) * DH:(h % 8) * DH + DH],
                                HR[0:64, h:h + 1],
                                SIN[0:64, h * 64:(h + 1) * 64])
                st.inc(r)
                self.w("v", st)
                self.vtt(YROW[0:1, :], VHL[0:1, :], self.NEGIBB[0:1, :],
                         OP.mult)
                self.vtt(YROW[0:1, 0:512], YROW[0:1, 0:512], P[4][0:1, 0:512],
                         OP.add)
                sv.inc(self.vtt(YROW[0:1, 512:1024], YROW[0:1, 512:1024],
                                P[5][0:1, 0:512], OP.add))
                self.w("y", self.sv)
                self.dma(YT[127:128, :], YROW[0:1, :])
                self.w("v", self.sd)
                ymk[d] = sv.n
            else:
                self.wv("v", ss, onact)
                self.vmemset(YT, 0.0)
                sv.inc(self.vtt(SQ2[:, 0:16], self.TB[:, 16:32],
                                self.TB[:, 0:16], OP.mult))
                self.w("t", sv, self.sd)
                for h in range(H):
                    bank = P[4] if h < 8 else P[5]
                    r = self.mm(bank[0:1, (h % 8) * DH:(h % 8) * DH + DH],
                                HR[0:64, 16 + h:17 + h],
                                SSTA[0:64, h * 64:(h + 1) * 64])
                st.inc(r)
                self.w("v", st)
                self.vcopy(UROW[0:1, 0:512], P[4][0:1, 0:512])
                sv.inc(self.vcopy(UROW[0:1, 512:1024], P[5][0:1, 0:512]))
                self.nc.vector.drain()
                self._dirty["v"].clear()
                for h in range(H):
                    self.nc.vector.tensor_scalar(
                        UROW[0:1, h * DH:(h + 1) * DH],
                        UROW[0:1, h * DH:(h + 1) * DH],
                        SQ2[0:1, h:h + 1], None, OP.mult)
                self.nc.vector.drain()
                self.vtt(UROW[0:1, :], VHR[0:1, :], UROW[0:1, :], OP.subtract)
                self.nc.vector.drain()
                for h in range(H):
                    self.nc.vector.tensor_scalar(
                        YROW[0:1, h * DH:(h + 1) * DH],
                        UROW[0:1, h * DH:(h + 1) * DH],
                        self.ND[0:1, t * H + h:t * H + h + 1], None, OP.mult)
                self._dirty["v"].add(("ARENA", 0, 1 << 40))
                sv.inc(self.nc.vector.tensor_scalar(
                    SQ2[:, 0:1], SQ2[:, 0:1], 1.0, None, OP.mult))
                self.w("y", self.sv)
                self.dma(YT[0:1, :], YROW[0:1, :])
                self.w("v", self.sd)
                ymk[d] = sv.n
            self.wv("v", sv, ymk[d])
            sv.inc(self.vred(ST[:, 0:1], YT, OP.add))
            vr = sv.n
            self.wv("s", sv, vr)
            ss.inc(self.sact(ZT, YT, AF.Square, accum=ST[:, 1:2]))
            sqm = ss.n
            self.wv("v", ss, sqm)
            self.vts(ST[:, 2:3], ST[:, 0:1], INV_D, None, OP.mult)
            self.vts(ST[:, 3:4], ST[:, 1:2], INV_D, None, OP.mult)
            self.vtt(ST[:, 4:5], ST[:, 2:3], ST[:, 2:3], OP.mult)
            self.vtt(ST[:, 3:4], ST[:, 3:4], ST[:, 4:5], OP.subtract)
            self.vts(ST[:, 3:4], ST[:, 3:4], LN_EPS, None, OP.add)
            sv.inc(self.vrecip(ST[:, 3:4], ST[:, 3:4]))
            self.wv("s", sv, sv.n)
            ss.inc(self.sact(ST[:, 3:4], ST[:, 3:4], AF.Sqrt))
            self.wv("v", ss, ss.n)
            sv.inc(self.vstt(ST[:, 4:5], ST[:, 2:3], -1.0, ST[:, 3:4],
                             OP.mult, OP.mult))
            self.wv("s", sv, sv.n)
            ss.inc(self.sact(ZT, YT, AF.Identity, bias=ST[:, 4:5],
                             scale=ST[:, 3:4]))
            onact = ss.n
            self.wv("v", ss, onact)
            xo = X[:, t * D:(t + 1) * D]
            self.vtt(SQ2, ZT, self.ONWB, OP.mult)
            self.vtt(SQ2, SQ2, self.ONBB, OP.add)
            if t == 0 or t == NT - 1:
                msk = self.EML if t == 0 else self.EMR
                self.vtt(P[6][:, 0:512], SQ2[:, 0:512], xo[:, 0:512], OP.add)
                self.vtt(P[7][:, 0:512], SQ2[:, 512:1024], xo[:, 512:1024],
                         OP.add)
                self.vts(P[6][:, 0:512], P[6][:, 0:512], msk[:, 0:1],
                         None, OP.mult)
                self.vts(P[7][:, 0:512], P[7][:, 0:512], msk[:, 0:1],
                         None, OP.mult)
                self.vcopy(xo[:, 0:512], P[6][:, 0:512])
                sv.inc(self.vcopy(xo[:, 512:1024], P[7][:, 0:512]))
            else:
                sv.inc(self.vtt(xo, SQ2, xo, OP.add))
            onred = sv.n


def _host_inputs(inputs, NT):
    x = np.asarray(inputs["L0_attn_x"], dtype=np.float32)
    B, L, _ = x.shape
    NCH = NT - 2
    LC = NCH * C
    W = np.ascontiguousarray(np.asarray(inputs["L0_attn_W"], np.float32).T)
    W1T = np.ascontiguousarray(np.asarray(inputs["L0_ffn_w1"], np.float32).T)
    W2T = np.ascontiguousarray(np.asarray(inputs["L0_ffn_w2"], np.float32).T)
    cw = np.asarray(inputs["L0_ffn_conv_w"], np.float32).reshape(DF, 3)
    CWc = np.ascontiguousarray(cw.reshape(NCK, 128, 3).transpose(1, 0, 2)
                               .reshape(128, 3 * NCK))
    CBc = np.ascontiguousarray(np.asarray(inputs["L0_ffn_conv_b"], np.float32)
                               .reshape(NCK, 128).T)
    knw = np.tile(np.asarray(inputs["L0_attn_k_norm_w"], np.float32), H)[None]
    onw = np.asarray(inputs["L0_attn_o_norm_w"], np.float32)[None]
    onb = np.asarray(inputs["L0_attn_o_norm_b"], np.float32)[None]
    eta = np.asarray(inputs["L0_attn_eta_raw"], np.float32)[:, None]
    lml = np.asarray(inputs["L0_attn_lam_logit"], np.float32)[:, None]
    up01 = np.triu(np.ones((128, 128), np.float32), 1)
    ident = np.eye(128, dtype=np.float32)
    identb = ident.astype(ml_dtypes.bfloat16)
    hind = np.zeros((16, D), np.float32)
    for h in range(H):
        hind[h, h * DH:(h + 1) * DH] = 1.0
    onesr = np.ones((1, 128), np.float32)
    ii = np.arange(128, dtype=np.float32)
    expg = ii[None, :] - ii[:, None]
    expc1 = np.tile((ii + 1.0)[:, None], (1, 16)).astype(np.float32)
    expc2 = np.tile((127.0 - ii)[:, None], (1, 16)).astype(np.float32)

    common = dict(WTH=W, CWc=CWc, CBc=CBc, KNW=knw,
                  ONW=onw, ONB=onb, ETA=eta, LML=lml, UP01=up01, IDENT=ident,
                  IDENTB=identb, HIND=hind, ONESR=onesr, EXPG=expg,
                  EXPC1=expc1, EXPC2=expc2)
    maps = []
    for core in range(8):
        b = core // 4
        q = core % 4
        t0 = q * LC
        lo, hi = t0 - C, t0 + LC + C
        xs = np.zeros((NT * C, D), np.float32)
        slo, shi = max(0, lo), min(L, hi)
        xs[slo - lo:shi - lo] = x[b, slo:shi]
        eml = np.ones((128, 1), np.float32)
        emr = np.ones((128, 1), np.float32)
        if q == 0:
            eml[127] = 0.0
        if q == 3:
            emr[0] = 0.0
        psel = np.zeros((128, 4), np.float32)
        if q > 0:
            psel[:, q - 1] = 1.0
        m = dict(common)
        m.update(x0=xs, EMASKL=eml, EMASKR=emr, PSEL=psel,
                 W1THP=W1T[core * 128:(core + 1) * 128],
                 W2THP=W2T[core * 512:(core + 1) * 512])
        maps.append(m)
    return maps


_NC_CACHE = {}


def kernel(**inputs):
    x = np.asarray(inputs["L0_attn_x"])
    B, L, _ = x.shape
    assert B == 2
    NCH = L // (4 * C)
    NT = NCH + 2
    if NT not in _NC_CACHE:
        _NC_CACHE[NT] = KB(NT).build()
    nc = _NC_CACHE[NT]
    maps = _host_inputs(inputs, NT)
    res = run_bass_kernel_spmd(nc, maps, core_ids=list(range(8)))
    out = np.zeros((B, L, D), np.float32)
    LC = NCH * C
    for core in range(8):
        b = core // 4
        q = core % 4
        out[b, q * LC:(q + 1) * LC] = np.asarray(res.results[core]["XOUT"])
    return out


# version 23 (final)
# speedup vs baseline: 1.0499x; 1.0141x over previous
import contextlib
import numpy as np
import ml_dtypes
import concourse.bass as bass
import concourse.mybir as mybir
from concourse.bass_utils import run_bass_kernel_spmd

F32 = mybir.dt.float32
BF16 = mybir.dt.bfloat16
FP8 = mybir.dt.float8e4
AF = mybir.ActivationFunctionType
OP = mybir.AluOpType
AX = mybir.AxisListType.X

D = 1024
H = 16
DH = 64
DF = 4096
C = 128
NCK = 32
LN_EPS = 1e-5
M32 = float(np.float32(12582912.0))
M52 = float(np.float32(6755399441055744.0))
INV_D = float(np.float32(1.0 / D))
INV_DH = float(np.float32(1.0 / DH))
JNEU = 3
STOP_AFTER = None
DEBUG_FFN = False

AR_LO = 0
AR_CONST = 5260
AR_HI = 11404
AR_END = 31200


class Sem:
    def __init__(self, h):
        self.h = h
        self.n = 0

    def inc(self, instr, k=1):
        instr.then_inc(self.h, k)
        self.n += k
        return instr


class KB:

    def __init__(self, NT):
        self.NT = NT
        self.NW = NT * C
        self.NCH = NT - 2
        self.NMW = self.NW // 256
        self.nc = bass.Bass("TRN2", target_bir_lowering=False)
        self._dirty = {"v": set(), "s": set()}
        self._sdpend = {}
        self._dma_sems = set()
        self._lastw = {}

    def _rng(self, a):
        t = getattr(a, "tensor", None)
        if t is None or not hasattr(t, "name"):
            return None
        try:
            ap = a.ap
            off = a.offset
            span = 0
            for step, cnt in ap:
                span += abs(int(step)) * (int(cnt) - 1)
            esz = 2 if a.dtype in (BF16, mybir.dt.float16) else (
                1 if a.dtype == FP8 else 4)
            return (t.name, int(off) * esz, (int(off) + span + 1) * esz)
        except Exception:
            return (t.name, 0, 1 << 40)

    def _nm(self, aps):
        out = set()
        for a in aps:
            r = self._rng(a)
            if r is not None:
                out.add(r[0])
        return out

    def _pre(self, eng, key, aps):
        dirty = self._dirty[key]
        for a in aps:
            r = self._rng(a)
            if r is None:
                continue
            for (nm, lo, hi) in dirty:
                if nm == r[0] and r[1] < hi and lo < r[2]:
                    eng.drain()
                    dirty.clear()
                    return

    def vtt(self, out, a, bb, op):
        self._pre(self.nc.vector, "v", [out, a, bb])
        r = self.nc.vector.tensor_tensor(out, a, bb, op)
        self._dirty["v"].add(self._rng(out) or ("?",0,0))
        return r

    def vts(self, out, a, s1, s2, op0, op1=None):
        ext = [x for x in (s1, s2) if x is not None and not isinstance(x, float)]
        self._pre(self.nc.vector, "v", [out, a] + ext)
        if op1 is None:
            r = self.nc.vector.tensor_scalar(out, a, s1, None, op0)
        else:
            r = self.nc.vector.tensor_scalar(out, a, s1, s2, op0, op1)
        self._dirty["v"].add(self._rng(out) or ("?",0,0))
        return r

    def vstt(self, out, in0, scalar, in1, op0, op1):
        ext = [scalar] if not isinstance(scalar, float) else []
        self._pre(self.nc.vector, "v", [out, in0, in1] + ext)
        r = self.nc.vector.scalar_tensor_tensor(out, in0, scalar, in1, op0, op1)
        self._dirty["v"].add(self._rng(out) or ("?",0,0))
        return r

    def vred(self, out, in_, op, absval=False):
        self._pre(self.nc.vector, "v", [out, in_])
        kw = dict(axis=AX, op=op)
        if absval:
            kw["apply_absolute_value"] = True
        r = self.nc.vector.tensor_reduce(out, in_, **kw)
        self._dirty["v"].add(self._rng(out) or ("?",0,0))
        return r

    def vcopy(self, out, in_):
        self._pre(self.nc.vector, "v", [out, in_])
        r = self.nc.vector.tensor_copy(out, in_)
        self._dirty["v"].add(self._rng(out) or ("?",0,0))
        return r

    def vrecip(self, out, in_):
        self._pre(self.nc.vector, "v", [out, in_])
        r = self.nc.vector.reciprocal(out, in_)
        self._dirty["v"].add(self._rng(out) or ("?",0,0))
        return r

    def vmemset(self, ap, val):
        self._pre(self.nc.vector, "v", [ap])
        r = self.nc.vector.memset(ap, val)
        self._dirty["v"].add(self._rng(ap) or ("?",0,0))
        return r

    def sact(self, out, in_, func, bias=0.0, scale=1.0, accum=None):
        aps = [out, in_] + [x for x in (bias, scale, accum)
                            if x is not None and not isinstance(x, float)]
        self._pre(self.nc.scalar, "s", aps)
        kw = dict(bias=bias, scale=scale)
        if accum is not None:
            kw["accum_out"] = accum
        r = self.nc.scalar.activation(out, in_, func, **kw)
        self._dirty["s"].add(self._rng(out) or ("?",0,0))
        if accum is not None:
            self._dirty["s"].add(self._rng(accum) or ("?",0,0))
        return r

    def scopy(self, out, in_):
        self._pre(self.nc.scalar, "s", [out, in_])
        r = self.nc.scalar.copy(out, in_)
        self._dirty["s"].add(self._rng(out) or ("?",0,0))
        return r

    def mm(self, out, lhsT, rhs, start=True, stop=True):
        return self.nc.tensor.matmul(out, lhsT, rhs, start=start, stop=stop)

    def tp(self, out, in_, ident):
        return self.nc.tensor.transpose(out, in_, ident)

    def w(self, engs, *sems):
        m = dict(v=self.nc.vector, s=self.nc.scalar, t=self.nc.tensor,
                 y=self.nc.sync, g=self.nc.gpsimd)
        for e in engs:
            for sm in sems:
                key = (e, id(sm))
                if sm.n > 0 and self._lastw.get(key, -1) < sm.n:
                    m[e].wait_ge(sm.h, sm.n)
                    self._lastw[key] = sm.n
                    if id(sm) in self._dma_sems and e != "y":
                        ps = self._psem(e)
                        if ps is not None:
                            self._sdpend[(e, id(sm))] = ps.n

    def wv(self, e, sm, val):
        if val <= 0:
            return
        key = (e, id(sm))
        if self._lastw.get(key, -1) < val:
            m = dict(v=self.nc.vector, s=self.nc.scalar, t=self.nc.tensor,
                     y=self.nc.sync, g=self.nc.gpsimd)
            m[e].wait_ge(sm.h, val)
            self._lastw[key] = val
            if id(sm) in self._dma_sems and e != "y":
                ps = self._psem(e)
                if ps is not None:
                    self._sdpend[(e, id(sm))] = ps.n

    def _psem(self, e):
        return {"v": getattr(self, "sv", None), "s": getattr(self, "ss", None),
                "t": getattr(self, "st", None),
                "g": getattr(self, "sc", None)}.get(e)

    def barrier(self):
        nc = self.nc
        for e in "vstg":
            for sm in (self.sd, self.sdw, self.sdx, self.sdn, self.sdu,
                       self.sv, self.ss, self.st, self.sg16):
                if sm.n > 0:
                    dict(v=nc.vector, s=nc.scalar, t=nc.tensor,
                         g=nc.gpsimd)[e].wait_ge(sm.h, sm.n)
        self.sv.inc(nc.vector.nop())
        self.ss.inc(nc.scalar.nop())
        self.st.inc(nc.tensor.nop())
        self.sb.inc(nc.gpsimd.nop())
        for sm in (self.sv, self.ss, self.st, self.sb):
            nc.sync.wait_ge(sm.h, sm.n)
        self._sdpend.clear()

    def dma(self, out, in_, sem=None, weak=False):
        sem = sem or self.sd
        eng = dict(v=self.nc.vector, s=self.nc.scalar, t=self.nc.tensor,
                   g=self.nc.gpsimd)
        for (e, smid), mark in list(self._sdpend.items()):
            if smid != id(sem):
                continue
            ps = self._psem(e)
            if ps.n == mark:
                ps.inc(eng[e].nop())
            tgt = mark + 1 if weak else ps.n
            key = ("y", id(ps))
            if self._lastw.get(key, -1) < tgt:
                self.nc.sync.wait_ge(ps.h, tgt)
                self._lastw[key] = tgt
            del self._sdpend[(e, smid)]
        r = self.nc.sync.dma_start(out, in_)
        sem.inc(r, 16)
        return r

    def ar_lo(self, n, ty=F32):
        o = self._lo
        self._lo += n
        assert self._lo <= AR_CONST, f"arena-lo overflow {self._lo}"
        v = self.AR[:, o:o + n]
        return v.bitcast(ty) if ty is not F32 else v

    def ar_hi(self, n, ty=F32):
        o = self._hi
        self._hi += n
        assert self._hi <= AR_END, f"arena-hi overflow {self._hi}"
        v = self.AR[:, o:o + n]
        return v.bitcast(ty) if ty is not F32 else v

    def ar_reset(self):
        self._lo = AR_LO
        self._hi = AR_HI

    def build(self):
        nc = self.nc
        NT, NW, NCH = self.NT, self.NW, self.NCH
        di = lambda n, s, ty: nc.dram_tensor(n, s, ty, kind="ExternalInput").ap()
        dxx = lambda n, s, ty: nc.dram_tensor(n, s, ty).ap()
        I = {}
        for nm, sh, ty in [
                ("x0", [NW, D], F32), ("WTH", [D, D], F32),
                ("W1THP", [128, DF], F32), ("W2THP", [DF // 8, D], F32),
                ("CWc", [128, 3 * NCK], F32), ("CBc", [128, NCK], F32),
                ("KNW", [1, D], F32), ("ONW", [1, D], F32), ("ONB", [1, D], F32),
                ("ETA", [16, 1], F32), ("LML", [16, 1], F32),
                ("UP01", [128, 128], F32), ("IDENT", [128, 128], F32),
                ("IDENTB", [128, 128], BF16), ("HIND", [16, D], F32),
                ("EMASKL", [128, 1], F32), ("EMASKR", [128, 1], F32),
                ("PSEL", [128, 4], F32), ("ONESR", [1, 128], F32),
                ("EXPG", [128, 128], F32), ("EXPC1", [128, 16], F32),
                ("EXPC2", [128, 16], F32)]:
            I[nm] = di(nm, sh, ty)
        self.I = I
        self.XOUT = nc.dram_tensor("XOUT", [NCH * C, D], F32,
                                   kind="ExternalOutput").ap()
        self.W1Q = dxx("W1Q", [D, DF], FP8)
        self.W2Q = dxx("W2Q", [DF, D], FP8)
        self.XQSP = dxx("XQSP", [D, NW], BF16)
        self.N2SP = dxx("N2SP", [DF, NW], BF16)
        self.USP = dxx("USP", [NCH * C, H * C], F32)
        self.WSP = dxx("WSP", [NCH * DH, H * C], F32)
        self.MCSP = dxx("MCSP", [NCH * DH, H * 2 * DH], F32)
        self.AGI = dxx("AGI", [64, H * DH], F32)
        self.W1QP = dxx("W1QP", [128, DF], FP8)
        self.W2QP = dxx("W2QP", [DF // 8, D], FP8)
        self.PSCI = dxx("PSCI", [1, 2], F32)
        self.PSCO = dxx("PSCO", [8, 2], F32)
        self.AGO = dxx("AGO", [256, H * DH], F32)
        self.HROWS = dxx("HROWS", [2, D], F32)
        if DEBUG_FFN:
            self.DBGY = dxx("DBGY", [DF, self.NW], F32)
            self.DBGS = dxx("DBGS", [DF, self.NW], F32)
            self.DBGQ = dxx("DBGQ", [128, self.NW], F32)
        self.HCOLS = dxx("HCOLS", [64, 32], F32)

        with contextlib.ExitStack() as ctx:
            en = ctx.enter_context
            SS = lambda n, s, ty: en(nc.sbuf_tensor(n, s, ty))
            self.sd = Sem(en(nc.semaphore("sd")))
            self.sdw = Sem(en(nc.semaphore("sdw")))
            self.sdx = Sem(en(nc.semaphore("sdx")))
            self.sdn = Sem(en(nc.semaphore("sdn")))
            self.sdu = Sem(en(nc.semaphore("sdu")))
            self.sv = Sem(en(nc.semaphore("sv")))
            self.ss = Sem(en(nc.semaphore("ss")))
            self.st = Sem(en(nc.semaphore("st")))
            self.sg16 = Sem(en(nc.semaphore("sg16")))
            self.sc = Sem(en(nc.semaphore("scol")))
            self.sb = Sem(en(nc.semaphore("sbar")))
            self._dma_sems = {id(s) for s in
                              (self.sd, self.sdw, self.sdx, self.sdn,
                               self.sdu, self.sg16)}

            self.X = SS("X", [128, NT * D], F32)
            self.slot = list(range(NT))
            self.AR = SS("ARENA", [128, AR_END], F32)
            self.IDENT = SS("sID", [128, 128], F32)
            self.IDB = SS("sIDB", [128, 128], BF16)
            self.UP = SS("sUP", [128, 128], F32)
            self.CW = SS("sCW", [128, 3 * NCK], F32)
            self.CB = SS("sCB", [128, NCK], F32)
            self.SC = SS("SCc", [128, 16], F32)
            self.TB = SS("TB", [128, 96], F32)
            self.LR = SS("LRt", [128, 48], F32)
            self.BR = SS("BRc", [16, 8], F32)
            self.ST = SS("STATS", [128, 8 * NT], F32)
            self.GB = SS("GBc", [128, NT * H], F32)
            self.ND = SS("NDc", [128, NT * H], F32)
            self.EML = SS("sEML", [128, 1], F32)
            self.EMR = SS("sEMR", [128, 1], F32)
            self.PSEL = SS("sPSEL", [128, 4], F32)
            self.ONER = SS("sONER", [1, 128], F32)
            self.G2C = SS("sG2C", [128, 32], F32)
            self.psum = [en(nc.psum_tensor(f"pb{i}", [128, 512], F32))
                         for i in range(8)]
            block = en(nc.Block())

            @block.sync
            def _(sync):
                self.program()
        return nc

    def program(self):
        self.ar_reset()
        self.KNWB = self.AR[:, AR_CONST + 0 * D:AR_CONST + 1 * D]
        self.ONWB = self.AR[:, AR_CONST + 1 * D:AR_CONST + 2 * D]
        self.ONBB = self.AR[:, AR_CONST + 2 * D:AR_CONST + 3 * D]
        self.NEGBB = self.AR[:, AR_CONST + 3 * D:AR_CONST + 4 * D]
        self.NEGIBB = self.AR[:, AR_CONST + 4 * D:AR_CONST + 5 * D]
        self.IDLC = self.AR[:, AR_CONST + 5 * D:AR_CONST + 5 * D + 1024]

        self.p0_consts()
        self.p1_wquant_dist()
        self.barrier()
        self.ffn()
        self.barrier()
        if STOP_AFTER != "ffn1":
            self.attn_a()
            self.barrier()
            if STOP_AFTER == "attn_a":
                pass
            else:
                self.attn_b()
            self.barrier()
            if STOP_AFTER not in ("attn", "attn_a"):
                self.ffn(reduced=True)
                self.barrier()
        NT = self.NT
        for t in range(1, NT - 1):
            self.dma(self.XOUT.rearrange("(t p) f -> t p f", p=128)[t - 1],
                     self.X[:, t * D:(t + 1) * D])
        self.nc.sync.wait_ge(self.sd.h, self.sd.n)

    def p0_consts(self):
        I, P = self.I, self.psum
        sd, sv, ss, st = self.sd, self.sv, self.ss, self.st
        self.ar_reset()
        ROW = self.ar_lo(512)
        HB = self.ar_lo(1024)
        STG = self.ar_lo(1024)
        self.dma(self.IDENT[:, :], I["IDENT"])
        self.dma(self.IDB[:, :], I["IDENTB"])
        self.dma(self.UP[:, :], I["UP01"])
        self.dma(self.CW[:, :], I["CWc"])
        self.dma(self.CB[:, :], I["CBc"])
        self.dma(self.EML[:, :], I["EMASKL"])
        self.dma(self.EMR[:, :], I["EMASKR"])
        self.dma(self.PSEL[:, :], I["PSEL"])
        self.dma(self.BR[0:16, 0:1], I["LML"])
        self.dma(self.BR[0:16, 1:2], I["ETA"])
        self.dma(self.ONER[0:1, :], I["ONESR"])
        self.dma(HB[0:16, :], I["HIND"])
        self.dma(self.X[:, 0:self.NT * D].rearrange("p (t f) -> p t f", f=D),
                 I["x0"].rearrange("(t p) f -> p t f", p=128))
        self.dma(self.ST[:, 0:16], I["EXPC1"])
        self.dma(self.ST[:, 16:32], I["EXPC2"])

        self.w("v", sd)
        self.vmemset(self.SC[:, 0:1], 1.0)
        self.vts(self.SC[:, 1:2], self.SC[:, 0:1],
                 float(np.float32(2.0 ** -30)), 1.0, OP.add, OP.subtract)
        self.vts(self.SC[:, 1:2], self.SC[:, 1:2],
                 float(np.float32(2.0 ** 30)), None, OP.mult)
        self.vts(self.SC[:, 2:3], self.SC[:, 1:2],
                 float(np.float32(M52 - M32)), M32, OP.mult, OP.add)
        self.vts(self.SC[:, 8:9], self.SC[:, 1:2], -1.0, 1.0, OP.mult, OP.add)
        self.vts(self.SC[:, 9:10], self.SC[:, 1:2],
                 float(np.float32(2.0 ** -29)), None, OP.mult)
        self.vtt(self.SC[:, 3:4], self.SC[:, 8:9], self.SC[:, 9:10], OP.add)
        sv.inc(self.vts(self.SC[:, 3:4], self.SC[:, 3:4],
                        float(np.float32(2.0 ** -23)), None, OP.mult))
        self.w("s", sd, sv)
        self.sact(self.BR[0:16, 0:1], self.BR[0:16, 0:1], AF.Sigmoid)
        ss.inc(self.sact(self.BR[0:16, 1:2], self.BR[0:16, 1:2], AF.Tanh))
        self.w("v", ss)
        self.vts(self.BR[0:16, 7:8], self.BR[0:16, 0:1], -1.0, 1.0,
                 OP.mult, OP.add)
        self.vtt(self.BR[0:16, 2:3], self.BR[0:16, 1:2], self.BR[0:16, 7:8],
                 OP.mult)
        self.vts(self.BR[0:16, 2:3], self.BR[0:16, 2:3],
                 float(np.float32(-1.0 / DH)), None, OP.mult)
        self.vts(self.BR[0:16, 3:4], self.BR[0:16, 2:3], -1.0, None, OP.mult)
        self.vrecip(self.BR[0:16, 4:5], self.BR[0:16, 2:3])
        sv.inc(self.vts(self.BR[0:16, 4:5], self.BR[0:16, 4:5], -1.0, None,
                        OP.mult))
        self.w("s", sv)
        ss.inc(self.sact(self.BR[0:16, 5:6], self.BR[0:16, 0:1], AF.Ln))
        self.w("s", ss)
        ss.inc(self.sact(self.BR[0:16, 6:7], self.BR[0:16, 5:6], AF.Exp,
                         scale=float(C)))
        self.w("y", ss)
        for j, col in enumerate([0, 2, 3, 4, 5, 6]):
            self.dma(ROW[0:1, 16 * j:16 * j + 16], self.BR[0:16, col:col + 1])
        self.w("t", sd)
        st.inc(self.mm(P[7][:, 0:96], self.ROWONES(), ROW[0:1, 0:96]))
        self.w("v", st)
        sv.inc(self.vcopy(self.TB[:, :], P[7][:, 0:96]))
        self.w("v", sv)
        self.vtt(self.LR[:, 32:48], self.ST[:, 0:16], self.TB[:, 64:80], OP.mult)
        sv.inc(self.vtt(self.LR[:, 16:32], self.ST[:, 16:32], self.TB[:, 64:80],
                        OP.mult))
        self.w("s", sv)
        self.sact(self.LR[:, 32:48], self.LR[:, 32:48], AF.Exp)
        ss.inc(self.sact(self.LR[:, 16:32], self.LR[:, 16:32], AF.Exp))
        self.w("v", ss)
        sv.inc(self.vtt(self.LR[:, 0:16], self.LR[:, 32:48], self.TB[:, 32:48],
                        OP.mult))
        self.w("t", sd, ss, sv)
        st.inc(self.mm(P[6][0:1, 0:512], self.BR[0:16, 3:4], HB[0:16, 0:512]))
        st.inc(self.mm(P[5][0:1, 0:512], self.BR[0:16, 3:4], HB[0:16, 512:1024]))
        self.w("v", st)
        self.vcopy(STG[0:1, 0:512], P[6][0:1, 0:512])
        sv.inc(self.vcopy(STG[0:1, 512:1024], P[5][0:1, 0:512]))
        self.w("t", sv)

        def bcast(dst, row):
            st.inc(self.mm(P[0][:, 0:512], self.ROWONES(), row[0:1, 0:512]))
            st.inc(self.mm(P[1][:, 0:512], self.ROWONES(), row[0:1, 512:1024]))
            self.w("v", st)
            self.vcopy(dst[:, 0:512], P[0][:, 0:512])
            sv.inc(self.vcopy(dst[:, 512:1024], P[1][:, 0:512]))
            self.w("t", sv)

        bcast(self.NEGBB, STG)
        st.inc(self.mm(P[6][0:1, 0:512], self.BR[0:16, 4:5], HB[0:16, 0:512]))
        st.inc(self.mm(P[5][0:1, 0:512], self.BR[0:16, 4:5], HB[0:16, 512:1024]))
        self.w("v", st)
        self.vcopy(STG[0:1, 0:512], P[6][0:1, 0:512])
        sv.inc(self.vcopy(STG[0:1, 512:1024], P[5][0:1, 0:512]))
        self.w("t", sv)
        bcast(self.NEGIBB, STG)
        for name, dst in [("KNW", self.KNWB), ("ONW", self.ONWB),
                          ("ONB", self.ONBB)]:
            self.dma(STG[0:1, 0:1024], I[name])
            self.w("t", sd)
            bcast(dst, STG)
        self.w("v", self.ss)
        for h in range(H):
            sv.inc(self.vts(self.IDLC[0:64, h * 64:(h + 1) * 64],
                            self.IDENT[0:64, 0:64],
                            self.TB[0:64, 80 + h:81 + h], None, OP.mult))
        self.barrier()

    def ROWONES(self):
        return self.ONER[0:1, :]

    def p1_wquant_dist(self):
        sd, sv, ss, st = self.sd, self.sv, self.ss, self.st
        P = self.psum
        nc = self.nc
        self.ar_reset()
        W1P = self.ar_hi(4096)
        W2P = self.ar_hi(4096)
        WQ81 = self.ar_hi(1024, FP8)
        WQ82 = self.ar_hi(1024, FP8)
        ACC2 = self.AR[:, 8:10]
        TMP = self.AR[0:8, 10:14]
        RR = self.AR[0:1, 14:18]
        G8 = self.AR[0:8, 18:20]
        nelem = float(D * DF)
        self.dma(W1P, self.I["W1THP"], sem=self.sdn)
        self.dma(W2P.rearrange("p (c f) -> p c f", f=D),
                 self.I["W2THP"].rearrange("(c p) f -> p c f", p=128),
                 sem=self.sdn)
        self.w("v", self.sdn)
        self.vred(ACC2[:, 0:1], W1P, OP.add, absval=True)
        sv.inc(self.vred(ACC2[:, 1:2], W2P, OP.add, absval=True))
        self.w("t", sv)
        st.inc(self.mm(P[7][0:1, 0:2], self.SC[:, 0:1], ACC2))
        self.w("v", st)
        sv.inc(self.vcopy(TMP[0:1, 0:2], P[7][0:1, 0:2]))
        self.w("y", sv)
        self.dma(self.PSCI, TMP[0:1, 0:2])
        self.w("g", self.sd)
        self.sc.inc(nc.gpsimd.collective_compute(
            "AllGather", OP.bypass,
            replica_groups=[[0, 1, 2, 3, 4, 5, 6, 7]],
            ins=[self.PSCI.opt()], outs=[self.PSCO.opt()]))
        self.w("y", self.sc)
        self.dma(G8[0:8, 0:2], self.PSCO)
        self.w("t", self.sd)
        st.inc(self.mm(P[7][0:1, 2:4], self.SC[0:8, 0:1], G8[0:8, 0:2]))
        self.w("v", st)
        self.vcopy(TMP[0:1, 0:2], P[7][0:1, 2:4])
        self.vts(TMP[0:1, 0:2], TMP[0:1, 0:2],
                 float(np.float32(1.0 / nelem)), 1e-5, OP.mult, OP.max)
        sv.inc(self.vrecip(TMP[0:1, 2:4], TMP[0:1, 0:2]))
        self.w("v", self.sv)
        self.vcopy(RR[0:1, 0:1], TMP[0:1, 0:1])
        self.vcopy(RR[0:1, 1:2], TMP[0:1, 2:3])
        self.vcopy(RR[0:1, 2:3], TMP[0:1, 1:2])
        sv.inc(self.vcopy(RR[0:1, 3:4], TMP[0:1, 3:4]))
        self.w("t", sv)
        st.inc(self.mm(P[7][:, 4:8], self.ROWONES(), RR[0:1, 0:4]))
        self.w("s", st)
        ss.inc(self.scopy(self.SC[:, 4:8], P[7][:, 4:8]))
        for W, WQ8, col in ((W1P, WQ81, 5), (W2P, WQ82, 7)):
            self.w("s", self.ss)
            self.wv("s", sv, sv.n)
            ss.inc(self.sact(W, W, AF.Identity,
                             scale=self.SC[:, col:col + 1],
                             bias=self.SC[:, 2:3]))
            self.wv("v", ss, ss.n)
            self.vts(W, W, self.SC[:, 2:3], 1.0, OP.subtract, OP.min)
            sv.inc(self.vts(WQ8, W, -1.0, None, OP.max))
            self.wv("y", sv, sv.n)
        self.dma(self.W1QP, WQ81)
        self.dma(self.W2QP.rearrange("(c p) f -> p c f", p=128),
                 WQ82.rearrange("p (c f) -> p c f", f=D))
        self.w("g", self.sd)
        self.sc.inc(nc.gpsimd.collective_compute(
            "AllGather", OP.bypass,
            replica_groups=[[0, 1, 2, 3, 4, 5, 6, 7]],
            ins=[self.W1QP.opt()], outs=[self.W1Q.opt()]))
        self.sc.inc(nc.gpsimd.collective_compute(
            "AllGather", OP.bypass,
            replica_groups=[[0, 1, 2, 3, 4, 5, 6, 7]],
            ins=[self.W2QP.opt()], outs=[self.W2Q.opt()]))

    def p1_wquant(self, WTH, WQ, nchunks, width, sccol):
        sd, sv, ss, st = self.sd, self.sv, self.ss, self.st
        P = self.psum
        self.ar_reset()
        WS = [self.ar_hi(width), self.ar_hi(width)]
        WQ8 = [self.ar_hi(width // 4, FP8), self.ar_hi(width // 4, FP8)]
        nelem = nchunks * 128 * width
        ACC = self.AR[:, 8:9]
        TMP = self.AR[:, 9:10]
        RCP = self.AR[:, 10:11]
        lsem = [self.sdn, self.sdu]
        lmark = [0, 0]

        def load(c):
            self.dma(WS[c % 2], WTH.rearrange("(c p) f -> c p f", p=128)[c],
                     sem=lsem[c % 2], weak=True)
            lmark[c % 2] = lsem[c % 2].n

        load(0)
        load(1)
        for c in range(nchunks):
            self.wv("v", lsem[c % 2], lmark[c % 2])
            if c == 0:
                sv.inc(self.vred(ACC, WS[0], OP.add, absval=True))
            else:
                self.vred(TMP, WS[c % 2], OP.add, absval=True)
                sv.inc(self.vtt(ACC, ACC, TMP, OP.add))
            if c + 2 < nchunks:
                self.wv("y", sv, sv.n)
                load(c + 2)
        self.w("t", sv)
        st.inc(self.mm(P[7][0:1, 0:1], ACC, self.SC[:, 0:1]))
        self.w("v", st)
        self.vcopy(TMP[0:1, 0:1], P[7][0:1, 0:1])
        self.vts(TMP[0:1, 0:1], TMP[0:1, 0:1],
                 float(np.float32(1.0 / nelem)), 1e-5, OP.mult, OP.max)
        sv.inc(self.vrecip(RCP[0:1, 0:1], TMP[0:1, 0:1]))
        RR = self.AR[0:1, 0:4]
        self.w("y", sv)
        self.dma(RR[0:1, 0:1], TMP[0:1, 0:1])
        self.dma(RR[0:1, 1:2], RCP[0:1, 0:1])
        self.w("t", sd)
        st.inc(self.mm(P[7][:, 0:2], self.ROWONES(), RR[0:1, 0:2]))
        self.w("s", st)
        ss.inc(self.scopy(self.SC[:, sccol:sccol + 2], P[7][:, 0:2]))
        qmark = [0, 0]
        vmark = [0, 0]
        ssem = [self.sdx, self.sdw]
        smark = [0, 0]
        load(0)
        load(1)
        lmark2 = [lmark[0], lmark[1]]
        for c in range(nchunks):
            b = c % 2
            self.wv("s", lsem[b], lmark2[b])
            ss.inc(self.sact(WS[b], WS[b], AF.Identity,
                             scale=self.SC[:, sccol + 1:sccol + 2],
                             bias=self.SC[:, 2:3]))
            qmark[b] = ss.n
            self.wv("v", ss, qmark[b])
            self.wv("v", ssem[b], smark[b])
            self.vts(WS[b], WS[b], self.SC[:, 2:3], 1.0, OP.subtract, OP.min)
            sv.inc(self.vts(WQ8[b], WS[b], -1.0, None, OP.max))
            vmark[b] = sv.n
            self.wv("y", sv, vmark[b])
            if c + 2 < nchunks:
                load(c + 2)
                lmark2[b] = lmark[b]
            self.dma(WQ.rearrange("(c p) f -> c p f", p=128)[c], WQ8[b],
                     sem=ssem[b], weak=True)
            smark[b] = ssem[b].n
        self.w("v", self.sd)

    def ffn(self, reduced=False):
        NT, NW, NMW = self.NT, self.NW, self.NMW
        if reduced:
            NMW = (NW - 2 * C) // 256
        wlo = (C - 1) if reduced else None
        sd, sv, ss, st = self.sd, self.sv, self.ss, self.st
        P = self.psum
        X, slot, ST = self.X, self.slot, self.ST
        self.ar_reset()
        XN = self.ar_lo(1024)
        XQB = self.ar_lo(1024, BF16)
        XQE = self.ar_lo(512, BF16)
        CVAB = self.ar_lo(516)
        CVa = CVAB[:, 0:258]
        CVb = CVAB[:, 258:516]
        ABa = self.ar_lo(256)
        ABb = self.ar_lo(256)
        AMXa = self.ar_lo(256)
        AMXb = self.ar_lo(256)
        S1B = self.ar_lo(258)
        S2B = self.ar_lo(256)
        N2P = [self.ar_lo(256, BF16), self.ar_lo(256, BF16)]
        CVP = CVAB[:, 0:512]
        ROWS = XN[:, 0:256]
        S1RT = self.ar_lo(128)
        W1R = self.ar_hi(8192, FP8)
        XQW = self.ar_hi(1040, BF16)
        YW = self.ar_hi(NCK * 258)
        S1ROW = self.ar_hi(NW)

        for t in range(NT):
            xt = X[:, slot[t] * D:(slot[t] + 1) * D]
            self.vred(ST[:, 0 * NT + t:0 * NT + t + 1], xt, OP.add)
            self.vred(ST[:, 4 * NT + t:4 * NT + t + 1], xt, OP.max)
            sv.inc(self.vred(ST[:, 5 * NT + t:5 * NT + t + 1], xt, OP.min))
            ss.inc(self.sact(XN, xt, AF.Square,
                             accum=ST[:, 1 * NT + t:1 * NT + t + 1]))
        self.w("v", ss, sv)
        mu = ST[:, 2 * NT:3 * NT]
        var = ST[:, 6 * NT:7 * NT]
        rstd = ST[:, 7 * NT:8 * NT]
        s1 = ST[:, 3 * NT:4 * NT]
        qs = ST[:, 4 * NT:5 * NT]
        qb = ST[:, 5 * NT:6 * NT]
        self.vts(mu, ST[:, 0:NT], INV_D, None, OP.mult)
        self.vts(var, ST[:, NT:2 * NT], INV_D, None, OP.mult)
        self.vtt(rstd, mu, mu, OP.mult)
        self.vtt(var, var, rstd, OP.subtract)
        self.vts(var, var, LN_EPS, None, OP.add)
        sv.inc(self.vrecip(var, var))
        self.w("s", sv)
        ss.inc(self.sact(rstd, var, AF.Sqrt))
        self.w("v", ss)
        self.vtt(s1, ST[:, 4 * NT:5 * NT], mu, OP.subtract)
        self.vtt(var, mu, ST[:, 5 * NT:6 * NT], OP.subtract)
        self.vtt(s1, s1, var, OP.max)
        self.vtt(s1, s1, rstd, OP.mult)
        self.vts(s1, s1, 1e-5, None, OP.max)
        g1 = ST[:, 0 * NT:1 * NT]
        self.vstt(g1, s1, float(np.float32(1.0 / 127.0)),
                  self.SC[:, 4:5].broadcast_to([128, NT]), OP.mult, OP.mult)
        self.vrecip(s1, s1)
        self.vts(s1, s1, 127.0, None, OP.mult)
        self.vtt(qs, rstd, s1, OP.mult)
        self.vtt(qb, mu, qs, OP.mult)
        sv.inc(self.vts(qb, qb, -1.0, None, OP.mult))
        self.w("t", sv)
        st.inc(self.tp(P[7][0:NT, 0:128], g1, self.IDENT[:, :]))
        self.w("s", st)
        ss.inc(self.scopy(S1RT[0:NT, :], P[7][0:NT, 0:128]))
        p7s_mark = ss.n
        self.w("y", ss)
        self.dma(S1ROW[0:1, :].rearrange("a (t c) -> a t c", c=128),
                 S1RT[0:NT, :])
        vQ_mark = [0] * NT
        vE_mark = [0] * NT
        yD_mark = [0] * NT
        tT_mark = [0] * NT
        for t in range(NT):
            xt = X[:, slot[t] * D:(slot[t] + 1) * D]
            half = t % 2
            xq = XQB.bitcast(F32)[:, half * 512:half * 512 + 512].bitcast(BF16)
            xe = XQE
            bank = P[6] if half == 0 else P[7]
            if t >= 1:
                self.wv("s", sv, vQ_mark[t - 1])
            ss.inc(self.sact(XN, xt, AF.Identity, bias=qb[:, t:t + 1],
                             scale=qs[:, t:t + 1]))
            self.wv("v", ss, ss.n)
            if t >= 2:
                self.wv("v", st, tT_mark[t - 2])
            sv.inc(self.vts(xq, XN, self.SC[:, 2:3], self.SC[:, 2:3],
                            OP.add, OP.subtract))
            vQ_mark[t] = sv.n
            self.wv("t", sv, vQ_mark[t])
            if t == 1:
                self.wv("t", ss, p7s_mark)
            if t >= 2:
                self.wv("t", sv, vE_mark[t - 2])
            for k in range(8):
                r = self.tp(bank[:, k * 64:k * 64 + 64].bitcast(BF16),
                            xq[:, k * 128:(k + 1) * 128], self.IDB[:, :])
            st.inc(r)
            tT_mark[t] = st.n
            self.wv("v", st, tT_mark[t])
            if t >= 1:
                self.wv("v", self.sdx, yD_mark[t - 1])
            sv.inc(self.vcopy(xe, bank[:, 0:512].bitcast(BF16)))
            vE_mark[t] = sv.n
            self.wv("y", sv, vE_mark[t])
            self.dma(self.XQSP.rearrange("(k p) w -> p k w", p=128)
                     [:, :, t * C:(t + 1) * C],
                     xe.rearrange("p (k c) -> p k c", k=8), sem=self.sdx,
                     weak=True)
            yD_mark[t] = self.sdx.n
        self.w("y", self.st, self.sdx, self.sc)
        self.dma(W1R.rearrange("p (k f) -> p k f", k=8),
                 self.W1Q.rearrange("(k p) f -> p k f", p=128), sem=self.sdw)
        w1r_mark = self.sdw.n
        bank_mark = [0, 0, 0, 0]
        p7_mark = vE_mark[NT - 1 if (NT - 1) % 2 == 1 else NT - 2]
        p6_mark = 0
        m1_last = 0
        n2dma_mark = [0, 0]
        xqw_mark = 0
        def xqw_load(j):
            if reduced:
                lo = wlo + 256 * j
                hi = lo + 258
                off = 0
            else:
                lo = max(0, 256 * j - 1)
                hi = min(NW, 256 * j + 257)
                off = 1 if j == 0 else 0
            n = hi - lo
            self.wv("y", st, m1_last)
            for k in range(8):
                self.dma(XQW.rearrange("p (k c) -> p k c", c=260)
                         [:, k:k + 1, off:off + n],
                         self.XQSP.rearrange("(k p) w -> p k w", p=128)
                         [:, k:k + 1, lo:hi], sem=self.sdx, weak=True)
            return self.sdx.n

        xqw_mark = xqw_load(0)
        for j in range(NMW):
            if reduced:
                lo = wlo + 256 * j
                hi = lo + 258
                off = 0
            else:
                lo = max(0, 256 * j - 1)
                hi = min(NW, 256 * j + 257)
                off = 1 if j == 0 else 0
            n = hi - lo
            self.wv("t", sd, sd.n)
            self.wv("t", sv, p7_mark)
            st.inc(self.mm(P[7][:, 0:n], self.ROWONES(), S1ROW[0:1, lo:hi]))
            self.wv("v", st, st.n)
            sv.inc(self.vcopy(S1B[:, off:off + n], P[7][:, 0:n]))
            p7_mark = sv.n
            if j == 0 and not reduced:
                sv.inc(self.vmemset(S1B[:, 0:1], 0.0))
                sv.inc(self.vmemset(
                    YW.rearrange("p (c w) -> p c w", w=258)[:, :, 0:1], 0.0))
            if j == NMW - 1 and not reduced:
                sv.inc(self.vmemset(S1B[:, 257:258], 0.0))
                sv.inc(self.vmemset(
                    YW.rearrange("p (c w) -> p c w", w=258)[:, :, 257:258], 0.0))
            s1b_mark = sv.n

            evac_mark = [0] * NCK
            silu_mark = [0] * NCK

            def maxacc(c):
                AB = ABa if c % 2 == 0 else ABb
                self.wv("v", ss, silu_mark[c])
                if c == 0:
                    sv.inc(self.vcopy(AMXa, AB))
                else:
                    srcb = AMXa if c % 2 == 1 else AMXb
                    dstb = AMXb if c % 2 == 1 else AMXa
                    sv.inc(self.vtt(dstb, srcb, AB, OP.max))

            self.wv("t", self.sdx, xqw_mark)
            self.wv("t", self.sdw, w1r_mark)
            for c in range(NCK):
                bank = P[c % 4]
                self.wv("t", sv, bank_mark[c % 4])
                for k in range(8):
                    r = self.mm(bank[:, 0:n],
                                W1R.rearrange("p (k f) -> p k f", k=8)
                                [:, k, c * 128:(c + 1) * 128],
                                XQW.rearrange("p (k c) -> p k c", c=260)
                                [:, k, off:off + n],
                                start=(k == 0), stop=(k == 7))
                st.inc(r)
                mmk = st.n
                CV = CVa if c % 2 == 0 else CVb
                yb = YW[:, c * 258:(c + 1) * 258]
                self.wv("v", st, mmk)
                sv.inc(self.vtt(YW[:, c * 258 + off:c * 258 + off + n],
                                bank[:, 0:n], S1B[:, off:off + n], OP.mult))
                bank_mark[c % 4] = sv.n
                self.wv("s", sv, sv.n)
                ss.inc(self.sact(CV[:, 0:256], yb[:, 0:256], AF.Identity,
                                 scale=self.CW[:, 3 * c:3 * c + 1],
                                 bias=self.CB[:, c:c + 1]))
                if c >= 1:
                    maxacc(c - 1)
                self.wv("v", ss, ss.n)
                self.vstt(CV[:, 0:256], yb[:, 1:257],
                          self.CW[:, 3 * c + 1:3 * c + 2], CV[:, 0:256],
                          OP.mult, OP.add)
                sv.inc(self.vstt(CV[:, 0:256], yb[:, 2:258],
                                 self.CW[:, 3 * c + 2:3 * c + 3], CV[:, 0:256],
                                 OP.mult, OP.add))
                evac_mark[c] = sv.n
                AB = ABa if c % 2 == 0 else ABb
                self.wv("s", sv, evac_mark[c])
                ss.inc(self.sact(yb[:, 1:257], CV[:, 0:256], AF.Silu))
                ss.inc(self.sact(AB, yb[:, 1:257], AF.Abs))
                silu_mark[c] = ss.n
                if c == NCK - 1:
                    m1_last = mmk
                if DEBUG_FFN and False:
                    self.w("y", self.ss)
                    self.dma(self.DBGS.rearrange("(c p) w -> c p w", p=128)
                             [c][:, 256 * j:256 * j + 256], yb[:, 1:257])
                    self.w("v", self.sd)
                    self.w("s", self.sd)
            maxacc(NCK - 1)
            AMX = AMXa if (NCK - 1) % 2 == 0 else AMXb
            self.wv("t", sv, sv.n)
            self.tp(P[6][:, 0:128], AMX[:, 0:128], self.IDENT[:, :])
            st.inc(self.tp(P[6][:, 128:256], AMX[:, 128:256], self.IDENT[:, :]))
            self.wv("v", st, st.n)
            am = ST[:, 6 * NT:6 * NT + 2]
            s2c = ST[:, 6 * NT + 2:6 * NT + 4]
            g2c = self.G2C[:, 2 * j:2 * j + 2]
            self.vred(am[:, 0:1], P[6][:, 0:128], OP.max)
            sv.inc(self.vred(am[:, 1:2], P[6][:, 128:256], OP.max))
            p6_mark = sv.n
            self.vts(am, am, 1e-5, None, OP.max)
            self.vrecip(s2c[:, 0:1], am[:, 0:1])
            self.vrecip(s2c[:, 1:2], am[:, 1:2])
            self.vts(s2c, s2c, 127.0, None, OP.mult)
            sv.inc(self.vstt(g2c, am, float(np.float32(0.5 / 127.0)),
                             self.SC[:, 6:7].broadcast_to([128, 2]),
                             OP.mult, OP.mult))
            self.wv("t", sv, sv.n)
            st.inc(self.tp(P[7][0:2, 0:128], s2c, self.IDENT[:, :]))
            self.wv("s", st, st.n)
            ss.inc(self.scopy(S1RT[64:66, :], P[7][0:2, 0:128]))
            self.wv("y", ss, ss.n)
            self.dma(ROWS[0:1, 0:256].rearrange("a (t c) -> a t c", c=128),
                     S1RT[64:66, :])
            self.wv("t", sd, sd.n)
            st.inc(self.mm(P[7][:, 0:256], self.ROWONES(), ROWS[0:1, 0:256]))
            self.wv("v", st, st.n)
            sv.inc(self.vcopy(S2B, P[7][:, 0:256]))
            p7_mark = sv.n
            next_xqw = xqw_load(j + 1) if j + 1 < NMW else xqw_mark
            self.wv("v", ss, silu_mark[NCK - 1])
            for c in range(0, NCK, 2):
                pi = (c // 2) % 2
                N2 = N2P[pi]
                n2sem = self.sdn if pi == 0 else self.sdu
                ybp = YW.rearrange("p (c w) -> p c w", w=258)[:, c:c + 2, 1:257]
                self.wv("v", n2sem, n2dma_mark[pi])
                self.vtt(CVP.rearrange("p (c w) -> p c w", w=256),
                         ybp, S2B.unsqueeze(1).broadcast_to([128, 2, 256]),
                         OP.mult)
                sv.inc(self.vts(N2, CVP, self.SC[:, 2:3],
                                self.SC[:, 2:3], OP.add, OP.subtract))
                self.wv("y", sv, sv.n)
                self.dma(self.N2SP.rearrange("(c p) w -> p c w", p=128)
                         [:, c:c + 2, 256 * j:256 * j + 256],
                         N2.rearrange("p (c w) -> p c w", w=256),
                         sem=n2sem, weak=True)
                n2dma_mark[pi] = n2sem.n
            xqw_mark = next_xqw
        self.w("y", self.st, self.sv, self.sdn, self.sdu, self.sc)
        self.dma(W1R.rearrange("p (c f) -> p c f", c=NCK),
                 self.W2Q.rearrange("(c p) f -> p c f", p=128), sem=self.sdw)
        w2r_mark = self.sdw.n
        ring_mark = [0] * 8
        res_mark = 0
        n2ring = XQW
        self.wv("t", self.sdw, w2r_mark)
        for j in range(NMW):
            for c in range(NCK):
                tile = n2ring[:, (c % 8) * 260:(c % 8) * 260 + 256]
                rsem = (self.sdx, self.sdn, self.sdu, self.sdw)[c % 4]
                self.wv("y", st, ring_mark[c % 8])
                self.dma(tile, self.N2SP.rearrange("(c p) w -> c p w", p=128)
                         [c][:, 256 * j:256 * j + 256], sem=rsem,
                         weak=True)
                self.wv("t", rsem, rsem.n)
                if c == 0:
                    self.wv("t", sv, res_mark)
                for tt in range(2):
                    for fh in range(2):
                        r = self.mm(P[2 + 2 * tt + fh][:, 0:512],
                                    tile[:, tt * 128:(tt + 1) * 128],
                                    W1R[:, c * 1024 + fh * 512:
                                        c * 1024 + fh * 512 + 512],
                                    start=(c == 0), stop=(c == NCK - 1))
                st.inc(r)
                ring_mark[c % 8] = st.n
            self.wv("v", st, st.n)
            for tt in range(2):
                t_idx = 2 * j + tt + (1 if reduced else 0)
                xo = X[:, t_idx * D:(t_idx + 1) * D]
                g2one = self.G2C[:, 2 * j + tt:2 * j + tt + 1]
                self.vstt(xo[:, 0:512], P[2 + 2 * tt][:, 0:512],
                          g2one, xo[:, 0:512], OP.mult, OP.add)
                sv.inc(self.vstt(xo[:, 512:1024], P[3 + 2 * tt][:, 0:512],
                                 g2one, xo[:, 512:1024], OP.mult, OP.add))
            res_mark = sv.n

    def attn_a(self):
        NT = self.NT
        JN = JNEU
        sd, sv, ss, st = self.sd, self.sv, self.ss, self.st
        P = self.psum
        X, slot, ST = self.X, self.slot, self.ST
        self.ar_reset()
        XN = self.ar_lo(1024)
        XHI = self.ar_lo(512, BF16)
        XLO = self.ar_lo(512, BF16)
        XATH = XHI
        XATL = XLO
        KF = self.ar_lo(16)
        KBAR = [[self.ar_lo(64), self.ar_lo(64)],
                [self.ar_lo(64), self.ar_lo(64)]]
        RS = [[self.ar_lo(128), self.ar_lo(128)],
              [self.ar_lo(128), self.ar_lo(128)]]
        X1S = [self.ar_lo(128), self.ar_lo(128)]
        BUPS = [[self.ar_lo(128), self.ar_lo(128)],
                [self.ar_lo(128), self.ar_lo(128)]]
        X1H = self.ar_lo(128)
        SQ = self.ar_lo(1024)
        VBTb = [self.ar_hi(1024), self.ar_hi(1024)]
        WHI = self.ar_hi(4096, BF16)
        WLO = self.ar_hi(4096, BF16)
        KTb = [self.ar_hi(1024), self.ar_hi(1024)]
        KTTb = [self.ar_hi(1024), self.ar_hi(1024)]
        X2S = self.ar_hi(1024)
        WNT = self.ar_hi(1024)
        MSKA = self.ar_hi(2048)
        MCB = self.ar_hi(1024)
        self.dma(XN[:, 0:128], self.I["EXPG"])
        self.w("v", sd)
        for h in range(H):
            sv.inc(self.vts(SQ[:, 0:128], XN[:, 0:128],
                            self.TB[:, 64 + h:65 + h], None, OP.mult))
            self.w("s", self.sv)
            ss.inc(self.sact(SQ[:, 0:128], SQ[:, 0:128], AF.Exp))
            self.w("v", ss)
            self.vtt(MSKA[:, h * 128:(h + 1) * 128], SQ[:, 0:128], self.UP[:, :],
                     OP.mult)
            sv.inc(self.vts(MSKA[:, h * 128:(h + 1) * 128],
                            MSKA[:, h * 128:(h + 1) * 128],
                            self.TB[:, 16 + h:17 + h], None, OP.mult))
        for k in range(8):
            self.dma(XN, self.I["WTH"].rearrange("(k p) f -> k p f", p=128)[k])
            self.w("v", sd, self.ss)
            whik = WHI[:, k * D:(k + 1) * D]
            wlok = WLO[:, k * D:(k + 1) * D]
            self.vcopy(whik, XN)
            self.vtt(SQ, XN, whik, OP.subtract)
            sv.inc(self.vcopy(wlok, SQ))
            self.w("y", sv)
        self.w("s", sv)
        self.w("t", sv)
        for t in range(NT):
            xt = X[:, slot[t] * D:(slot[t] + 1) * D]
            sv.inc(self.vred(ST[:, t:t + 1], xt, OP.add))
            ss.inc(self.sact(XN, xt, AF.Square, accum=ST[:, NT + t:NT + t + 1]))
        self.w("v", ss, sv)
        lmu = ST[:, 2 * NT:3 * NT]
        lrstd = ST[:, 3 * NT:4 * NT]
        lbias = ST[:, 4 * NT:5 * NT]
        lvar = ST[:, 5 * NT:6 * NT]
        self.vts(lmu, ST[:, 0:NT], INV_D, None, OP.mult)
        self.vts(lvar, ST[:, NT:2 * NT], INV_D, None, OP.mult)
        self.vtt(lrstd, lmu, lmu, OP.mult)
        self.vtt(lvar, lvar, lrstd, OP.subtract)
        self.vts(lvar, lvar, LN_EPS, None, OP.add)
        sv.inc(self.vrecip(lvar, lvar))
        self.w("s", sv)
        ss.inc(self.sact(lrstd, lvar, AF.Sqrt))
        self.w("v", ss)
        sv.inc(self.vstt(lbias, lmu, -1.0, lrstd, OP.mult, OP.mult))
        ln_mark = sv.n
        pm = [dict() for _ in range(NT)]
        t1m = [0, 0]; t3m = [0, 0]; t5m = [0, 0]; t7m = [0, 0]
        t9m = [0, 0]; v2m = [0, 0]; v4m = [0, 0]; v6m = [0, 0]
        v8m = [0, 0]; v10m = [0, 0]
        xsem = [self.sdn, self.sdu, self.sdx, self.sdw]
        xspm = [0, 0, 0, 0]
        rsm = [0, 0]
        s10m = [0, 0]
        self._sp_mark = 0

        def prelude_slices(t):
            d = t % 2
            KT, KTT, VBT = KTb[d], KTTb[d], VBTb[d]
            xt = X[:, slot[t] * D:(slot[t] + 1) * D]
            m = pm[t]
            pv = pm[t - 1] if t > 0 else {}

            def s1():
                self.wv("s", sv, ln_mark)
                if t > 0:
                    self.wv("s", sv, pv["kt"])
                ss.inc(self.sact(XN, xt, AF.Identity,
                                 bias=lbias[:, t:t + 1],
                                 scale=lrstd[:, t:t + 1]))
                if t > 0:
                    self.wv("s", st, pv["xtp"])
                    self.wv("s", st, pv["st1"])
                ss.inc(self.scopy(XHI, XN))
                m["hi"] = ss.n
                self.wv("v", ss, m["hi"])
                if t > 0:
                    self.wv("v", st, pv["xtp"])
                sv.inc(self.vtt(SQ, XN, XHI, OP.subtract))
                m["splitv"] = sv.n
                self.wv("s", sv, m["splitv"])
                ss.inc(self.scopy(XLO, SQ))
                m["split"] = ss.n

            def s2():
                self.wv("t", ss, m["split"])
                if t > 0:
                    self.wv("t", ss, pv["ktt"])
                for k in range(8):
                    self.tp(P[6][:, k * 64:k * 64 + 64].bitcast(BF16),
                            XHI[:, k * 128:(k + 1) * 128], self.IDB[:, :])
                for k in range(8):
                    r = self.tp(P[7][:, k * 64:k * 64 + 64].bitcast(BF16),
                                XLO[:, k * 128:(k + 1) * 128], self.IDB[:, :])
                st.inc(r)
                m["xtp"] = st.n
                self.wv("s", st, m["xtp"])
                if t > 0:
                    self.wv("s", st, pv["st1"])
                self.scopy(XATH, P[6][:, 0:512].bitcast(BF16))
                ss.inc(self.scopy(XATL, P[7][:, 0:512].bitcast(BF16)))
                m["xat"] = ss.n

            def static_q(fh, half):
                bank = P[fh]
                self.wv("t", ss, m["xat"])
                for k in range(4 * half, 4 * half + 4):
                    wh = WHI[:, k * D + fh * 512:k * D + fh * 512 + 512]
                    wl = WLO[:, k * D + fh * 512:k * D + fh * 512 + 512]
                    xh = XATH[:, k * 128:(k + 1) * 128]
                    xl = XATL[:, k * 128:(k + 1) * 128]
                    self.mm(bank[:, 0:512], xh, wh,
                            start=(k == 0), stop=False)
                    self.mm(bank[:, 0:512], xl, wh, start=False, stop=False)
                    r = self.mm(bank[:, 0:512], xh, wl, start=False,
                                stop=(k == 7))
                st.inc(r)
                m["st%d" % fh] = st.n

            def s5():
                self.wv("v", st, m["st1"])
                self.vtt(VBT[:, 0:512], P[0][:, 0:512], self.NEGBB[:, 0:512],
                         OP.mult)
                sv.inc(self.vtt(VBT[:, 512:1024], P[1][:, 0:512],
                                self.NEGBB[:, 512:1024], OP.mult))
                m["vbt"] = sv.n

            def s56():
                s5()
                s6()

            def s78():
                s7()
                s8()

            def s6():
                if t > 0:
                    self.wv("s", sv, pv["gb"])
                ss.inc(self.sact(SQ, XN, AF.Square))
                m["ksq"] = ss.n
                self.wv("v", ss, m["ksq"])
                self.vred(KF.rearrange("p (h o) -> p h o", o=1),
                          SQ.rearrange("p (h d) -> p h d", d=DH), OP.add)
                self.vts(KF, KF, INV_DH, None, OP.mult)
                self.vtt(KF, KF, self.SC[:, 3:4].broadcast_to([128, 16]),
                         OP.add)
                sv.inc(self.vrecip(KF, KF))
                m["kfa"] = sv.n
                self.wv("s", sv, m["kfa"])
                ss.inc(self.sact(KF, KF, AF.Sqrt))
                m["kfs"] = ss.n

            def s7():
                self.wv("v", ss, m["kfs"])
                self.vtt(KT.rearrange("p (h d) -> p h d", d=DH),
                         XN.rearrange("p (h d) -> p h d", d=DH),
                         KF.rearrange("p (h o) -> p h o", o=1)
                         .broadcast_to([128, 16, DH]), OP.mult)
                sv.inc(self.vtt(KT, KT, self.KNWB, OP.mult))
                m["kt"] = sv.n

            def s8():
                self.wv("s", sv, m["kt"])
                ss.inc(self.sact(SQ, KT, AF.Square))
                m["ksq2"] = ss.n
                self.wv("v", ss, m["ksq2"])
                sv.inc(self.vred(self.GB[:, (t * H) + 0:(t * H) + H]
                                 .rearrange("p (h o) -> p h o", o=1),
                                 SQ.rearrange("p (h d) -> p h d", d=DH),
                                 OP.add))
                m["gb"] = sv.n
                sv.inc(self.vtt(self.ND[:, t * H:t * H + H],
                                self.GB[:, t * H:t * H + H],
                                self.TB[:, 48:64], OP.add))
                self.wv("t", sv, m["kt"])
                self.wv("t", ss, m["xat"])
                for k in range(4):
                    self.tp(P[6][:, k * 128:(k + 1) * 128],
                            KT[:, k * 128:(k + 1) * 128], self.IDENT[:, :])
                st.inc(self.tp(P[7][:, 0:128], KT[:, 512:640],
                               self.IDENT[:, :]))
                for k in range(5, 8):
                    r = self.tp(P[7][:, (k - 4) * 128:(k - 3) * 128],
                                KT[:, k * 128:(k + 1) * 128],
                                self.IDENT[:, :])
                st.inc(r)
                m["ktp"] = st.n
                self.wv("s", st, m["ktp"])
                if t > 1:
                    if "g_last" in pm[t - 2]:
                        self.wv("s", st, pm[t - 2]["g_last"])
                    if "halo" in pm[t - 2]:
                        self.wv("s", sv, pm[t - 2]["halo"])
                self.scopy(KTT[:, 0:512], P[6][:, 0:512])
                ss.inc(self.scopy(KTT[:, 512:1024], P[7][:, 0:512]))
                m["ktt"] = ss.n

            import os as _os9
            if "wholestatic" in _os9.environ.get("ATTNA_MODE", ""):
                return [s1, s2,
                        lambda: (static_q(0, 0), static_q(0, 1)),
                        lambda: (static_q(1, 0), static_q(1, 1)),
                        s5, s6, s7, s8]
            return [s1, s2,
                    lambda: static_q(0, 0), lambda: static_q(0, 1),
                    lambda: static_q(1, 0), lambda: static_q(1, 1),
                    s56, s78]

        def halo(t):
            d = t % 2
            KTT, VBT = KTTb[d], VBTb[d]
            col = 127 if t == 0 else 0
            base = 0 if t == 0 else 16
            kv = KTT.rearrange("p (k c) -> p k c", c=128)
            self.wv("v", ss, pm[t]["ktt"])
            self.vcopy(X1H[0:64, 0:16].rearrange("p (k o) -> p k o", o=2)
                       [:, :, 0:1], kv[0:64, :, col:col + 1])
            sv.inc(self.vcopy(X1H[0:64, 0:16]
                              .rearrange("p (k o) -> p k o", o=2)
                              [:, :, 1:2], kv[64:128, :, col:col + 1]))
            pm[t]["halo"] = sv.n
            self.w("y", sv)
            self.dma(self.HCOLS[:, base:base + 16], X1H[0:64, 0:16])
            self.dma(self.HROWS[t // (NT - 1):t // (NT - 1) + 1, :],
                     VBT[col:col + 1, :])
            self.w("v", self.sd)

        GB_ = [P[2], P[4]]
        S0 = [P[3][:, 0:128], P[5][:, 256:384]]
        S1 = [P[3][:, 128:256], P[5][:, 384:512]]

        def stage0(t, p):
            d = t % 2
            KT, KTT, VBT = KTb[d], KTTb[d], VBTb[d]
            s = p % 2
            for q, h in ((0, 2 * p), (1, 2 * p + 1)):
                kth = KT[:, h * DH:(h + 1) * DH]
                self.wv("v", st, t9m[q])
                self.wv("s", st, t9m[q])
                self.wv("s", sv, pm[t]["vbt"])
                self.vts(RS[q][s][:, DH:128], kth, self.LR[:, h:h + 1],
                         None, OP.mult)
                ss.inc(self.scopy(RS[q][s][:, 0:DH],
                                  VBT[:, h * DH:(h + 1) * DH]))
                rsm[q] = ss.n
                sv.inc(self.vts(KBAR[q][s], kth,
                                self.LR[:, 16 + h:17 + h], None, OP.mult))
            for q, h in ((0, 2 * p), (1, 2 * p + 1)):
                kfs = KTT[(h % 2) * 64:(h % 2) * 64 + 64,
                          (h // 2) * 128:(h // 2) * 128 + 128]
                self.wv("t", ss, pm[t]["ktt"])
                self.wv("t", sv, v2m[q])
                st.inc(self.mm(GB_[q][:, s * 128:s * 128 + 128], kfs, kfs))
                t1m[q] = st.n
            pm[t]["g_last"] = st.n
            for q, h in ((0, 2 * p), (1, 2 * p + 1)):
                self.wv("v", st, t1m[q])
                sv.inc(self.vtt(BUPS[q][s], GB_[q][:, s * 128:s * 128 + 128],
                                MSKA[:, h * 128:(h + 1) * 128], OP.mult))
                v2m[q] = sv.n

        def emit_pair(t, p):
            d = t % 2
            KT, KTT, VBT = KTb[d], KTTb[d], VBTb[d]
            hA, hB = 2 * p, 2 * p + 1
            sl = (p % 4) * 256
            s = p % 2

            def hS(h):
                return (KT[:, h * DH:(h + 1) * DH],
                        KTT[(h % 2) * 64:(h % 2) * 64 + 64,
                            (h // 2) * 128:(h // 2) * 128 + 128],
                        X2S[:, sl + (h % 2) * 128:sl + (h % 2) * 128 + 128])

            for q in (0, 1):
                self.wv("t", sv, v2m[q])
                self.wv("t", ss, rsm[q])
                if q == 1:
                    self.wv("t", sv, v10m[1])
                    self.wv("t", ss, s10m[1])
                st.inc(self.mm(S0[q], BUPS[q][s], RS[q][s]))
                t3m[q] = st.n
            for q in (0, 1):
                self.wv("v", st, t3m[q])
                sv.inc(self.vtt(X1S[q], RS[q][s], S0[q], OP.subtract))
                v4m[q] = sv.n
            for q in (0, 1):
                self.wv("t", sv, v4m[q])
                st.inc(self.mm(S1[q], BUPS[q][s], X1S[q]))
                t5m[q] = st.n
            for q in (0, 1):
                self.wv("v", st, t5m[q])
                sv.inc(self.vtt(X1S[q], RS[q][s], S1[q], OP.subtract))
                v6m[q] = sv.n
            for q in (0, 1):
                self.wv("t", sv, v6m[q])
                st.inc(self.mm(S0[q], BUPS[q][s], X1S[q]))
                t7m[q] = st.n
            for q, h in ((0, hA), (1, hB)):
                kth, kfs, x2h = hS(h)
                self.wv("v", st, t7m[q])
                self.wv("v", xsem[p % 4], xspm[p % 4])
                sv.inc(self.vtt(x2h, RS[q][s], S0[q], OP.subtract))
                v8m[q] = sv.n
            self.wv("y", sv, v8m[1])
            self.dma(self.USP.rearrange("(c p) w -> c p w", p=128)[t - 1]
                     [:, p * 256:(p + 1) * 256], X2S[:, sl:sl + 256],
                     sem=xsem[p % 4], weak=True)
            for q, h in ((0, hA), (1, hB)):
                kth, kfs, x2h = hS(h)
                mb = sl + q * 128
                self.wv("t", sv, v8m[1])
                self.wv("t", sv, v10m[1])
                self.wv("t", ss, s10m[1])
                st.inc(self.mm(P[5][0:DH, q * 256:q * 256 + DH],
                               x2h[:, DH:128], KBAR[q][s]))
                st.inc(self.mm(P[5][0:DH, q * 256 + DH:q * 256 + 128],
                               KBAR[q][s], x2h[:, 0:DH]))
                st.inc(self.tp(P[5][0:DH, q * 256 + 128:q * 256 + 256],
                               x2h[:, DH:128], self.IDENT[:, :]))
                t9m[q] = st.n
            for q, h in ((0, hA), (1, hB)):
                mb = sl + q * 128
                self.wv("v", st, t9m[1])
                self.wv("v", xsem[p % 4], xspm[p % 4])
                self.wv("s", st, t9m[1])
                self.wv("s", xsem[p % 4], xspm[p % 4])
                self.vtt(MCB[0:DH, mb:mb + DH],
                         P[5][0:DH, q * 256:q * 256 + DH],
                         self.IDLC[0:DH, h * 64:(h + 1) * 64], OP.add)
                self.vcopy(MCB[0:DH, mb + DH:mb + 128],
                           P[5][0:DH, q * 256 + DH:q * 256 + 128])
                sv.inc(self.vcopy(WNT[0:DH, mb:mb + 128],
                                  P[5][0:DH, q * 256 + 128:q * 256 + 256]))
                v10m[q] = sv.n
                s10m[q] = ss.n
            self.wv("y", sv, v10m[1])
            self.wv("y", ss, s10m[1])
            self.dma(self.WSP.rearrange("(c p) w -> c p w", p=DH)[t - 1]
                     [:, p * 256:(p + 1) * 256], WNT[0:DH, sl:sl + 256],
                     sem=xsem[p % 4], weak=True)
            self.dma(self.MCSP.rearrange("(c p) w -> c p w", p=DH)[t - 1]
                     [:, p * 256:(p + 1) * 256], MCB[0:DH, sl:sl + 256],
                     sem=xsem[p % 4], weak=True)
            xspm[p % 4] = xsem[p % 4].n

        for s in prelude_slices(0):
            if s is not None:
                s()
        halo(0)
        for s in prelude_slices(1):
            if s is not None:
                s()
        stage0(1, 0)
        for t in range(1, NT - 1):
            slices = prelude_slices(t + 1) if t + 1 < NT else [None] * 8
            for p in range(H // 2):
                emit_pair(t, p)
                if slices[p] is not None:
                    slices[p]()
                if p + 1 < H // 2:
                    stage0(t, p + 1)
                elif t + 1 <= NT - 2:
                    stage0(t + 1, 0)

        halo(NT - 1)
        self.w("v", self.sd)

    def attn_b(self):
        NT, NCH = self.NT, self.NCH
        sd, sv, ss, st = self.sd, self.sv, self.ss, self.st
        P = self.psum
        X, slot, ST = self.X, self.slot, self.ST
        self.ar_reset()
        YTb = [self.ar_lo(1024), self.ar_lo(1024)]
        ZTb = [self.ar_lo(1024), self.ar_lo(1024)]
        SQ2 = self.ar_lo(1024)
        HR = self.ar_lo(32)
        MCB = self.ar_hi(2048)
        MCB2 = self.ar_hi(2048)
        UBb = [self.ar_hi(2048), self.ar_hi(2048)]
        WTBb = [self.ar_hi(2048), self.ar_hi(2048)]
        VHL = self.ar_hi(1024)
        VHR = self.ar_hi(1024)
        YROW = self.ar_hi(1024)
        UROW = self.ar_hi(1024)
        SIN = self.ar_hi(1024)
        SSTA = self.ar_hi(1024)
        AGB = self.ar_hi(1024)

        def s_step(MC, bank_lo, bank_hi, state, wt=None):
            if wt:
                wt()
            for h in range(H):
                bank = bank_lo if h < 8 else bank_hi
                cb = (h % 8) * 64
                r = self.mm(bank[0:64, cb:cb + 64],
                            MC[0:DH, h * 128:h * 128 + DH],
                            state[0:64, h * 64:(h + 1) * 64])
            st.inc(r)
            smm = st.n
            self.wv("v", st, smm)
            MCv = MC[0:64, 0:2048].rearrange("p (h x) -> p h x", x=128)
            self.vtt(state[0:64, 0:512].rearrange("p (h x) -> p h x", x=64),
                     bank_lo[0:64, 0:512].rearrange("p (h x) -> p h x", x=64),
                     MCv[:, 0:8, DH:128], OP.add)
            sv.inc(self.vtt(state[0:64, 512:1024]
                            .rearrange("p (h x) -> p h x", x=64),
                            bank_hi[0:64, 0:512]
                            .rearrange("p (h x) -> p h x", x=64),
                            MCv[:, 8:16, DH:128], OP.add))
            return smm, sv.n

        lsem = [self.sdn, self.sdu]
        mcm = [0, 0]
        sv.inc(self.vmemset(SSTA[0:64, :], 0.0))
        self.dma(MCB[0:DH, :],
                 self.MCSP.rearrange("(c p) w -> c p w", p=DH)[0],
                 sem=lsem[0], weak=True)
        mcm[0] = lsem[0].n
        sadd = sv.n
        smm = 0
        for ch in range(NCH):
            MC = MCB if ch % 2 == 0 else MCB2
            if ch + 1 < NCH:
                MC2 = MCB2 if ch % 2 == 0 else MCB
                self.wv("y", st, smm)
                self.dma(MC2[0:DH, :],
                         self.MCSP.rearrange("(c p) w -> c p w",
                                             p=DH)[ch + 1],
                         sem=lsem[(ch + 1) % 2], weak=True)
                mcm[(ch + 1) % 2] = lsem[(ch + 1) % 2].n

            def wt(c=ch):
                self.wv("t", lsem[c % 2], mcm[c % 2])
                self.wv("t", sv, sadd)
            smm, sadd = s_step(MC, P[(ch % 2) * 2], P[(ch % 2) * 2 + 1],
                               SSTA, wt)
        self.w("y", sv)
        self.dma(self.AGI, SSTA[0:64, :])
        self.w("g", self.sd)
        self.sc.inc(self.nc.gpsimd.collective_compute(
            "AllGather", OP.bypass,
            replica_groups=[[0, 1, 2, 3], [4, 5, 6, 7]],
            ins=[self.AGI.opt()], outs=[self.AGO.opt()]))
        self.w("y", self.sc)
        for q in range(4):
            self.dma(AGB[0:64, :],
                     self.AGO.rearrange("(q p) w -> q p w", p=64)[q])
            self.w("v", self.sd)
            if q == 0:
                sv.inc(self.vts(SIN[0:64, :], AGB[0:64, :],
                                self.PSEL[0:64, 0:1], None, OP.mult))
            else:
                sv.inc(self.vstt(SIN[0:64, :], AGB[0:64, :],
                                 self.PSEL[0:64, q:q + 1], SIN[0:64, :],
                                 OP.mult, OP.add))
            self.w("y", sv)
        sv.inc(self.vcopy(SSTA[0:64, :], SIN[0:64, :]))
        sadd = sv.n
        smm = 0
        self.dma(HR[0:64, :], self.HCOLS)
        self.dma(VHL[0:1, :], self.HROWS[0:1, :])
        self.dma(VHR[0:1, :], self.HROWS[1:2, :])
        self.w("v", self.sd)
        ubm = [0, 0]
        onact = 0
        onred = 0
        ymk = [0, 0]
        ufx = [0, 0]

        def load_tile(tt):
            ch = tt - 1
            dd = tt % 2
            self.dma(UBb[dd], self.USP.rearrange("(c p) w -> c p w",
                                                 p=128)[ch],
                     sem=lsem[dd], weak=True)
            self.dma(WTBb[dd][0:DH, :],
                     self.WSP.rearrange("(c p) w -> c p w", p=DH)[ch],
                     sem=lsem[dd], weak=True)
            MC = MCB if ch % 2 == 0 else MCB2
            self.dma(MC[0:DH, :],
                     self.MCSP.rearrange("(c p) w -> c p w", p=DH)[ch],
                     sem=lsem[dd], weak=True)
            ubm[dd] = lsem[dd].n

        if NT > 2:
            load_tile(1)
        for t in range(NT):
            d = t % 2
            YT, ZT = YTb[d], ZTb[d]
            if 1 <= t <= NT - 2:
                ch = t - 1
                if t + 1 <= NT - 2:
                    self.wv("y", st, ufx[d])
                    self.wv("y", sv, onred)
                    load_tile(t + 1)
                self.wv("t", lsem[d], ubm[d])
                self.wv("t", sv, sadd)
                self.wv("t", sv, max(ymk))
                for h in range(H):
                    bank = P[4] if h < 8 else P[5]
                    cb = (h % 8) * 64
                    r = self.mm(bank[:, cb:cb + 64],
                                WTBb[d][0:DH, h * 128:(h + 1) * 128],
                                SSTA[0:64, h * 64:(h + 1) * 64])
                st.inc(r)
                ufx[d] = st.n
                self.wv("v", st, ufx[d])
                self.wv("v", ss, onact)
                UBv = UBb[d].rearrange("p (h x) -> p h x", x=128)
                NDv = self.ND[:, t * H:(t + 1) * H]
                for g in range(2):
                    Yv = YT[:, g * 512:(g + 1) * 512].rearrange(
                        "p (h x) -> p h x", x=64)
                    bank = (P[4], P[5])[g]
                    self.vtt(Yv, bank[:, 0:512].rearrange(
                        "p (h x) -> p h x", x=64),
                        UBv[:, g * 8:(g + 1) * 8, 0:DH], OP.add)
                    sv.inc(self.vtt(Yv, Yv,
                                    NDv[:, g * 8:(g + 1) * 8].unsqueeze(2)
                                    .broadcast_to([128, 8, DH]), OP.mult))
                ymk[d] = sv.n
                MC = MCB if ch % 2 == 0 else MCB2

                def wt2(c=ch, dd=d):
                    self.wv("t", lsem[dd], ubm[dd])
                    self.wv("t", sv, sadd)
                smm, sadd = s_step(MC, P[(ch % 2) * 2], P[(ch % 2) * 2 + 1],
                                   SSTA, wt2)
            elif t == 0:
                self.wv("v", ss, onact)
                sv.inc(self.vmemset(YT, 0.0))
                self.w("t", sv, self.sd)
                for h in range(H):
                    bank = P[4] if h < 8 else P[5]
                    r = self.mm(bank[0:1, (h % 8) * DH:(h % 8) * DH + DH],
                                HR[0:64, h:h + 1],
                                SIN[0:64, h * 64:(h + 1) * 64])
                st.inc(r)
                self.w("v", st)
                self.vtt(YROW[0:1, :], VHL[0:1, :], self.NEGIBB[0:1, :],
                         OP.mult)
                self.vtt(YROW[0:1, 0:512], YROW[0:1, 0:512], P[4][0:1, 0:512],
                         OP.add)
                sv.inc(self.vtt(YROW[0:1, 512:1024], YROW[0:1, 512:1024],
                                P[5][0:1, 0:512], OP.add))
                self.w("y", self.sv)
                self.dma(YT[127:128, :], YROW[0:1, :])
                self.w("v", self.sd)
                ymk[d] = sv.n
            else:
                self.wv("v", ss, onact)
                self.vmemset(YT, 0.0)
                sv.inc(self.vtt(SQ2[:, 0:16], self.TB[:, 16:32],
                                self.TB[:, 0:16], OP.mult))
                self.w("t", sv, self.sd)
                for h in range(H):
                    bank = P[4] if h < 8 else P[5]
                    r = self.mm(bank[0:1, (h % 8) * DH:(h % 8) * DH + DH],
                                HR[0:64, 16 + h:17 + h],
                                SSTA[0:64, h * 64:(h + 1) * 64])
                st.inc(r)
                self.w("v", st)
                self.vcopy(UROW[0:1, 0:512], P[4][0:1, 0:512])
                sv.inc(self.vcopy(UROW[0:1, 512:1024], P[5][0:1, 0:512]))
                self.nc.vector.drain()
                self._dirty["v"].clear()
                for h in range(H):
                    self.nc.vector.tensor_scalar(
                        UROW[0:1, h * DH:(h + 1) * DH],
                        UROW[0:1, h * DH:(h + 1) * DH],
                        SQ2[0:1, h:h + 1], None, OP.mult)
                self.nc.vector.drain()
                self.vtt(UROW[0:1, :], VHR[0:1, :], UROW[0:1, :], OP.subtract)
                self.nc.vector.drain()
                for h in range(H):
                    self.nc.vector.tensor_scalar(
                        YROW[0:1, h * DH:(h + 1) * DH],
                        UROW[0:1, h * DH:(h + 1) * DH],
                        self.ND[0:1, t * H + h:t * H + h + 1], None, OP.mult)
                self._dirty["v"].add(("ARENA", 0, 1 << 40))
                sv.inc(self.nc.vector.tensor_scalar(
                    SQ2[:, 0:1], SQ2[:, 0:1], 1.0, None, OP.mult))
                self.w("y", self.sv)
                self.dma(YT[0:1, :], YROW[0:1, :])
                self.w("v", self.sd)
                ymk[d] = sv.n
            self.wv("v", sv, ymk[d])
            sv.inc(self.vred(ST[:, 0:1], YT, OP.add))
            vr = sv.n
            self.wv("s", sv, vr)
            ss.inc(self.sact(ZT, YT, AF.Square, accum=ST[:, 1:2]))
            sqm = ss.n
            self.wv("v", ss, sqm)
            self.vts(ST[:, 2:3], ST[:, 0:1], INV_D, None, OP.mult)
            self.vts(ST[:, 3:4], ST[:, 1:2], INV_D, None, OP.mult)
            self.vtt(ST[:, 4:5], ST[:, 2:3], ST[:, 2:3], OP.mult)
            self.vtt(ST[:, 3:4], ST[:, 3:4], ST[:, 4:5], OP.subtract)
            self.vts(ST[:, 3:4], ST[:, 3:4], LN_EPS, None, OP.add)
            sv.inc(self.vrecip(ST[:, 3:4], ST[:, 3:4]))
            self.wv("s", sv, sv.n)
            ss.inc(self.sact(ST[:, 3:4], ST[:, 3:4], AF.Sqrt))
            self.wv("v", ss, ss.n)
            sv.inc(self.vstt(ST[:, 4:5], ST[:, 2:3], -1.0, ST[:, 3:4],
                             OP.mult, OP.mult))
            self.wv("s", sv, sv.n)
            ss.inc(self.sact(ZT, YT, AF.Identity, bias=ST[:, 4:5],
                             scale=ST[:, 3:4]))
            onact = ss.n
            self.wv("v", ss, onact)
            xo = X[:, t * D:(t + 1) * D]
            self.vtt(SQ2, ZT, self.ONWB, OP.mult)
            self.vtt(SQ2, SQ2, self.ONBB, OP.add)
            if t == 0 or t == NT - 1:
                msk = self.EML if t == 0 else self.EMR
                self.vtt(P[6][:, 0:512], SQ2[:, 0:512], xo[:, 0:512], OP.add)
                self.vtt(P[7][:, 0:512], SQ2[:, 512:1024], xo[:, 512:1024],
                         OP.add)
                self.vts(P[6][:, 0:512], P[6][:, 0:512], msk[:, 0:1],
                         None, OP.mult)
                self.vts(P[7][:, 0:512], P[7][:, 0:512], msk[:, 0:1],
                         None, OP.mult)
                self.vcopy(xo[:, 0:512], P[6][:, 0:512])
                sv.inc(self.vcopy(xo[:, 512:1024], P[7][:, 0:512]))
            else:
                sv.inc(self.vtt(xo, SQ2, xo, OP.add))
            onred = sv.n


def _host_inputs(inputs, NT):
    x = np.asarray(inputs["L0_attn_x"], dtype=np.float32)
    B, L, _ = x.shape
    NCH = NT - 2
    LC = NCH * C
    W = np.ascontiguousarray(np.asarray(inputs["L0_attn_W"], np.float32).T)
    W1T = np.ascontiguousarray(np.asarray(inputs["L0_ffn_w1"], np.float32).T)
    W2T = np.ascontiguousarray(np.asarray(inputs["L0_ffn_w2"], np.float32).T)
    cw = np.asarray(inputs["L0_ffn_conv_w"], np.float32).reshape(DF, 3)
    CWc = np.ascontiguousarray(cw.reshape(NCK, 128, 3).transpose(1, 0, 2)
                               .reshape(128, 3 * NCK))
    CBc = np.ascontiguousarray(np.asarray(inputs["L0_ffn_conv_b"], np.float32)
                               .reshape(NCK, 128).T)
    knw = np.tile(np.asarray(inputs["L0_attn_k_norm_w"], np.float32), H)[None]
    onw = np.asarray(inputs["L0_attn_o_norm_w"], np.float32)[None]
    onb = np.asarray(inputs["L0_attn_o_norm_b"], np.float32)[None]
    eta = np.asarray(inputs["L0_attn_eta_raw"], np.float32)[:, None]
    lml = np.asarray(inputs["L0_attn_lam_logit"], np.float32)[:, None]
    up01 = np.triu(np.ones((128, 128), np.float32), 1)
    ident = np.eye(128, dtype=np.float32)
    identb = ident.astype(ml_dtypes.bfloat16)
    hind = np.zeros((16, D), np.float32)
    for h in range(H):
        hind[h, h * DH:(h + 1) * DH] = 1.0
    onesr = np.ones((1, 128), np.float32)
    ii = np.arange(128, dtype=np.float32)
    expg = ii[None, :] - ii[:, None]
    expc1 = np.tile((ii + 1.0)[:, None], (1, 16)).astype(np.float32)
    expc2 = np.tile((127.0 - ii)[:, None], (1, 16)).astype(np.float32)

    common = dict(WTH=W, CWc=CWc, CBc=CBc, KNW=knw,
                  ONW=onw, ONB=onb, ETA=eta, LML=lml, UP01=up01, IDENT=ident,
                  IDENTB=identb, HIND=hind, ONESR=onesr, EXPG=expg,
                  EXPC1=expc1, EXPC2=expc2)
    maps = []
    for core in range(8):
        b = core // 4
        q = core % 4
        t0 = q * LC
        lo, hi = t0 - C, t0 + LC + C
        xs = np.zeros((NT * C, D), np.float32)
        slo, shi = max(0, lo), min(L, hi)
        xs[slo - lo:shi - lo] = x[b, slo:shi]
        eml = np.ones((128, 1), np.float32)
        emr = np.ones((128, 1), np.float32)
        if q == 0:
            eml[127] = 0.0
        if q == 3:
            emr[0] = 0.0
        psel = np.zeros((128, 4), np.float32)
        if q > 0:
            psel[:, q - 1] = 1.0
        m = dict(common)
        m.update(x0=xs, EMASKL=eml, EMASKR=emr, PSEL=psel,
                 W1THP=W1T[core * 128:(core + 1) * 128],
                 W2THP=W2T[core * 512:(core + 1) * 512])
        maps.append(m)
    return maps


_NC_CACHE = {}


def kernel(**inputs):
    x = np.asarray(inputs["L0_attn_x"])
    B, L, _ = x.shape
    assert B == 2
    NCH = L // (4 * C)
    NT = NCH + 2
    if NT not in _NC_CACHE:
        _NC_CACHE[NT] = KB(NT).build()
    nc = _NC_CACHE[NT]
    maps = _host_inputs(inputs, NT)
    res = run_bass_kernel_spmd(nc, maps, core_ids=list(range(8)))
    out = np.zeros((B, L, D), np.float32)
    LC = NCH * C
    for core in range(8):
        b = core // 4
        q = core % 4
        out[b, q * LC:(q + 1) * LC] = np.asarray(res.results[core]["XOUT"])
    return out
